# version 03
# baseline (speedup 1.0000x reference)
import numpy as np
import ml_dtypes
from contextlib import ExitStack

import concourse.bass as bass
import concourse.mybir as mybir
from concourse.bass_utils import run_bass_kernel_spmd
from concourse.alu_op_type import AluOpType as ALU

F32 = mybir.dt.float32
BF16 = mybir.dt.bfloat16
U8 = mybir.dt.uint8
AF = mybir.ActivationFunctionType
AX = mybir.AxisListType
BF = ml_dtypes.bfloat16

NCORES = 8
NT = 4096
D = 1024
NTT = NT // 128
DJ = D // 128

ENGS = ("tensor", "vector", "scalar", "gpsimd", "sync")


class Prog:
    def __init__(self, nc, stack):
        self.nc = nc
        self.stack = stack
        self.stream = {e: [] for e in ENGS}
        self.sems = {}
        self.semval = {}
        self.uid = 0
        self.c = {}
        self._consts = {}
        self._cpool = None
        self.alias = {}
        self.base = {}
        self._pool = {"c": [], "s": [], "g": []}
        self._pool_i = {"c": 0, "s": 0, "g": 0}
        self._pooling = False

    def _phys(self, name, cls=None):
        ph = self.alias.get(name)
        if ph is not None:
            return ph
        if not self._pooling:
            self.alias[name] = name
            self.base[name] = 0
            return name
        if cls is None:
            return None
        pool, i = self._pool[cls], self._pool_i[cls]
        if i < len(pool):
            ph = pool[i]
        else:
            ph = "z%s%d" % (cls, len(pool))
            pool.append(ph)
            self.alias[ph] = ph
            self.base[ph] = 0
        self._pool_i[cls] += 1
        self.alias[name] = ph
        self.base[name] = self.semval.get(ph, 0)
        return ph

    def pool_start(self):
        self._pooling = True

    def new_epoch(self):
        for e in ENGS:
            for name in sorted(self.semval):
                v = self.semval[name]
                if v > 0:
                    h = self.sems[name]
                    self.stream[e].append(
                        lambda en, h=h, v=v: en.wait_ge(h, v))
        self.alias = {k: v for k, v in self.alias.items() if k == v}
        self.base = {k: v for k, v in self.base.items() if k in self.alias}
        self._pool_i = {"c": 0, "s": 0, "g": 0}

    def sem(self, name, cls="c"):
        name = self._phys(name, cls)
        if name not in self.sems:
            self.sems[name] = self.stack.enter_context(self.nc.semaphore(name))
            self.semval[name] = 0
        return self.sems[name]

    def val(self, name):
        ph = self.alias.get(name)
        if ph is None:
            return 0
        return self.semval.get(ph, 0) - self.base[name]

    def do(self, eng, fn, inc=None):
        if inc is None:
            self.stream[eng].append(fn)
            return None
        name, k = inc
        ph = self._phys(name, "c")
        h = self.sem(ph)
        self.semval[ph] += k
        v = self.semval[ph] - self.base[name]

        def wrapped(e, fn=fn, h=h, k=k):
            fn(e).then_inc(h, k)

        self.stream[eng].append(wrapped)
        return v

    def wait(self, eng, name, val):
        if val <= 0:
            return
        ph = self.alias.get(name)
        assert ph is not None, f"wait on unbound pooled sem {name}"
        val = val + self.base[name]
        assert ph in self.semval and val <= self.semval[ph], \
            f"wait on future {name}={val} (posted {self.semval.get(ph)})"
        h = self.sems[ph]
        self.stream[eng].append(lambda e, h=h, v=val: e.wait_ge(h, v))

    def bk(self, b):
        return "bk%d" % b

    def bank_acquire(self, eng, b):
        self.wait(eng, self.bk(b), self.val(self.bk(b)))

    def dma(self, q, out, in_, sem=None):
        sem = sem or ("dma_" + q)
        self.sem(sem, "g" if q == "gpsimd" else "s")
        return self.do(q, lambda e: e.dma_start(out=out, in_=in_), inc=(sem, 16))

    def sb(self, name, shape, dtype):
        self.uid += 1
        return self.stack.enter_context(
            self.nc.sbuf_tensor(f"{name}_{self.uid}", shape, dtype))

    def ps(self, name, shape, dtype=F32):
        self.uid += 1
        return self.stack.enter_context(
            self.nc.psum_tensor(f"{name}_{self.uid}", shape, dtype))

    def constf(self, value, dtype=F32):
        key = (float(value), dtype)
        if key not in self._consts:
            if self._cpool is None:
                self._cpool = self.sb("cpool", [128, 64], F32)
            idx = len(self._consts)
            assert idx < 64
            ap = self._cpool[:, idx:idx + 1]
            if dtype != F32:
                ap = self._cpool[:, idx:idx + 1].bitcast(dtype)[:, 0:1]
            self._consts[key] = (ap, float(value))
        return self._consts[key][0]

    def finalize(self):
        if self._consts:
            h = self.sem("cst2")
            items = list(self._consts.values())
            pre = []
            for n, (ap, val) in enumerate(items):
                if n == len(items) - 1:
                    pre.append(lambda e, ap=ap, val=val, h=h:
                               e.memset(ap, val).then_inc(h, 1))
                else:
                    pre.append(lambda e, ap=ap, val=val: e.memset(ap, val))
            self.stream["gpsimd"] = pre + self.stream["gpsimd"]
            for e in ENGS:
                if e != "gpsimd" and self.stream[e]:
                    self.stream[e] = [lambda en, h=h: en.wait_ge(h, 1)] + self.stream[e]
        with self.nc.Block() as block:
            for e in ENGS:
                ops = self.stream[e]
                if ops:
                    getattr(block, e)(lambda eng, ops=ops: [op(eng) for op in ops])


class Arena:

    def __init__(self, p, bytes_pp):
        self.buf = p.sb("arena", [128, bytes_pp], U8)
        self.cap = bytes_pp
        self.off = 0
        self.peak = 0

    def alloc(self, cols, dtype, parts=128):
        nb = cols * mybir.dt.size(dtype)
        nbal = (nb + 31) // 32 * 32
        assert self.off + nbal <= self.cap, \
            f"arena overflow {self.off}+{nbal}>{self.cap}"
        o = self.off
        self.off += nbal
        self.peak = max(self.peak, self.off)
        v = self.buf[:, o:o + nb].bitcast(dtype)
        if parts != 128:
            v = v[0:parts, :]
        return v

    def mark(self):
        return self.off

    def reset(self, mark):
        self.off = mark


def cdiv(a, b):
    return -(-a // b)


def make_banks(p):
    return [p.ps(f"bank{i}", [128, 512], F32) for i in range(8)]


def emit_const_setup(p, cd):
    p.c["ident"] = p.sb("ident", [128, 128], BF16)
    p.c["ident_f"] = p.sb("identf", [128, 128], F32)
    p.dma("sync", p.c["ident"][:, :], cd["ident"][:, :], "cst")
    p.dma("sync", p.c["ident_f"][:, :], cd["ident_f"][:, :], "cst")
    ob = p.sb("ones_bf", [1, 512], BF16)
    of = p.sb("ones_f", [1, 512], F32)
    oc = p.sb("ones_col", [128, 1], BF16)
    p.do("gpsimd", lambda e: e.memset(ob[:, :], 1.0), inc=("cst", 1))
    p.do("gpsimd", lambda e: e.memset(of[:, :], 1.0), inc=("cst", 1))
    p.do("gpsimd", lambda e: e.memset(oc[:, :], 1.0), inc=("cst", 1))
    p.c["ones_bf"] = ob
    p.c["ones_f"] = of
    p.c["ones_col"] = oc
    v = p.val("cst")
    for e in ENGS:
        p.wait(e, "cst", v)


def emit_ln_tiles(p, ar, src_ap_fn, n_tiles, consume_cb, eps=1e-5, pfx="ln"):
    s_dv = pfx + "_dv"
    xts = [ar.alloc(D, F32) for _ in range(3)]
    sts = [ar.alloc(16, F32) for _ in range(3)]
    dv0 = p.val(s_dv)
    fins = {}
    for i in range(n_tiles):
        xt, st = xts[i % 3], sts[i % 3]
        s_ld = pfx + "_ld%d" % (i % 3)
        if i >= 3:
            for sem, val in fins[i - 3]:
                p.wait("sync", sem, val)
        p.dma("sync", xt[:, :], src_ap_fn(i), s_ld)
        p.wait("vector", s_ld, p.val(s_ld))
        if i >= 3:
            for sem, val in fins[i - 3]:
                p.wait("vector", sem, val)
        p.do("vector", lambda e, xt=xt, st=st: e.bn_stats(st[:, 4:10], xt[:, 0:512]),
             inc=(s_dv, 1))
        p.do("vector", lambda e, xt=xt, st=st: e.bn_stats(st[:, 10:16], xt[:, 512:1024]),
             inc=(s_dv, 1))
        p.wait("vector", s_dv, p.val(s_dv))
        p.do("vector", lambda e, st=st: e.bn_aggr(st[:, 0:2], st[:, 4:16]),
             inc=(s_dv, 1))
        p.wait("scalar", s_dv, p.val(s_dv))
        epsc = p.constf(eps)
        p.do("scalar", lambda e, st=st, epsc=epsc: e.activation(
            st[:, 2:3], st[:, 1:2], AF.Sqrt, bias=epsc, scale=1.0),
            inc=(s_dv, 1))
        p.wait("vector", s_dv, p.val(s_dv))
        p.do("vector", lambda e, st=st: e.reciprocal(st[:, 2:3], st[:, 2:3]),
             inc=(s_dv, 1))
        p.wait("vector", s_dv, p.val(s_dv))
        p.do("vector", lambda e, st=st: e.scalar_tensor_tensor(
            st[:, 3:4], st[:, 0:1], -1.0, st[:, 2:3], ALU.mult, ALU.mult),
            inc=(s_dv, 1))
        fins[i] = consume_cb(i, xt, st, (s_dv, p.val(s_dv)))


def emit_ln_to_fm(p, ar, banks, src_ap_fn, dst_cb, n_tiles=NTT, eps=1e-5,
                  scale_bias=None, extra_cb=None, pfx="lnt", bank_ids=(6, 7)):
    nbs = [ar.alloc(D, BF16) for _ in range(2)]
    pts = [banks[b][:, 0:64].bitcast(BF16) for b in bank_ids]
    s_nm, s_tr, s_cp = pfx + "_nm", pfx + "_tr", pfx + "_cp"
    nm0, tr0, cp0 = p.val(s_nm), p.val(s_tr), p.val(s_cp)
    p.sem(s_cp)

    def consume2(i, xt, st, st_ready):
        nbi = nbs[i % 2]
        p.wait("vector", st_ready[0], st_ready[1])
        if i >= 2:
            p.wait("vector", s_tr, tr0 + DJ * (i - 1))
        p.do("vector", lambda e, xt=xt, st=st, nbi=nbi: e.tensor_scalar(
            nbi[:, :], xt[:, :], st[:, 0:1], st[:, 2:3], ALU.subtract, ALU.mult),
            inc=(s_nm, 1))
        fins = [(s_nm, p.val(s_nm))]
        if extra_cb is not None:
            fins += extra_cb(i, xt, st, st_ready)
        p.wait("tensor", s_nm, p.val(s_nm))
        for j in range(DJ):
            idx = i * DJ + j
            bid = bank_ids[idx % 2]
            pb = pts[idx % 2]
            p.bank_acquire("tensor", bid)
            p.do("tensor", lambda e, pb=pb, nbi=nbi, j=j: e.transpose(
                pb, nbi[:, j * 128:(j + 1) * 128], p.c["ident"][:, :]),
                inc=(s_tr, 1))
            p.wait("vector", s_tr, tr0 + idx + 1)
            dst = dst_cb(i, j)
            rel = (p.bk(bid), 1)
            if scale_bias is None:
                p.do("vector", lambda e, pb=pb, dst=dst: e.tensor_copy(dst, pb),
                     inc=rel)
            else:
                w_sb, b_sb = scale_bias
                p.do("vector", lambda e, pb=pb, dst=dst, j=j:
                     e.tensor_scalar(dst, pb, w_sb[:, j:j + 1], b_sb[:, j:j + 1],
                                     ALU.mult, ALU.add), inc=rel)
        return fins

    emit_ln_tiles(p, ar, src_ap_fn, n_tiles, consume2, eps=eps, pfx=pfx)
    return [(p.bk(bank_ids[0]), p.val(p.bk(bank_ids[0]))),
            (p.bk(bank_ids[1]), p.val(p.bk(bank_ids[1])))]


def emit_proj_fm(p, xn_t, w_sb, out_cb, banks, bank_ids, pfx,
                 n_out_tiles=8, tok_chunks=8, brow=None, kdj=DJ, tok_w=512,
                 xn_nt=NT, w_stride=None):
    s_pe = pfx + "_pe"
    W = w_stride if w_stride is not None else n_out_tiles * 128
    n = 0
    for oc in range(n_out_tiles):
        for tc in range(tok_chunks):
            bid = bank_ids[n % len(bank_ids)]
            pb = banks[bid]
            p.bank_acquire("tensor", bid)
            for j in range(kdj):
                last = (j == kdj - 1 and brow is None)
                p.do("tensor", lambda e, pb=pb, j=j, oc=oc, tc=tc, last=last:
                     e.matmul(
                         pb[:, 0:tok_w],
                         w_sb[:, j * W + oc * 128: j * W + oc * 128 + 128],
                         xn_t[:, j * xn_nt + tc * tok_w: j * xn_nt + (tc + 1) * tok_w],
                         start=(j == 0), stop=last),
                     inc=(s_pe, 1) if last else None)
            if brow is not None:
                p.do("tensor", lambda e, pb=pb, oc=oc: e.matmul(
                    pb[:, 0:tok_w], brow[0:1, oc * 128:oc * 128 + 128],
                    p.c["ones_bf"][0:1, 0:tok_w], start=False, stop=True),
                    inc=(s_pe, 1))
            v = p.val(s_pe)
            out_cb(oc, tc, pb, (s_pe, v), (p.bk(bid), 1))
            n += 1


def emit_proj_tm(p, xn_t, w_sb, out_cb, banks, bank_ids, pfx,
                 out_w=1024, erow=None, kdj=DJ, n_tok_tiles=NTT, xn_nt=NT,
                 tiles=None, w_stride=None):
    s_pe = pfx + "_pe"
    wst = w_stride if w_stride is not None else out_w
    ncol = cdiv(out_w, 512)
    ngrp = len(bank_ids) // ncol
    assert ngrp >= 2
    if tiles is None:
        tiles = range(n_tok_tiles)
    for idx, i in enumerate(tiles):
        g = idx % ngrp
        pbs = []
        for k in range(ncol):
            w0, w1 = k * 512, min(out_w, (k + 1) * 512)
            bid = bank_ids[g * ncol + k]
            pb = banks[bid]
            p.bank_acquire("tensor", bid)
            pbs.append((pb, w0, w1, (p.bk(bid), 1)))
            for j in range(kdj):
                last = (j == kdj - 1 and erow is None and k == ncol - 1)
                p.do("tensor", lambda e, pb=pb, j=j, i=i, w0=w0, w1=w1,
                     last2=(j == kdj - 1 and erow is None): e.matmul(
                         pb[:, 0:w1 - w0],
                         xn_t[:, j * xn_nt + i * 128: j * xn_nt + i * 128 + 128],
                         w_sb[:, j * wst + w0: j * wst + w1],
                         start=(j == 0), stop=last2),
                     inc=(s_pe, 1) if last else None)
            if erow is not None:
                p.do("tensor", lambda e, pb=pb, w0=w0, w1=w1: e.matmul(
                    pb[:, 0:w1 - w0], p.c["ones_bf"][0:1, 0:128],
                    erow[0:1, w0:w1], start=False, stop=True),
                    inc=(s_pe, 1) if k == ncol - 1 else None)
        v = p.val(s_pe)
        out_cb(i, pbs, (s_pe, v))


def emit_attn_causal(p, ar, banks, io, pfx, scale=0.125, stop_after=None):
    H = 16
    m0 = ar.mark()
    v_sb = ar.alloc(NTT * 1040, BF16)
    msk = ar.alloc(4 * 512, BF16)
    for m in range(4):
        p.dma("gpsimd", msk[:, m * 512:(m + 1) * 512], io["masks"][m], pfx + "wm")

    m1 = ar.mark()
    xn_t = ar.alloc(DJ * NT, BF16)
    m_ln = ar.mark()
    xn_specs = emit_ln_to_fm(
        p, ar, banks, lambda i: io["res_in"][i * 128:(i + 1) * 128, :],
        lambda i, j: xn_t[:, j * NT + i * 128: j * NT + i * 128 + 128],
        pfx=pfx + "ln")
    ar.reset(m_ln)
    wqk = ar.alloc(DJ * 1024, BF16)
    wv = ar.alloc(DJ * 1040, BF16)
    ero = ar.alloc(1040, BF16, parts=1)
    bro = ar.alloc(1024, BF16, parts=1)
    bq4 = [ar.alloc(512, BF16) for _ in range(4)]

    for sem, vv in xn_specs:
        p.wait("gpsimd", sem, vv)
    for j in range(DJ):
        p.dma("gpsimd", wqk[:, j * 1024:(j + 1) * 1024], io["wq_t"][j], pfx + "wq")
    p.dma("gpsimd", bro[:, :], io["qb_row"][:, :], pfx + "wq")
    wq_rdy = p.val(pfx + "wq")

    for sem, vv in xn_specs:
        p.wait("tensor", sem, vv)
    p.wait("tensor", pfx + "wq", wq_rdy)

    nst = [0]

    def qk_out(dst):
        def cb(oc, tc, pb, pe_ready, rel):
            k = nst[0]
            sl = bq4[k % 4]
            s_sl = pfx + "qs%d" % (k % 4)
            if k >= 4:
                p.wait("scalar", s_sl, p.val(s_sl))
            p.wait("scalar", pe_ready[0], pe_ready[1])
            p.do("scalar", lambda e, pb=pb, sl=sl: e.copy(sl[:, :], pb[:, :]),
                 inc=rel)
            p.wait("sync", rel[0], p.val(rel[0]))
            p.dma("sync", dst[2 * oc, 0:64, tc * 512:(tc + 1) * 512],
                  sl[0:64, :], s_sl)
            p.dma("sync", dst[2 * oc + 1, 0:64, tc * 512:(tc + 1) * 512],
                  sl[64:128, :], s_sl)
            nst[0] += 1
        return cb

    emit_proj_fm(p, xn_t, wqk, qk_out(io["qt"]), banks, (0, 1), pfx + "q",
                 brow=bro)
    p.wait("gpsimd", pfx + "q_pe", p.val(pfx + "q_pe"))
    for j in range(DJ):
        p.dma("gpsimd", wqk[:, j * 1024:(j + 1) * 1024], io["wk_t"][j], pfx + "wk")
    p.dma("gpsimd", bro[:, :], io["kb_row"][:, :], pfx + "wk")
    wk_rdy = p.val(pfx + "wk")
    p.wait("tensor", pfx + "wk", wk_rdy)
    emit_proj_fm(p, xn_t, wqk, qk_out(io["kt"]), banks, (0, 1), pfx + "k",
                 brow=bro)
    for h in range(H):
        for cch in range(8):
            p.dma("sync", io["kt"][h, 64:65, cch * 512:(cch + 1) * 512],
                  p.c["ones_bf"][0:1, 0:512], pfx + "qso")

    p.wait("gpsimd", pfx + "k_pe", p.val(pfx + "k_pe"))
    for j in range(DJ):
        p.dma("gpsimd", wv[:, j * 1040:(j + 1) * 1040], io["wv_aug"][j], pfx + "wv")
    p.dma("gpsimd", ero[:, :], io["erow_v"][:, :], pfx + "wv")
    wv_rdy = p.val(pfx + "wv")
    p.wait("tensor", pfx + "wv", wv_rdy)

    def v_out(i, pbs, pe_ready):
        p.wait("vector", pe_ready[0], pe_ready[1])
        for n, (pb, w0, w1, rel) in enumerate(pbs):
            p.do("vector", lambda e, pb=pb, i=i, w0=w0, w1=w1: e.tensor_copy(
                v_sb[:, i * 1040 + w0: i * 1040 + w1], pb[:, 0:w1 - w0]),
                inc=rel)

    emit_proj_tm(p, xn_t, wv, v_out, banks, (2, 3, 4, 5, 6, 7), pfx + "v",
                 out_w=1040, erow=ero)
    v_fin = [(p.bk(b), p.val(p.bk(b))) for b in (2, 3, 4, 5, 6, 7)]
    if stop_after == "qkv":
        for kk in range(4):
            sname = pfx + "qs%d" % kk
            p.wait("sync", sname, p.val(sname))
        p.wait("sync", pfx + "qso", p.val(pfx + "qso"))
        for sem, vv in v_fin:
            p.wait("sync", sem, vv)
        return

    ar.reset(m1)
    m2 = ar.mark()
    qh = [ar.alloc(NT, BF16) for _ in range(2)]
    kh = [ar.alloc(NT, BF16) for _ in range(2)]
    mx = ar.alloc(NTT, F32)
    mxc = ar.alloc(8, F32)
    mxs = ar.alloc(128, BF16)
    s_pa = pfx + "pa"
    s_pam = pfx + "pam"
    s_mx = pfx + "mxv"
    s_mt = pfx + "mtr"
    s_mc = pfx + "mcp"
    s_pad = pfx + "pad"
    p.sem(s_pad)
    for kk in range(4):
        sname = pfx + "qs%d" % kk
        p.wait("sync", sname, p.val(sname))
    p.wait("sync", pfx + "qso", p.val(pfx + "qso"))
    p.wait("sync", pfx + "v_pe", p.val(pfx + "v_pe"))
    for sem, vv in v_fin:
        p.wait("sync", sem, vv)
    for h in range(H):
        qsl, ksl = qh[h % 2], kh[h % 2]
        s_hl = pfx + "hl%d" % (h % 2)
        if h >= 2:
            p.wait("sync", s_pad, p.val(s_pad))
        p.dma("sync", qsl[0:64, :], io["qt"][h, 0:64, :], s_hl)
        p.dma("sync", ksl[0:64, :], io["kt"][h, 0:64, :], s_hl)
        p.wait("tensor", s_hl, p.val(s_hl))
        nk = [0]
        s_cmb = pfx + "cmb"
        for qt_i in range(NTT):
            kcmax = (qt_i * 128 + 127) // 512 + 1
            p.wait("vector", s_cmb, p.val(s_cmb))
            for kc in range(kcmax):
                bid = nk[0] % 2
                pb = banks[bid]
                p.bank_acquire("tensor", bid)
                p.do("tensor", lambda e, pb=pb, qsl=qsl, ksl=ksl, qt_i=qt_i,
                     kc=kc: e.matmul(
                         pb[:, :], qsl[0:64, qt_i * 128:qt_i * 128 + 128],
                         ksl[0:64, kc * 512:(kc + 1) * 512],
                         start=True, stop=True), inc=(s_pam, 1))
                p.wait("vector", s_pam, p.val(s_pam))
                p.do("vector", lambda e, pb=pb, kc=kc: e.tensor_reduce(
                    mxc[:, kc:kc + 1], pb[:, :], AX.X, ALU.max),
                    inc=(p.bk(bid), 1))
                nk[0] += 1
            p.wait("vector", p.bk(0), p.val(p.bk(0)))
            p.wait("vector", p.bk(1), p.val(p.bk(1)))
            p.do("vector", lambda e, qt_i=qt_i, kcmax=kcmax: e.tensor_reduce(
                mx[:, qt_i:qt_i + 1], mxc[:, 0:kcmax], AX.X, ALU.max),
                inc=(s_cmb, 1))
        p.wait("vector", s_cmb, p.val(s_cmb))
        p.do("vector", lambda e: e.tensor_scalar_mul(mx[:, :], mx[:, :], -1.0),
             inc=(s_mx, 1))
        p.wait("tensor", s_mx, p.val(s_mx))
        p.bank_acquire("tensor", 2)
        p.do("tensor", lambda e: e.transpose(
            banks[2][0:32, 0:128], mx[:, :], p.c["ident_f"][:, :]),
            inc=(s_mt, 1))
        p.wait("scalar", s_mt, p.val(s_mt))
        p.wait("scalar", s_pad, p.val(s_pad))
        p.do("scalar", lambda e: e.copy(mxs[0:32, :], banks[2][0:32, 0:128]),
             inc=(p.bk(2), 1))
        p.wait("sync", p.bk(2), p.val(p.bk(2)))
        p.dma("sync",
              io["qt"][h, 64:65, :].rearrange("o (t q) -> (o t) q", q=128),
              mxs[0:32, :], s_pad)
    passa_done = p.val(s_pad)
    if stop_after == "passa":
        p.wait("sync", s_pad, passa_done)
        return

    ar.reset(m2)
    ctx_sb = ar.alloc(DJ * NT, BF16)
    m3 = ar.mark()
    qh2 = [ar.alloc(NT, BF16) for _ in range(2)]
    kh2 = [ar.alloc(NT, BF16) for _ in range(2)]
    psl = [ar.alloc(512, BF16) for _ in range(3)]
    tmp64 = ar.alloc(512, BF16, parts=64)
    rl = ar.alloc(512, F32, parts=1)
    bl = ar.alloc(512, F32, parts=64)
    s_smm = pfx + "smm"
    s_sx = pfx + "sx"
    s_cm = pfx + "cm"
    s_ce = pfx + "ce"
    s_od = pfx + "od"
    p.sem(s_ce)
    p.sem(s_od)
    p.wait("sync", s_pad, passa_done)
    for sem, vv in v_fin:
        p.wait("tensor", sem, vv)
    p.wait("vector", pfx + "wm", p.val(pfx + "wm"))
    for h in range(H):
        qsl, ksl = qh2[h % 2], kh2[h % 2]
        s_h2 = pfx + "h2%d" % (h % 2)
        if h >= 2:
            p.wait("sync", s_cm, p.val(s_cm))
        p.dma("sync", qsl[0:65, :], io["qt"][h, 0:65, :], s_h2)
        p.dma("sync", ksl[0:65, :], io["kt"][h, 0:65, :], s_h2)
        p.wait("tensor", s_h2, p.val(s_h2))
        for qc in range(8):
            nkb = 4 * qc + 4
            exp_done = {}
            p.bank_acquire("tensor", 4)
            for kb in range(nkb + 1):
                if kb < nkb:
                    bid = kb % 2
                    pb = banks[bid]
                    p.bank_acquire("tensor", bid)
                    p.do("tensor", lambda e, pb=pb, qsl=qsl, ksl=ksl, kb=kb,
                         qc=qc: e.matmul(
                             pb[:, :], ksl[0:65, kb * 128:kb * 128 + 128],
                             qsl[0:65, qc * 512:(qc + 1) * 512],
                             start=True, stop=True), inc=(s_smm, 1))
                    sl = psl[kb % 3]
                    p.wait("scalar", s_smm, p.val(s_smm))
                    if kb >= 3:
                        p.wait("scalar", s_cm, p.val(s_cm))
                    scc = p.constf(scale)
                    p.do("scalar", lambda e, pb=pb, sl=sl, scc=scc: e.activation(
                        sl[:, :], pb[:, :], AF.Exp, scale=scc),
                        inc=(p.bk(bid), 1))
                    dj = kb - 4 * qc
                    if dj >= 0:
                        p.wait("vector", p.bk(bid), p.val(p.bk(bid)))
                        p.do("vector", lambda e, sl=sl, dj=dj: e.tensor_tensor(
                            sl[:, :], sl[:, :], msk[:, dj * 512:(dj + 1) * 512],
                            ALU.mult), inc=(s_sx, 1))
                        exp_done[kb] = (s_sx, p.val(s_sx))
                    else:
                        exp_done[kb] = (p.bk(bid), p.val(p.bk(bid)))
                if kb >= 1:
                    kbp = kb - 1
                    slp = psl[kbp % 3]
                    p.wait("tensor", exp_done[kbp][0], exp_done[kbp][1])
                    p.do("tensor", lambda e, slp=slp, kbp=kbp, h=h, qc=qc,
                         nkb=nkb: e.matmul(
                             banks[4][0:65, :],
                             v_sb[:, kbp * 1040 + h * 65: kbp * 1040 + h * 65 + 65],
                             slp[:, :], start=(kbp == 0), stop=(kbp == nkb - 1)),
                         inc=(s_cm, 1))
            p.wait("vector", s_cm, p.val(s_cm))
            p.wait("vector", pfx + "blm", p.val(pfx + "blm"))
            p.do("vector", lambda e: e.reciprocal(rl[:, :], banks[4][64:65, :]),
                 inc=(pfx + "rl", 1))
            p.wait("tensor", pfx + "rl", p.val(pfx + "rl"))
            p.bank_acquire("tensor", 5)
            p.do("tensor", lambda e: e.matmul(
                banks[5][0:64, :], p.c["ones_f"][0:1, 0:64], rl[:, :],
                start=True, stop=True), inc=(pfx + "blm", 1))
            p.wait("scalar", pfx + "blm", p.val(pfx + "blm"))
            p.wait("scalar", p.bk(4), p.val(p.bk(4)))
            p.do("scalar", lambda e: e.copy(bl[:, :], banks[5][0:64, :]),
                 inc=(p.bk(5), 1))
            p.wait("vector", p.bk(5), p.val(p.bk(5)))
            col0 = (h // 2) * NT + qc * 512
            if h % 2 == 0:
                p.do("vector", lambda e, col0=col0: e.tensor_tensor(
                    ctx_sb[0:64, col0:col0 + 512], banks[4][0:64, :],
                    bl[:, :], ALU.mult), inc=(p.bk(4), 1))
            else:
                if qc >= 1 or h >= 2:
                    p.wait("vector", s_od, p.val(s_od))
                p.do("vector", lambda e: e.tensor_tensor(
                    tmp64[:, :], banks[4][0:64, :], bl[:, :], ALU.mult),
                    inc=(p.bk(4), 1))
                p.wait("sync", p.bk(4), p.val(p.bk(4)))
                p.dma("sync", ctx_sb[64:128, col0:col0 + 512], tmp64[:, :],
                      s_od)
    ctx_fin = [(p.bk(4), p.val(p.bk(4))), (s_od, p.val(s_od))]
    if stop_after == "passb":
        for sem, vv in ctx_fin:
            p.wait("sync", sem, vv)
        return

    ar.reset(m3)
    wo = ar.alloc(DJ * 1024, BF16)
    bor = ar.alloc(1024, BF16, parts=1)
    rts = [ar.alloc(D, F32) for _ in range(2)]
    ros = [ar.alloc(D, F32) for _ in range(2)]
    s_w2 = pfx + "w2"
    p.wait("gpsimd", s_cm, p.val(s_cm))
    for sem, vv in ctx_fin:
        p.wait("gpsimd", sem, vv)
    p.wait("gpsimd", pfx + "blm", p.val(pfx + "blm"))
    p.wait("gpsimd", p.bk(5), p.val(p.bk(5)))
    for j in range(DJ):
        p.dma("gpsimd", wo[:, j * 1024:(j + 1) * 1024], io["wo_t"][j], s_w2)
    p.dma("gpsimd", bor[:, :], io["bo_row"][:, :], s_w2)
    w2_rdy = p.val(s_w2)
    s_pw = pfx + "pw"
    p.wait("tensor", s_w2, w2_rdy)
    for sem, vv in ctx_fin:
        p.wait("tensor", sem, vv)
    rt_guard = {}
    for i in range(NTT):
        rti, roi = rts[i % 2], ros[i % 2]
        s_rl2 = pfx + "rld%d" % (i % 2)
        s_ro = pfx + "rout%d" % (i % 2)
        if i >= 2:
            for sem, vv in rt_guard[i - 2]:
                p.wait("sync", sem, vv)
        p.dma("sync", rti[:, :], io["res_in"][i * 128:(i + 1) * 128, :], s_rl2)
        g4 = 4 * (i % 2)
        for k in range(2):
            bid_e = g4 + k
            pbe = banks[bid_e]
            p.bank_acquire("tensor", bid_e)
            for n, h in enumerate(range(0, H, 2)):
                p.do("tensor", lambda e, pbe=pbe, h=h, i=i, k=k, n=n: e.matmul(
                    pbe[:, :],
                    ctx_sb[0:64,
                           (h // 2) * NT + i * 128:(h // 2) * NT + i * 128 + 128],
                    wo[0:64,
                       (h // 2) * 1024 + k * 512:(h // 2) * 1024 + (k + 1) * 512],
                    start=(n == 0), stop=False))
            p.do("tensor", lambda e, pbe=pbe, k=k: e.matmul(
                pbe[:, :], p.c["ones_bf"][0:1, 0:128],
                bor[0:1, k * 512:(k + 1) * 512], start=False, stop=True),
                inc=(s_pw, 1))
            bid_o = g4 + 2 + k
            pbo = banks[bid_o]
            p.bank_acquire("tensor", bid_o)
            for n, h in enumerate(range(1, H, 2)):
                p.do("tensor", lambda e, pbo=pbo, h=h, i=i, k=k, n=n: e.matmul(
                    pbo[:, :],
                    ctx_sb[64:128,
                           (h // 2) * NT + i * 128:(h // 2) * NT + i * 128 + 128],
                    wo[64:128,
                       (h // 2) * 1024 + k * 512:(h // 2) * 1024 + (k + 1) * 512],
                    start=(n == 0), stop=(n == H // 2 - 1)),
                    inc=(s_pw, 1) if n == H // 2 - 1 else None)
        p.wait("vector", s_pw, p.val(s_pw))
        p.wait("vector", s_rl2, p.val(s_rl2))
        if i >= 2:
            p.wait("vector", s_ro, p.val(s_ro))
        specs = []
        for k in range(2):
            bid_e, bid_o = g4 + k, g4 + 2 + k
            p.do("vector", lambda e, pb=banks[bid_e], rti=rti, roi=roi, k=k:
                 e.tensor_tensor(roi[:, k * 512:(k + 1) * 512], pb[:, :],
                                 rti[:, k * 512:(k + 1) * 512], ALU.add),
                 inc=(p.bk(bid_e), 1))
            specs.append((p.bk(bid_e), p.val(p.bk(bid_e))))
            p.wait("vector", p.bk(bid_e), p.val(p.bk(bid_e)))
            p.do("vector", lambda e, pb=banks[bid_o], roi=roi, k=k:
                 e.tensor_tensor(roi[:, k * 512:(k + 1) * 512],
                                 roi[:, k * 512:(k + 1) * 512], pb[:, :],
                                 ALU.add),
                 inc=(p.bk(bid_o), 1))
            specs.append((p.bk(bid_o), p.val(p.bk(bid_o))))
        rt_guard[i] = specs
        for sem, vv in specs:
            p.wait("sync", sem, vv)
        p.dma("sync", io["res_out"][i * 128:(i + 1) * 128, :], roi[:, :], s_ro)
    p.wait("sync", pfx + "rout0", p.val(pfx + "rout0"))
    p.wait("sync", pfx + "rout1", p.val(pfx + "rout1"))
    ar.reset(m0)


def make_vaug(Wv_in_out):
    w = Wv_in_out.reshape(1024, 16, 64)
    aug = np.zeros((1024, 16, 65), np.float32)
    aug[:, :, 0:64] = w
    return np.ascontiguousarray(aug.reshape(1024, 1040)).reshape(DJ, 128, 1040)


def make_erow(bv):
    e = np.zeros((16, 65), np.float32)
    e[:, 0:64] = bv.reshape(16, 64)
    e[:, 64] = 1.0
    return e.reshape(1, 1040).copy()


def make_masks():
    r = np.arange(128)[:, None]
    c = np.arange(512)[None, :]
    return np.stack([(c >= r + 128 * j).astype(np.float32) for j in range(4)])


def prep_l0_attn(ins):
    W_qkv = np.asarray(ins["L0_attn_W_qkv"], np.float32)
    Wq = np.ascontiguousarray(W_qkv[0].reshape(1024, 1024).T)
    Wk = np.ascontiguousarray(W_qkv[1].reshape(1024, 1024).T)
    Wv = np.ascontiguousarray(W_qkv[2].reshape(1024, 1024).T)
    WoT = np.ascontiguousarray(np.asarray(ins["L0_attn_W_o"], np.float32).T)
    z = np.zeros((1, 1024), np.float32)
    return dict(
        wq_t=np.ascontiguousarray(Wq.reshape(DJ, 128, 1024)),
        wk_t=np.ascontiguousarray(Wk.reshape(DJ, 128, 1024)),
        wv_aug=np.ascontiguousarray(make_vaug(Wv)),
        erow_v=make_erow(np.zeros((1024,), np.float32)),
        qb_row=z.copy(), kb_row=z.copy(),
        wo_t=np.ascontiguousarray(WoT.reshape(DJ, 128, 1024)),
        bo_row=z.copy(),
        masks=make_masks(),
    )


def prep_l2_attn(ins):
    f = lambda k: np.ascontiguousarray(np.asarray(ins[k], np.float32))
    return dict(
        wq_t=f("L2_attn_Wq").reshape(DJ, 128, 1024),
        wk_t=f("L2_attn_Wk").reshape(DJ, 128, 1024),
        wv_aug=np.ascontiguousarray(make_vaug(f("L2_attn_Wv"))),
        erow_v=make_erow(np.asarray(ins["L2_attn_bv"], np.float32)),
        qb_row=np.asarray(ins["L2_attn_bq"], np.float32).reshape(1, 1024),
        kb_row=np.asarray(ins["L2_attn_bk"], np.float32).reshape(1, 1024),
        wo_t=f("L2_attn_Wo").reshape(DJ, 128, 1024),
        bo_row=np.asarray(ins["L2_attn_bo"], np.float32).reshape(1, 1024),
        masks=make_masks(),
    )


def const_inputs():
    return dict(ident=np.eye(128, dtype=BF), ident_f=np.eye(128, dtype=np.float32))


ATTN_IO_SPECS = [("wq_t", [DJ, 128, 1024]), ("wk_t", [DJ, 128, 1024]),
                 ("wv_aug", [DJ, 128, 1040]), ("erow_v", [1, 1040]),
                 ("qb_row", [1, 1024]), ("kb_row", [1, 1024]),
                 ("wo_t", [DJ, 128, 1024]), ("bo_row", [1, 1024]),
                 ("masks", [4, 128, 512])]


def build_attn_launch(pfx="a0", scale=0.125, stop_after=None):
    nc = bass.Bass("TRN2", target_bir_lowering=False)
    stack = ExitStack()
    io = dict(
        res_in=nc.dram_tensor("res_in", [NT, D], F32, kind="ExternalInput"),
        res_out=nc.dram_tensor("res_out", [NT, D], F32, kind="ExternalOutput"),
        qt=nc.dram_tensor("qt_scr", [16, 65, NT], BF16),
        kt=nc.dram_tensor("kt_scr", [16, 65, NT], BF16),
    )
    for nm, shp in ATTN_IO_SPECS:
        io[nm] = nc.dram_tensor(nm, shp, F32, kind="ExternalInput")
    cd = dict(
        ident=nc.dram_tensor("ident", [128, 128], BF16, kind="ExternalInput"),
        ident_f=nc.dram_tensor("ident_f", [128, 128], F32, kind="ExternalInput"),
    )
    p = Prog(nc, stack)
    with stack:
        banks = make_banks(p)
        ar = Arena(p, 176 * 1024)
        emit_const_setup(p, cd)
        emit_attn_causal(p, ar, banks, io, pfx, scale=scale, stop_after=stop_after)
        p.finalize()
    return nc




def emit_l2_ffn(p, ar, banks, io, pfx):
    CH = 512
    NCH = NT // CH
    TPC = CH // 128
    m0 = ar.mark()
    wr = ar.alloc(DJ * 1024, BF16)
    wk = ar.alloc(DJ * 1024, BF16)
    wv = ar.alloc(DJ * 1024, BF16)
    mu = ar.alloc(8, F32)
    mur = ar.alloc(DJ, F32)
    murn = ar.alloc(DJ, F32)
    muk = ar.alloc(DJ, F32)
    mukn = ar.alloc(DJ, F32)
    s_w = pfx + "w"
    for j in range(DJ):
        p.dma("gpsimd", wr[:, j * 1024:(j + 1) * 1024], io["wr_t"][j], s_w)
        p.dma("gpsimd", wk[:, j * 1024:(j + 1) * 1024], io["f2k_t"][j], s_w)
        p.dma("gpsimd", wv[:, j * 1024:(j + 1) * 1024], io["wv_t"][j], s_w)
    p.dma("gpsimd", mur[:, :], io["mu_r"][:, :], s_w)
    p.dma("gpsimd", muk[:, :], io["mu_k"][:, :], s_w)
    w_rdy = p.val(s_w)
    p.wait("vector", s_w, w_rdy)
    p.do("vector", lambda e: e.tensor_scalar(murn[:, :], mur[:, :], -1.0, 1.0,
                                             ALU.mult, ALU.add), inc=(pfx + "mu", 1))
    p.do("vector", lambda e: e.tensor_scalar(mukn[:, :], muk[:, :], -1.0, 1.0,
                                             ALU.mult, ALU.add), inc=(pfx + "mu", 1))
    p.wait("vector", pfx + "mu", p.val(pfx + "mu"))
    p.wait("tensor", s_w, w_rdy)

    XW = CH + 1
    xns = [ar.alloc(DJ * XW, BF16) for _ in range(2)]
    xrt = ar.alloc(DJ * CH, BF16)
    xkt = ar.alloc(DJ * CH, BF16)
    k2t = ar.alloc(DJ * CH, BF16)
    rtk = ar.alloc(TPC * 1024, BF16)
    rst = [ar.alloc(D, F32) for _ in range(2)]
    rso = [ar.alloc(D, F32) for _ in range(2)]
    tmpv = ar.alloc(1024, F32)
    p.do("gpsimd", lambda e: e.memset(xns[0][:, :], 0.0), inc=(pfx + "hz", 1))
    p.wait("vector", pfx + "hz", 1)
    s_mv = pfx + "mv"
    s_xr = pfx + "xr"
    guards = {"r_pe": None, "k_pe": None, "v_pe": None, "mv": 0}

    for c in range(NCH):
        xn = xns[c % 2]
        if c >= 2:
            p.wait("vector", s_xr, guards["xr%d" % (c - 2)])
            p.wait("vector", pfx + "hc", p.val(pfx + "hc"))
        if c >= 1:
            prevnm = pfx + "ln_nm"
            p.wait("sync", prevnm, p.val(prevnm))
        mln = ar.mark()
        specs = emit_ln_to_fm(
            p, ar, banks,
            lambda i, c=c: io["res_in"][c * CH + i * 128: c * CH + (i + 1) * 128, :],
            lambda i, j, xn=xn: xn[:, j * XW + 1 + i * 128: j * XW + 1 + i * 128 + 128],
            n_tiles=TPC, pfx=pfx + "ln")
        ar.reset(mln)
        for sem, vv in specs:
            p.wait("vector", sem, vv)
        if c > 0:
            prev = xns[(c - 1) % 2]
            p.do("vector", lambda e, xn=xn, prev=prev: e.tensor_copy(
                xn[:, 0:DJ * XW].rearrange("p (j w) -> p j w", w=XW)[:, :, 0:1],
                prev[:, 0:DJ * XW].rearrange("p (j w) -> p j w", w=XW)[:, :, CH:CH + 1]),
                inc=(pfx + "hc", 1))
            p.wait("vector", pfx + "hc", p.val(pfx + "hc"))
        if guards["r_pe"] is not None:
            p.wait("vector", guards["r_pe"][0], guards["r_pe"][1])
            p.wait("vector", guards["k_pe"][0], guards["k_pe"][1])
        for j in range(DJ):
            cur = xn[:, j * XW + 1: j * XW + 1 + CH]
            prv = xn[:, j * XW: j * XW + CH]
            for dst, m0_, m1_ in ((xrt, murn, mur), (xkt, mukn, muk)):
                p.do("vector", lambda e, j=j, dst=dst, prv=prv, m0_=m0_:
                     e.tensor_scalar(dst[:, j * CH:(j + 1) * CH], prv,
                                     m0_[:, j:j + 1], None, ALU.mult, ALU.bypass),
                     inc=(s_xr, 1))
                p.wait("vector", s_xr, p.val(s_xr))
                p.do("vector", lambda e, j=j, dst=dst, cur=cur, m1_=m1_:
                     e.scalar_tensor_tensor(dst[:, j * CH:(j + 1) * CH], cur,
                                            m1_[:, j:j + 1],
                                            dst[:, j * CH:(j + 1) * CH],
                                            ALU.mult, ALU.add), inc=(s_xr, 1))
        guards["xr%d" % c] = p.val(s_xr)
        p.wait("tensor", s_xr, p.val(s_xr))

        mv_guard = guards["mv"]
        first_r = [True]

        def r_out(i, pbs, pe_ready, c=c, mv_guard=mv_guard, first_r=first_r):
            p.wait("scalar", pe_ready[0], pe_ready[1])
            if first_r[0] and mv_guard:
                p.wait("scalar", s_mv, mv_guard)
                first_r[0] = False
            for pb, w0, w1, rel in pbs:
                p.do("scalar", lambda e, pb=pb, i=i, w0=w0, w1=w1: e.activation(
                    rtk[:, i * 1024 + w0: i * 1024 + w1], pb[:, 0:w1 - w0],
                    AF.Sigmoid), inc=rel)
        emit_proj_tm(p, xrt, wr, r_out, banks, (0, 1, 2, 3), pfx + "r",
                     out_w=1024, kdj=DJ, xn_nt=CH, n_tok_tiles=TPC)
        guards["r_pe"] = (pfx + "r_pe", p.val(pfx + "r_pe"))

        if guards["v_pe"] is not None:
            p.wait("scalar", guards["v_pe"][0], guards["v_pe"][1])

        def k_out(oc, tc, pb, pe_ready, rel, c=c):
            p.wait("scalar", pe_ready[0], pe_ready[1])
            dst = k2t[:, oc * CH + tc * 512: oc * CH + tc * 512 + 512]
            p.do("scalar", lambda e, pb=pb, dst=dst: e.activation(
                dst, pb[:, :], AF.Relu), inc=rel)
            p.wait("vector", rel[0], p.val(rel[0]))
            p.do("vector", lambda e, dst=dst: e.tensor_tensor(
                dst, dst, dst, ALU.mult), inc=(pfx + "k2", 1))
        emit_proj_fm(p, xkt, wk, k_out, banks, (4, 5), pfx + "k",
                     n_out_tiles=8, tok_chunks=1, kdj=DJ, tok_w=512, xn_nt=CH)
        guards["k_pe"] = (pfx + "k_pe", p.val(pfx + "k_pe"))
        p.wait("tensor", pfx + "k2", p.val(pfx + "k2"))

        def v_out(i, pbs, pe_ready, c=c):
            ii = c * TPC + i
            rti, roi = rst[ii % 2], rso[ii % 2]
            s_rl = pfx + "rld%d" % (ii % 2)
            s_ro = pfx + "rout%d" % (ii % 2)
            if ii >= 2:
                p.wait("sync", s_mv, p.val(s_mv))
            p.dma("sync", rti[:, :], io["res_in"][ii * 128:(ii + 1) * 128, :],
                  s_rl)
            p.wait("vector", pe_ready[0], pe_ready[1])
            p.wait("vector", s_rl, p.val(s_rl))
            if ii >= 2:
                p.wait("vector", s_ro, p.val(s_ro))
            for pb, w0, w1, rel in pbs:
                p.wait("vector", s_mv, p.val(s_mv))
                p.do("vector", lambda e, pb=pb, i=i, w0=w0, w1=w1: e.tensor_tensor(
                    tmpv[:, w0:w1], pb[:, 0:w1 - w0],
                    rtk[:, i * 1024 + w0: i * 1024 + w1], ALU.mult), inc=rel)
                p.wait("vector", rel[0], p.val(rel[0]))
                p.do("vector", lambda e, rti=rti, roi=roi, w0=w0, w1=w1:
                     e.tensor_tensor(roi[:, w0:w1], tmpv[:, w0:w1],
                                     rti[:, w0:w1], ALU.add), inc=(s_mv, 1))
            p.wait("sync", s_mv, p.val(s_mv))
            p.dma("sync", io["res_out"][ii * 128:(ii + 1) * 128, :], roi[:, :],
                  s_ro)
        emit_proj_tm(p, k2t, wv, v_out, banks, (0, 1, 2, 3), pfx + "v",
                     out_w=1024, kdj=DJ, xn_nt=CH, n_tok_tiles=TPC)
        guards["v_pe"] = (pfx + "v_pe", p.val(pfx + "v_pe"))
        guards["mv"] = p.val(s_mv)
    p.wait("sync", pfx + "rout0", p.val(pfx + "rout0"))
    p.wait("sync", pfx + "rout1", p.val(pfx + "rout1"))
    ar.reset(m0)


def prep_l2_ffn(ins):
    f = lambda k: np.ascontiguousarray(np.asarray(ins[k], np.float32))
    return dict(
        wr_t=np.ascontiguousarray(f("L2_ffn_W_r").T.reshape(DJ, 128, 1024)),
        f2k_t=np.ascontiguousarray(f("L2_ffn_W_k").T.reshape(DJ, 128, 1024)),
        wv_t=np.ascontiguousarray(f("L2_ffn_W_v").T.reshape(DJ, 128, 1024)),
        mu_r=np.ascontiguousarray(f("L2_ffn_mu_r").reshape(DJ, 128).T),
        mu_k=np.ascontiguousarray(f("L2_ffn_mu_k").reshape(DJ, 128).T),
    )


L2FFN_IO_SPECS = [("wr_t", [DJ, 128, 1024]), ("f2k_t", [DJ, 128, 1024]),
                  ("wv_t", [DJ, 128, 1024]), ("mu_r", [128, DJ]),
                  ("mu_k", [128, DJ])]


def emit_l1_attn(p, ar, banks, io, pfx):
    EPS = 1e-6
    m0 = ar.mark()
    kv_sb = ar.alloc(DJ * 1024, BF16)
    ksum = ar.alloc(8, BF16)
    ksf = ar.alloc(8, F32)
    wslot = ar.alloc(DJ * 2048, BF16)
    brow = ar.alloc(1024, BF16, parts=1)

    m1 = ar.mark()
    xn_t = ar.alloc(DJ * NT, BF16)
    mln = ar.mark()
    xn_specs = emit_ln_to_fm(
        p, ar, banks, lambda i: io["res_in"][i * 128:(i + 1) * 128, :],
        lambda i, j: xn_t[:, j * NT + i * 128: j * NT + i * 128 + 128],
        pfx=pfx + "ln")
    ar.reset(mln)
    kst = [ar.alloc(1024, BF16) for _ in range(2)]
    vst = [ar.alloc(1024, BF16) for _ in range(2)]
    qst = [ar.alloc(512, BF16) for _ in range(3)]

    s_wv = pfx + "wkv"
    for sem, vv in xn_specs:
        p.wait("gpsimd", sem, vv)
    for j in range(DJ):
        p.dma("gpsimd", wslot[:, j * 2048:(j + 1) * 2048], io["wkv_t"][j], s_wv)
    p.wait("tensor", s_wv, p.val(s_wv))
    for sem, vv in xn_specs:
        p.wait("tensor", sem, vv)

    def kv_out(i, pbs, pe_ready):
        ks, vs = kst[i % 2], vst[i % 2]
        s_ks = pfx + "ks%d" % (i % 2)
        s_vs = pfx + "vs%d" % (i % 2)
        if i >= 2:
            p.wait("scalar", s_ks, p.val(s_ks))
            p.wait("vector", s_vs, p.val(s_vs))
        p.wait("scalar", pe_ready[0], pe_ready[1])
        p.wait("vector", pe_ready[0], pe_ready[1])
        vrel = []
        for pb, w0, w1, rel in pbs:
            if w0 < 1024:
                p.do("scalar", lambda e, pb=pb, ks=ks, w0=w0, w1=w1:
                     e.activation(ks[:, w0:w1], pb[:, 0:w1 - w0], AF.Relu),
                     inc=rel)
                p.wait("vector", rel[0], p.val(rel[0]))
                p.do("vector", lambda e, ks=ks, w0=w0, w1=w1: e.tensor_scalar(
                    ks[:, w0:w1], ks[:, w0:w1], float(EPS), None, ALU.add,
                    ALU.bypass), inc=(pfx + "ke", 1))
            else:
                p.do("vector", lambda e, pb=pb, vs=vs, w0=w0, w1=w1:
                     e.tensor_copy(vs[:, w0 - 1024:w1 - 1024], pb[:, 0:w1 - w0]),
                     inc=rel)
                vrel.append((rel[0], p.val(rel[0])))
        p.wait("sync", pfx + "ke", p.val(pfx + "ke"))
        p.dma("sync", io["k_d"][i], ks[:, :], s_ks)
        for sem, vv in vrel:
            p.wait("sync", sem, vv)
        p.dma("sync", io["v_d"][i], vs[:, :], s_vs)

    emit_proj_tm(p, xn_t, wslot, kv_out, banks, (0, 1, 2, 3, 4, 5, 6, 7),
                 pfx + "kv", out_w=2048, kdj=DJ)

    s_wq = pfx + "wq"
    p.wait("gpsimd", pfx + "kv_pe", p.val(pfx + "kv_pe"))
    for j in range(DJ):
        p.dma("gpsimd", wslot[:, j * 2048:j * 2048 + 1024], io["wq_t"][j], s_wq)
    p.wait("tensor", s_wq, p.val(s_wq))
    nq = [0]

    def q_out(oc, tc, pb, pe_ready, rel):
        k = nq[0]
        qs = qst[k % 3]
        s_qs = pfx + "qs%d" % (k % 3)
        if k >= 3:
            p.wait("scalar", s_qs, p.val(s_qs))
        p.wait("scalar", pe_ready[0], pe_ready[1])
        p.do("scalar", lambda e, pb=pb, qs=qs: e.activation(
            qs[:, :], pb[:, :], AF.Relu), inc=rel)
        p.wait("vector", rel[0], p.val(rel[0]))
        p.do("vector", lambda e, qs=qs: e.tensor_scalar(
            qs[:, :], qs[:, :], float(EPS), None, ALU.add, ALU.bypass),
            inc=(pfx + "qe", 1))
        p.wait("sync", pfx + "qe", p.val(pfx + "qe"))
        p.dma("sync", io["q_d"][oc, :, tc * 512:(tc + 1) * 512], qs[:, :], s_qs)
        nq[0] += 1

    emit_proj_fm(p, xn_t, wslot, q_out, banks, (0, 1), pfx + "q",
                 n_out_tiles=8, tok_chunks=8, kdj=DJ, tok_w=512, w_stride=2048)

    ar.reset(m1)
    at_t = ar.alloc(DJ * NT, BF16)
    zb = ar.alloc(NT, F32)
    rz = zb[0:1, :]
    ksl = [ar.alloc(1024, BF16) for _ in range(2)]
    vsl = [ar.alloc(512, BF16) for _ in range(2)]
    qpc = [ar.alloc(DJ * 512, BF16) for _ in range(2)]
    rst = [ar.alloc(D, F32) for _ in range(2)]
    rso = [ar.alloc(D, F32) for _ in range(2)]
    p.wait("sync", pfx + "q_pe", p.val(pfx + "q_pe"))
    for s in range(2):
        p.wait("sync", pfx + "ks%d" % s, p.val(pfx + "ks%d" % s))
        p.wait("sync", pfx + "vs%d" % s, p.val(pfx + "vs%d" % s))
    for s in range(3):
        p.wait("sync", pfx + "qs%d" % s, p.val(pfx + "qs%d" % s))
    p.bank_acquire("tensor", 0)
    for i in range(NTT):
        sl = ksl[i % 2]
        s_sl = pfx + "kl%d" % (i % 2)
        if i >= 2:
            p.wait("sync", pfx + "km", p.val(pfx + "km"))
        p.dma("sync", sl[:, :], io["k_d"][i], s_sl)
        p.wait("tensor", s_sl, p.val(s_sl))
        for j in range(DJ):
            p.do("tensor", lambda e, sl=sl, j=j, i=i: e.matmul(
                banks[0][:, j:j + 1], sl[:, j * 128:(j + 1) * 128],
                p.c["ones_col"][:, 0:1],
                start=(i == 0 and j == 0), stop=(i == NTT - 1 and j == DJ - 1)),
                inc=(pfx + "km", 1) if j == DJ - 1 else None)
    p.wait("vector", pfx + "km", p.val(pfx + "km"))
    p.do("vector", lambda e: e.tensor_copy(ksf[:, :], banks[0][:, 0:8]),
         inc=(pfx + "ksv", 1))
    p.wait("vector", pfx + "ksv", p.val(pfx + "ksv"))
    p.do("vector", lambda e: e.tensor_copy(ksum[:, :], ksf[:, :]),
         inc=(p.bk(0), 1))
    p.wait("tensor", p.bk(0), p.val(p.bk(0)))
    p.wait("tensor", pfx + "ksv", p.val(pfx + "ksv"))

    for eh in range(2):
        for i in range(NTT):
            ksl_i, vsl_i = ksl[i % 2], vsl[i % 2]
            s_kl = pfx + "cl%d" % (i % 2)
            if i >= 2 or eh == 1:
                p.wait("sync", pfx + "cm", p.val(pfx + "cm"))
            elif eh == 0 and i < 2:
                p.wait("sync", pfx + "km", p.val(pfx + "km"))
            p.dma("sync", ksl_i[:, :], io["k_d"][i], s_kl)
            p.dma("sync", vsl_i[:, :], io["v_d"][i, :, eh * 512:(eh + 1) * 512],
                  s_kl)
            p.wait("tensor", s_kl, p.val(s_kl))
            if i == 0:
                for j in range(DJ):
                    p.bank_acquire("tensor", j)
            for j in range(DJ):
                p.do("tensor", lambda e, ksl_i=ksl_i, vsl_i=vsl_i, j=j, i=i:
                     e.matmul(banks[j][:, :], ksl_i[:, j * 128:(j + 1) * 128],
                              vsl_i[:, :], start=(i == 0),
                              stop=(i == NTT - 1)),
                     inc=(pfx + "cm", 1) if j == DJ - 1 else None)
        p.wait("vector", pfx + "cm", p.val(pfx + "cm"))
        for j in range(DJ):
            p.do("vector", lambda e, j=j, eh=eh: e.tensor_copy(
                kv_sb[:, j * 1024 + eh * 512: j * 1024 + eh * 512 + 512],
                banks[j][:, :]), inc=(p.bk(j), 1))
    kv_done = [(p.bk(j), p.val(p.bk(j))) for j in range(DJ)]

    zm_guard = {}
    for nc_ in range(8):
        sl = qpc[nc_ % 2]
        s_sl = pfx + "ql%d" % (nc_ % 2)
        if nc_ >= 2:
            p.wait("sync", pfx + "zm", zm_guard[nc_ - 2])
        for j in range(DJ):
            p.dma("sync", sl[:, j * 512:(j + 1) * 512],
                  io["q_d"][j, :, nc_ * 512:(nc_ + 1) * 512], s_sl)
        p.wait("tensor", s_sl, p.val(s_sl))
        bz = 2 + nc_ % 2
        p.bank_acquire("tensor", bz)
        for j in range(DJ):
            p.do("tensor", lambda e, sl=sl, j=j, bz=bz: e.matmul(
                banks[bz][0:1, :], ksum[:, j:j + 1], sl[:, j * 512:(j + 1) * 512],
                start=(j == 0), stop=(j == DJ - 1)),
                inc=(pfx + "zm", 1) if j == DJ - 1 else None)
        zm_guard[nc_] = p.val(pfx + "zm")
        p.wait("vector", pfx + "zm", p.val(pfx + "zm"))
        p.do("vector", lambda e, nc_=nc_, bz=bz: e.tensor_scalar(
            rz[0:1, nc_ * 512:(nc_ + 1) * 512], banks[bz][0:1, :], float(EPS),
            None, ALU.add, ALU.bypass), inc=(p.bk(bz), 1))
    p.wait("vector", p.bk(2), p.val(p.bk(2)))
    p.wait("vector", p.bk(3), p.val(p.bk(3)))
    p.do("vector", lambda e: e.reciprocal(rz[0:1, :], rz[0:1, :]),
         inc=(pfx + "rz", 1))
    p.wait("tensor", pfx + "rz", p.val(pfx + "rz"))
    for nc_ in range(8):
        bz = 2 + nc_ % 2
        p.bank_acquire("tensor", bz)
        p.do("tensor", lambda e, nc_=nc_, bz=bz: e.matmul(
            banks[bz][:, :], p.c["ones_f"][0:1, 0:128],
            rz[0:1, nc_ * 512:(nc_ + 1) * 512], start=True, stop=True),
            inc=(pfx + "zb", 1))
        p.wait("scalar", pfx + "zb", p.val(pfx + "zb"))
        p.do("scalar", lambda e, nc_=nc_, bz=bz: e.copy(
            zb[:, nc_ * 512:(nc_ + 1) * 512], banks[bz][:, :]),
            inc=(p.bk(bz), 1))
    p.wait("vector", p.bk(2), p.val(p.bk(2)))
    p.wait("vector", p.bk(3), p.val(p.bk(3)))

    for sem, vv in kv_done:
        p.wait("tensor", sem, vv)
    am_guard = {}
    for nc_ in range(8):
        sl = qpc[nc_ % 2]
        s_sl = pfx + "fl%d" % (nc_ % 2)
        if nc_ >= 2:
            p.wait("sync", pfx + "am", am_guard[nc_ - 2])
        elif nc_ == 0:
            p.wait("sync", pfx + "zm", p.val(pfx + "zm"))
        for j in range(DJ):
            p.dma("sync", sl[:, j * 512:(j + 1) * 512],
                  io["q_d"][j, :, nc_ * 512:(nc_ + 1) * 512], s_sl)
        p.wait("tensor", s_sl, p.val(s_sl))
        for oc in range(8):
            bb = (2 * nc_ + oc) % 2 + 4
            p.bank_acquire("tensor", bb)
            for j in range(DJ):
                p.do("tensor", lambda e, sl=sl, j=j, oc=oc, bb=bb: e.matmul(
                    banks[bb][:, :],
                    kv_sb[:, j * 1024 + oc * 128: j * 1024 + oc * 128 + 128],
                    sl[:, j * 512:(j + 1) * 512], start=(j == 0),
                    stop=(j == DJ - 1)),
                    inc=(pfx + "am", 1) if j == DJ - 1 else None)
            p.wait("vector", pfx + "am", p.val(pfx + "am"))
            p.do("vector", lambda e, oc=oc, nc_=nc_, bb=bb: e.tensor_tensor(
                at_t[:, oc * NT + nc_ * 512: oc * NT + (nc_ + 1) * 512],
                banks[bb][:, :], zb[:, nc_ * 512:(nc_ + 1) * 512], ALU.mult),
                inc=(p.bk(bb), 1))
        am_guard[nc_] = p.val(pfx + "am")
    at_done = [(p.bk(4), p.val(p.bk(4))), (p.bk(5), p.val(p.bk(5)))]

    s_wo = pfx + "wo"
    p.wait("gpsimd", pfx + "am", p.val(pfx + "am"))
    for j in range(DJ):
        p.dma("gpsimd", wslot[:, j * 2048:j * 2048 + 1024], io["wout_t"][j],
              s_wo)
    p.dma("gpsimd", brow[:, :], io["bout_row"][:, :], s_wo)
    p.wait("tensor", s_wo, p.val(s_wo))
    for sem, vv in at_done:
        p.wait("tensor", sem, vv)
    ro_guard = {}

    def o_out(i, pbs, pe_ready):
        rti, roi = rst[i % 2], rso[i % 2]
        s_rl = pfx + "rld%d" % (i % 2)
        s_ro = pfx + "rout%d" % (i % 2)
        if i >= 2:
            for sem, vv in ro_guard[i - 2]:
                p.wait("sync", sem, vv)
        p.dma("sync", rti[:, :], io["res_in"][i * 128:(i + 1) * 128, :], s_rl)
        p.wait("vector", pe_ready[0], pe_ready[1])
        p.wait("vector", s_rl, p.val(s_rl))
        if i >= 2:
            p.wait("vector", s_ro, p.val(s_ro))
        specs = []
        for pb, w0, w1, rel in pbs:
            p.do("vector", lambda e, pb=pb, rti=rti, roi=roi, w0=w0, w1=w1:
                 e.tensor_tensor(roi[:, w0:w1], pb[:, 0:w1 - w0],
                                 rti[:, w0:w1], ALU.add), inc=rel)
            specs.append((rel[0], p.val(rel[0])))
        ro_guard[i] = specs
        for sem, vv in specs:
            p.wait("sync", sem, vv)
        p.dma("sync", io["res_out"][i * 128:(i + 1) * 128, :], roi[:, :], s_ro)

    emit_proj_tm(p, at_t, wslot, o_out, banks, (0, 1, 2, 3), pfx + "o",
                 out_w=1024, kdj=DJ, erow=brow, w_stride=2048)
    p.wait("sync", pfx + "rout0", p.val(pfx + "rout0"))
    p.wait("sync", pfx + "rout1", p.val(pfx + "rout1"))
    ar.reset(m0)


def prep_l1_attn(ins):
    W = np.asarray(ins["L1_attn_W_qkv"], np.float32)
    Wq = np.ascontiguousarray(W[0:1024].T)
    Wk = np.ascontiguousarray(W[1024:2048].T)
    Wv = np.ascontiguousarray(W[2048:3072].T)
    wkv = np.concatenate([Wk, Wv], axis=1)
    Wout = np.ascontiguousarray(np.asarray(ins["L1_attn_W_out"], np.float32).T)
    return dict(
        wkv_t=np.ascontiguousarray(wkv.reshape(DJ, 128, 2048)),
        wq_t=np.ascontiguousarray(Wq.reshape(DJ, 128, 1024)),
        wout_t=np.ascontiguousarray(Wout.reshape(DJ, 128, 1024)),
        bout_row=np.asarray(ins["L1_attn_b_out"], np.float32).reshape(1, 1024),
    )


L1ATTN_IO_SPECS = [("wkv_t", [DJ, 128, 2048]), ("wq_t", [DJ, 128, 1024]),
                   ("wout_t", [DJ, 128, 1024]), ("bout_row", [1, 1024])]
L1ATTN_SCRATCH = [("k_d", [NTT, 128, 1024]),
                  ("v_d", [NTT, 128, 1024]), ("q_d", [DJ, 128, NT])]


def emit_l0_ffn(p, ar, banks, io, pfx, mode="full"):
    m0 = ar.mark()
    wg = ar.alloc(DJ * 8, BF16)
    OH = ar.alloc(NTT * 8, BF16)
    gate = ar.alloc(NTT, F32)
    rstt = [ar.alloc(16, F32) for _ in range(2)]
    s_wg = pfx + "wg"
    for j in range(DJ):
        p.dma("gpsimd", wg[:, j * 8:(j + 1) * 8], io["wg_t"][j], s_wg)
    wg_rdy = p.val(s_wg)

    m1 = ar.mark()
    xn0 = ar.alloc(DJ * NT, BF16)
    mln = ar.mark()
    xn_specs = emit_ln_to_fm(
        p, ar, banks, lambda i: io["res_in"][i * 128:(i + 1) * 128, :],
        lambda i, j: xn0[:, j * NT + i * 128: j * NT + i * 128 + 128],
        pfx=pfx + "ln")
    ar.reset(mln)
    for sem, vv in xn_specs:
        p.wait("tensor", sem, vv)
    p.wait("tensor", s_wg, wg_rdy)

    def route_out(i, pbs, pe_ready):
        (pb, w0, w1, rel) = pbs[0]
        st = rstt[i % 2]
        s_rt = pfx + "rt"
        p.wait("vector", pe_ready[0], pe_ready[1])
        if i >= 2:
            p.wait("vector", s_rt, p.val(s_rt))
        p.do("vector", lambda e, pb=pb, st=st: e.tensor_reduce(
            st[:, 0:1], pb[:, 0:8], AX.X, ALU.max), inc=(s_rt, 1))
        p.wait("vector", s_rt, p.val(s_rt))
        if mode == "full":
            p.do("vector", lambda e, pb=pb, st=st, i=i: e.tensor_scalar(
                OH[:, i * 8:(i + 1) * 8], pb[:, 0:8], st[:, 0:1], None,
                ALU.is_equal, ALU.bypass), inc=(s_rt, 1))
            p.do("vector", lambda e, st=st: e.tensor_scalar(
                st[:, 1:2], st[:, 0:1], -1.0, None, ALU.mult, ALU.bypass),
                inc=(s_rt, 1))
            p.wait("scalar", s_rt, p.val(s_rt))
            p.do("scalar", lambda e, pb=pb, st=st: e.activation(
                st[:, 8:16], pb[:, 0:8], AF.Exp, bias=st[:, 1:2], scale=1.0),
                inc=rel)
            p.wait("vector", rel[0], p.val(rel[0]))
            p.do("vector", lambda e, st=st: e.tensor_reduce(
                st[:, 2:3], st[:, 8:16], AX.X, ALU.add), inc=(s_rt, 1))
            p.wait("vector", s_rt, p.val(s_rt))
            p.do("vector", lambda e, st=st, i=i: e.reciprocal(
                gate[:, i:i + 1], st[:, 2:3]), inc=(s_rt, 1))
        else:
            p.do("vector", lambda e, pb=pb, st=st, i=i: e.tensor_scalar(
                OH[:, i * 8:(i + 1) * 8], pb[:, 0:8], st[:, 0:1], None,
                ALU.is_equal, ALU.bypass), inc=rel)
            p.wait("vector", rel[0], p.val(rel[0]))
            p.do("vector", lambda e: e.nop(), inc=(s_rt, 1))

    emit_proj_tm(p, xn0, wg, route_out, banks, (0, 1, 2, 3), pfx + "lg",
                 out_w=8, kdj=DJ)

    if mode == "counts":
        p.bank_acquire("tensor", 4)
        p.wait("tensor", pfx + "rt", p.val(pfx + "rt"))
        for i in range(NTT):
            p.do("tensor", lambda e, i=i: e.matmul(
                banks[4][0:8, 0:1], OH[:, i * 8:(i + 1) * 8],
                p.c["ones_col"][:, 0:1], start=(i == 0), stop=(i == NTT - 1)),
                inc=(pfx + "cm", 1) if i == NTT - 1 else None)
        p.wait("vector", pfx + "cm", p.val(pfx + "cm"))
        cnt = ar.alloc(1, F32, parts=8)
        p.do("vector", lambda e, cnt=cnt: e.tensor_copy(cnt[:, :],
                                                        banks[4][0:8, 0:1]),
             inc=(p.bk(4), 1))
        p.wait("sync", p.bk(4), p.val(p.bk(4)))
        p.dma("sync", io["counts_out"][:, :], cnt[:, :], pfx + "co")
        p.wait("sync", pfx + "co", p.val(pfx + "co"))
        ar.reset(m0)
        return

    ca = ar.alloc(8, F32, parts=8)
    sel = ar.alloc(1, F32, parts=8)
    offs = ar.alloc(1, F32, parts=8)
    oh_t = ar.alloc(NT, BF16, parts=8)
    loc = ar.alloc(NT, F32, parts=8)
    wfin = ar.alloc(NT, BF16, parts=8)
    wcols = ar.alloc(NTT * 8, F32)
    s_ca = pfx + "ca"
    for sem, vv in xn_specs:
        p.wait("gpsimd", sem, vv)
    p.dma("gpsimd", ca[:, :], io["counts_all"][:, :], s_ca)
    p.dma("gpsimd", sel[:, :], io["sel"][:, :], s_ca)
    ca_rdy = p.val(s_ca)
    p.wait("tensor", s_ca, ca_rdy)
    p.bank_acquire("tensor", 4)
    p.do("tensor", lambda e: e.matmul(banks[4][0:8, 0:1], ca[:, :], sel[:, :],
                                      start=True, stop=True),
         inc=(pfx + "om", 1))
    p.wait("vector", pfx + "om", p.val(pfx + "om"))
    p.do("vector", lambda e: e.tensor_copy(offs[:, :], banks[4][0:8, 0:1]),
         inc=(p.bk(4), 1))

    p.wait("tensor", pfx + "rt", p.val(pfx + "rt"))
    for i in range(NTT):
        bid = 4 + i % 2
        pb = banks[bid][:, 0:64].bitcast(BF16)[0:8, 0:128]
        p.bank_acquire("tensor", bid)
        p.do("tensor", lambda e, pb=pb, i=i: e.transpose(
            pb, OH[:, i * 8:(i + 1) * 8], p.c["ident"][:, :]),
            inc=(pfx + "ot", 1))
        p.wait("vector", pfx + "ot", p.val(pfx + "ot"))
        p.do("vector", lambda e, pb=pb, i=i: e.tensor_copy(
            oh_t[:, i * 128:(i + 1) * 128], pb), inc=(p.bk(bid), 1))
    p.wait("vector", p.bk(4), p.val(p.bk(4)))
    p.wait("vector", p.bk(5), p.val(p.bk(5)))
    p.do("vector", lambda e: e.tensor_tensor_scan(
        loc[:, :], oh_t[:, :], oh_t[:, :], 0.0, ALU.add, ALU.bypass),
        inc=(pfx + "sc", 1))
    p.wait("vector", pfx + "sc", p.val(pfx + "sc"))
    p.do("vector", lambda e: e.tensor_tensor(loc[:, :], loc[:, :], oh_t[:, :],
                                             ALU.subtract), inc=(pfx + "sc", 1))
    p.wait("vector", pfx + "sc", p.val(pfx + "sc"))
    p.do("vector", lambda e: e.tensor_scalar(
        loc[:, :], loc[:, :], offs[:, 0:1], float(NT), ALU.add, ALU.is_lt),
        inc=(pfx + "sc", 1))
    p.wait("vector", pfx + "sc", p.val(pfx + "sc"))
    p.do("vector", lambda e: e.tensor_tensor(wfin[:, :], loc[:, :], oh_t[:, :],
                                             ALU.mult), inc=(pfx + "sc", 1))
    p.wait("tensor", pfx + "sc", p.val(pfx + "sc"))
    for i in range(NTT):
        bid = 4 + i % 2
        pbb = banks[bid][:, 0:64].bitcast(BF16)[0:128, 0:8]
        p.bank_acquire("tensor", bid)
        p.do("tensor", lambda e, pbb=pbb, i=i: e.transpose(
            pbb, wfin[:, i * 128:(i + 1) * 128], p.c["ident"][0:8, 0:8]),
            inc=(pfx + "wt", 1))
        p.wait("vector", pfx + "wt", p.val(pfx + "wt"))
        p.do("vector", lambda e, pbb=pbb, i=i: e.tensor_scalar(
            wcols[:, i * 8:(i + 1) * 8], pbb, gate[:, i:i + 1], None,
            ALU.mult, ALU.bypass), inc=(p.bk(bid), 1))
    wc_done = p.val(p.bk(4)), p.val(p.bk(5))
    p.wait("vector", p.bk(4), wc_done[0])
    p.wait("vector", p.bk(5), wc_done[1])

    NQ = 8
    TPQ = NTT // NQ
    acc = ar.alloc(TPQ * 1024, F32)
    wex = [ar.alloc(DJ * 1024, BF16) for _ in range(2)]
    ber = [ar.alloc(1024, BF16, parts=1) for _ in range(2)]
    s_ac = pfx + "ac"
    for q in range(NQ):
        if q >= 1:
            p.wait("sync", pfx + "ro", p.val(pfx + "ro"))
        p.dma("sync", acc[:, :].rearrange("p (t c) -> p t c", c=1024),
              io["res_in"][:, :].rearrange("(qq t pp) c -> qq pp t c", pp=128,
                                           t=TPQ)[q], s_ac)
        p.wait("vector", s_ac, p.val(s_ac))
        for e_ in range(8):
            ge = q * 8 + e_
            sl, bl_ = wex[ge % 2], ber[ge % 2]
            s_we = pfx + "we%d" % (ge % 2)
            if ge >= 2:
                p.wait("gpsimd", pfx + "em", p.val(pfx + "em"))
            for j in range(DJ):
                p.dma("gpsimd", sl[:, j * 1024:(j + 1) * 1024],
                      io["w_ex%d" % e_][j], s_we)
            p.dma("gpsimd", bl_[:, :], io["b_ex%d" % e_][:, :], s_we)
            p.wait("tensor", s_we, p.val(s_we))
            for t in range(TPQ):
                gt = q * TPQ + t
                for k in range(2):
                    bid = 2 * (gt % 2) + k
                    pb = banks[bid]
                    p.bank_acquire("tensor", bid)
                    for j in range(DJ):
                        p.do("tensor", lambda e, pb=pb, sl=sl, j=j, gt=gt, k=k:
                             e.matmul(
                                 pb[:, :],
                                 xn0[:, j * NT + gt * 128: j * NT + gt * 128 + 128],
                                 sl[:, j * 1024 + k * 512: j * 1024 + (k + 1) * 512],
                                 start=(j == 0), stop=False))
                    p.do("tensor", lambda e, pb=pb, bl_=bl_, k=k: e.matmul(
                        pb[:, :], p.c["ones_bf"][0:1, 0:128],
                        bl_[0:1, k * 512:(k + 1) * 512], start=False, stop=True),
                        inc=(pfx + "em", 1))
                    p.wait("vector", pfx + "em", p.val(pfx + "em"))
                    p.do("vector", lambda e, pb=pb, t=t, k=k, gt=gt, e_=e_:
                         e.scalar_tensor_tensor(
                             acc[:, t * 1024 + k * 512: t * 1024 + (k + 1) * 512],
                             pb[:, :], wcols[:, gt * 8 + e_:gt * 8 + e_ + 1],
                             acc[:, t * 1024 + k * 512: t * 1024 + (k + 1) * 512],
                             ALU.mult, ALU.add), inc=(p.bk(bid), 1))
        p.wait("sync", p.bk(0), p.val(p.bk(0)))
        p.wait("sync", p.bk(1), p.val(p.bk(1)))
        p.wait("sync", p.bk(2), p.val(p.bk(2)))
        p.wait("sync", p.bk(3), p.val(p.bk(3)))
        p.dma("sync",
              io["res_out"][:, :].rearrange("(qq t pp) c -> qq pp t c",
                                            pp=128, t=TPQ)[q],
              acc[:, :].rearrange("p (t c) -> p t c", c=1024), pfx + "ro")
    p.wait("sync", pfx + "ro", p.val(pfx + "ro"))
    ar.reset(m0)


def prep_l0_ffn(ins, core_id):
    wg = np.ascontiguousarray(np.asarray(ins["L0_ffn_wg"], np.float32))
    W = np.asarray(ins["L0_ffn_W"], np.float32)
    b = np.asarray(ins["L0_ffn_b"], np.float32)
    sel = (np.arange(8) < core_id).astype(np.float32).reshape(8, 1)
    d = dict(
        wg_t=wg.reshape(DJ, 128, 8),
        sel=sel,
    )
    for e_ in range(8):
        d["w_ex%d" % e_] = np.ascontiguousarray(W[e_].T.reshape(DJ, 128, 1024))
        d["b_ex%d" % e_] = b[e_].reshape(1, 1024).copy()
    return d


L0FFN_IO_SPECS = ([("wg_t", [DJ, 128, 8]), ("sel", [8, 1])] +
                  [("w_ex%d" % e_, [DJ, 128, 1024]) for e_ in range(8)] +
                  [("b_ex%d" % e_, [1, 1024]) for e_ in range(8)])


def emit_l1_ffn_pre(p, ar, banks, io, pfx):
    m0 = ar.mark()
    hst = [ar.alloc(D, F32) for _ in range(3)]

    def hs_cb(i, xt, st, st_ready):
        h = hst[i % 3]
        s_hs = pfx + "hs%d" % (i % 3)
        p.wait("vector", st_ready[0], st_ready[1])
        if i >= 3:
            p.wait("vector", s_hs, p.val(s_hs))
        p.do("vector", lambda e, xt=xt, st=st, h=h: e.tensor_scalar(
            h[:, :], xt[:, :], st[:, 0:1], st[:, 2:3], ALU.subtract, ALU.mult),
            inc=(pfx + "hv", 1))
        p.wait("sync", pfx + "hv", p.val(pfx + "hv"))
        p.dma("sync", io["hs_d"][i * 128:(i + 1) * 128, :], h[:, :], s_hs)
        return [(pfx + "hv", p.val(pfx + "hv"))]

    mln = ar.mark()
    emit_ln_tiles(p, ar, lambda i: io["res_in"][i * 128:(i + 1) * 128, :],
                  NTT, hs_cb, pfx=pfx + "l0")
    ar.reset(mln)
    for s in range(3):
        p.wait("sync", pfx + "hs%d" % s, p.val(pfx + "hs%d" % s))

    w1c = ar.alloc(DJ, F32)
    b1c = ar.alloc(DJ, F32)
    p.wait("gpsimd", pfx + "hv", p.val(pfx + "hv"))
    p.dma("gpsimd", w1c[:, :], io["ln1w"][:, :], pfx + "lw")
    p.dma("gpsimd", b1c[:, :], io["ln1b"][:, :], pfx + "lw")
    p.wait("vector", pfx + "lw", p.val(pfx + "lw"))
    ln1 = ar.alloc(DJ * NT, BF16)
    mln2 = ar.mark()
    specs = emit_ln_to_fm(
        p, ar, banks, lambda i: io["hs_d"][i * 128:(i + 1) * 128, :],
        lambda i, j: ln1[:, j * NT + i * 128: j * NT + i * 128 + 128],
        scale_bias=(w1c, b1c), pfx=pfx + "l1")
    ar.reset(mln2)
    wslot = ar.alloc(DJ * 1024, BF16)
    stg = [ar.alloc(1024, BF16) for _ in range(2)]
    for sem, vv in specs:
        p.wait("tensor", sem, vv)

    for which, wkey, dkey in (("q", "f1q_t", "q_d"), ("k", "wkrep_t", "k_d"),
                              ("v", "wvrep_t", "v_d")):
        s_w = pfx + "w" + which
        if which != "q":
            prev = {"k": "q", "v": "k"}[which]
            p.wait("gpsimd", pfx + prev + "_pe", p.val(pfx + prev + "_pe"))
            for s in range(2):
                nm = pfx + prev + "s%d" % s
                p.wait("vector", nm, p.val(nm))
        for sem, vv in specs:
            p.wait("gpsimd", sem, vv)
        for j in range(DJ):
            p.dma("gpsimd", wslot[:, j * 1024:(j + 1) * 1024], io[wkey][j], s_w)
        p.wait("tensor", s_w, p.val(s_w))

        def out_cb(i, pbs, pe_ready, which=which, dkey=dkey):
            sl = stg[i % 2]
            s_st = pfx + which + "s%d" % (i % 2)
            if i >= 2:
                p.wait("vector", s_st, p.val(s_st))
            p.wait("vector", pe_ready[0], pe_ready[1])
            vrel = []
            for pb, w0, w1, rel in pbs:
                p.do("vector", lambda e, pb=pb, sl=sl, w0=w0, w1=w1:
                     e.tensor_copy(sl[:, w0:w1], pb[:, 0:w1 - w0]), inc=rel)
                vrel.append((rel[0], p.val(rel[0])))
            for sem, vv in vrel:
                p.wait("sync", sem, vv)
            p.dma("sync", io[dkey][i], sl[:, :], s_st)

        emit_proj_tm(p, ln1, wslot, out_cb, banks, (0, 1, 2, 3), pfx + which,
                     out_w=1024, kdj=DJ)
    done = []
    for which in ("q", "k", "v"):
        for s in range(2):
            nm = pfx + which + "s%d" % s
            p.wait("sync", nm, p.val(nm))
            done.append((nm, p.val(nm)))
        done.append((pfx + which + "_pe", p.val(pfx + which + "_pe")))
    ar.reset(m0)
    return done


def emit_l1_ffn_post(p, ar, banks, io, pfx, prev_done=(), dbg_tiles=None):
    m0 = ar.mark()
    for sem, vv in prev_done:
        p.wait("gpsimd", sem, vv)
    amask = ar.alloc(128, F32)
    p.dma("gpsimd", amask[:, :], io["amask"][:, :], pfx + "am")
    p.wait("vector", pfx + "am", p.val(pfx + "am"))

    m1 = ar.mark()
    at_t = ar.alloc(DJ * NT, BF16)
    qsl = [ar.alloc(1024, BF16) for _ in range(2)]
    ksl = [ar.alloc(1024, BF16) for _ in range(2)]
    vsl = [ar.alloc(1024, BF16) for _ in range(2)]
    tmp = ar.alloc(1024, BF16)
    sc = ar.alloc(128, F32)
    pr = ar.alloc(128, F32)
    mx = ar.alloc(64, F32)
    accv = ar.alloc(1024, F32)
    attn = ar.alloc(1024, BF16)
    s_sc = pfx + "sc"
    NTILES = NTT if dbg_tiles is None else dbg_tiles
    for buf in (tmp, sc, pr, mx, accv, attn):
        p.do("gpsimd", lambda e, buf=buf: e.memset(buf, 0.0), inc=(s_sc, 1))
    p.wait("vector", s_sc, p.val(s_sc))
    p.wait("scalar", s_sc, p.val(s_sc))
    p.wait("tensor", s_sc, p.val(s_sc))
    for i in range(NTILES):
        qs = qsl[i % 2]
        s_q = pfx + "ql%d" % (i % 2)
        if i >= 2:
            p.wait("sync", s_sc, p.val(s_sc))
        p.dma("sync", qs[:, :], io["q_d"][i], s_q)
        p.wait("vector", s_q, p.val(s_q))
        for t in range(8):
            ks = ksl[t % 2]
            s_k = pfx + "kl%d" % (t % 2)
            if t >= 2 or i >= 1:
                p.wait("sync", s_sc, p.val(s_sc))
            p.dma("sync", ks[:, :], io["kall"][t, i], s_k)
            p.wait("vector", s_k, p.val(s_k))
            p.wait("vector", s_sc, p.val(s_sc))
            p.do("vector", lambda e, qs=qs, ks=ks: e.tensor_tensor(
                tmp[:, :], qs[:, :], ks[:, :], ALU.mult), inc=(s_sc, 1))
            p.wait("vector", s_sc, p.val(s_sc))
            p.do("vector", lambda e, t=t: e.tensor_reduce(
                sc[:, t * 16:(t + 1) * 16],
                tmp[:, :].rearrange("p (h d) -> p h d", d=64), AX.X, ALU.add),
                inc=(s_sc, 1))
        p.wait("vector", s_sc, p.val(s_sc))
        p.do("vector", lambda e: e.tensor_tensor(sc[:, :], sc[:, :],
                                                 amask[:, :], ALU.add),
             inc=(s_sc, 1))
        p.wait("vector", s_sc, p.val(s_sc))
        scv = sc[:, :].rearrange("p (t h) -> p h t", h=16)
        p.do("vector", lambda e, scv=scv: e.tensor_reduce(
            mx[:, 0:16], scv, AX.X, ALU.max), inc=(s_sc, 1))
        p.wait("vector", s_sc, p.val(s_sc))
        mxb = mx[:, 0:16].rearrange("p h -> p () h").broadcast_to((128, 8, 16))
        p.do("vector", lambda e, mxb=mxb: e.tensor_tensor(
            sc[:, :].rearrange("p (t h) -> p t h", h=16),
            sc[:, :].rearrange("p (t h) -> p t h", h=16), mxb, ALU.subtract),
            inc=(s_sc, 1))
        p.wait("scalar", s_sc, p.val(s_sc))
        scc = p.constf(0.125)
        p.do("scalar", lambda e, scc=scc: e.activation(
            pr[:, :], sc[:, :], AF.Exp, scale=scc), inc=(s_sc, 1))
        p.wait("vector", s_sc, p.val(s_sc))
        prv = pr[:, :].rearrange("p (t h) -> p h t", h=16)
        p.do("vector", lambda e, prv=prv: e.tensor_reduce(
            mx[:, 16:32], prv, AX.X, ALU.add), inc=(s_sc, 1))
        p.wait("vector", s_sc, p.val(s_sc))
        p.do("vector", lambda e: e.reciprocal(mx[:, 32:48], mx[:, 16:32]),
             inc=(s_sc, 1))
        for t in range(8):
            vs = vsl[t % 2]
            s_v = pfx + "vl%d" % (t % 2)
            if t >= 2 or i >= 1:
                p.wait("sync", s_sc, p.val(s_sc))
            p.dma("sync", vs[:, :], io["vall"][t, i], s_v)
            p.wait("vector", s_v, p.val(s_v))
            p.wait("vector", s_sc, p.val(s_sc))
            prb = pr[:, t * 16:(t + 1) * 16].rearrange(
                "p h -> p h ()").broadcast_to((128, 16, 64))
            if t == 0:
                p.do("vector", lambda e, vs=vs, prb=prb: e.tensor_tensor(
                    accv[:, :].rearrange("p (h d) -> p h d", d=64),
                    vs[:, :].rearrange("p (h d) -> p h d", d=64), prb,
                    ALU.mult), inc=(s_sc, 1))
            else:
                p.do("vector", lambda e, vs=vs, prb=prb: e.tensor_tensor(
                    tmp[:, :].rearrange("p (h d) -> p h d", d=64),
                    vs[:, :].rearrange("p (h d) -> p h d", d=64), prb,
                    ALU.mult), inc=(s_sc, 1))
                p.wait("vector", s_sc, p.val(s_sc))
                p.do("vector", lambda e: e.tensor_tensor(
                    accv[:, :], accv[:, :], tmp[:, :].bitcast(F32) if False
                    else tmp[:, :], ALU.add), inc=(s_sc, 1))
            p.wait("vector", s_sc, p.val(s_sc))
        rzb = mx[:, 32:48].rearrange("p h -> p h ()").broadcast_to((128, 16, 64))
        p.do("vector", lambda e, rzb=rzb: e.tensor_tensor(
            attn[:, :].rearrange("p (h d) -> p h d", d=64),
            accv[:, :].rearrange("p (h d) -> p h d", d=64), rzb, ALU.mult),
            inc=(s_sc, 1))
        p.wait("tensor", s_sc, p.val(s_sc))
        for j in range(DJ):
            bid = 6 + (i * DJ + j) % 2
            pb = banks[bid][:, 0:64].bitcast(BF16)
            p.bank_acquire("tensor", bid)
            p.do("tensor", lambda e, pb=pb, j=j: e.transpose(
                pb, attn[:, j * 128:(j + 1) * 128], p.c["ident"][:, :]),
                inc=(pfx + "tr", 1))
            p.wait("vector", pfx + "tr", p.val(pfx + "tr"))
            p.do("vector", lambda e, pb=pb, i=i, j=j: e.tensor_copy(
                at_t[:, j * NT + i * 128: j * NT + i * 128 + 128], pb),
                inc=(p.bk(bid), 1))
        p.wait("sync", p.bk(6), p.val(p.bk(6)))
        p.wait("sync", p.bk(7), p.val(p.bk(7)))
        p.do("vector", lambda e: e.nop(), inc=(s_sc, 1))

    wslot = ar.alloc(DJ * 1024, BF16)
    hst = [ar.alloc(D, F32) for _ in range(2)]
    hao = [ar.alloc(D, F32) for _ in range(2)]
    s_wp = pfx + "wp"
    p.wait("gpsimd", pfx + "tr", p.val(pfx + "tr"))
    for j in range(DJ):
        p.dma("gpsimd", wslot[:, j * 1024:(j + 1) * 1024], io["wproj_t"][j],
              s_wp)
    p.wait("tensor", s_wp, p.val(s_wp))
    p.wait("tensor", p.bk(6), p.val(p.bk(6)))
    p.wait("tensor", p.bk(7), p.val(p.bk(7)))
    ha_guard = {}

    def proj_out(i, pbs, pe_ready):
        ht, ho = hst[i % 2], hao[i % 2]
        s_h = pfx + "hl%d" % (i % 2)
        s_o = pfx + "ho%d" % (i % 2)
        if i >= 2:
            for sem, vv in ha_guard[i - 2]:
                p.wait("sync", sem, vv)
        p.dma("sync", ht[:, :], io["hs_d"][i * 128:(i + 1) * 128, :], s_h)
        p.wait("vector", pe_ready[0], pe_ready[1])
        p.wait("vector", s_h, p.val(s_h))
        if i >= 2:
            p.wait("vector", s_o, p.val(s_o))
        specs = []
        for pb, w0, w1, rel in pbs:
            p.do("vector", lambda e, pb=pb, ht=ht, ho=ho, w0=w0, w1=w1:
                 e.tensor_tensor(ho[:, w0:w1], pb[:, 0:w1 - w0], ht[:, w0:w1],
                                 ALU.add), inc=rel)
            specs.append((rel[0], p.val(rel[0])))
        ha_guard[i] = specs
        for sem, vv in specs:
            p.wait("sync", sem, vv)
        p.dma("sync", io["ha_d"][i * 128:(i + 1) * 128, :], ho[:, :], s_o)

    emit_proj_tm(p, at_t, wslot, proj_out, banks, (0, 1, 2, 3), pfx + "pj",
                 out_w=1024, kdj=DJ)
    for s in range(2):
        p.wait("sync", pfx + "ho%d" % s, p.val(pfx + "ho%d" % s))

    ar.reset(m1)
    w2c = ar.alloc(DJ, F32)
    b2c = ar.alloc(DJ, F32)
    p.wait("gpsimd", pfx + "pj_pe", p.val(pfx + "pj_pe"))
    p.wait("sync", pfx + "pj_pe", p.val(pfx + "pj_pe"))
    for s in range(2):
        p.wait("sync", pfx + "ho%d" % s, p.val(pfx + "ho%d" % s))
        p.wait("gpsimd", pfx + "ho%d" % s, p.val(pfx + "ho%d" % s))
    p.dma("gpsimd", w2c[:, :], io["ln2w"][:, :], pfx + "l2w")
    p.dma("gpsimd", b2c[:, :], io["ln2b"][:, :], pfx + "l2w")
    p.wait("vector", pfx + "l2w", p.val(pfx + "l2w"))
    ln2 = ar.alloc(DJ * NT, BF16)
    mln = ar.mark()
    specs2 = emit_ln_to_fm(
        p, ar, banks, lambda i: io["ha_d"][i * 128:(i + 1) * 128, :],
        lambda i, j: ln2[:, j * NT + i * 128: j * NT + i * 128 + 128],
        scale_bias=(w2c, b2c), pfx=pfx + "l2")
    ar.reset(mln)
    for sem, vv in specs2:
        p.wait("tensor", sem, vv)

    wrt = ar.alloc(DJ * 8, BF16)
    wcols = ar.alloc(NTT * 8, F32)
    rstt = [ar.alloc(24, F32) for _ in range(2)]
    for sem, vv in specs2:
        p.wait("gpsimd", sem, vv)
    p.wait("gpsimd", pfx + "l2_nm", p.val(pfx + "l2_nm"))
    for j in range(DJ):
        p.dma("gpsimd", wrt[:, j * 8:(j + 1) * 8], io["wrt_t"][j], pfx + "rw")
    p.wait("tensor", pfx + "rw", p.val(pfx + "rw"))

    def rt_out(i, pbs, pe_ready):
        (pb, w0, w1, rel) = pbs[0]
        st = rstt[i % 2]
        s_rt = pfx + "rt"
        p.wait("vector", pe_ready[0], pe_ready[1])
        if i >= 2:
            p.wait("vector", s_rt, p.val(s_rt))
        p.do("vector", lambda e, pb=pb, st=st: e.tensor_reduce(
            st[:, 0:1], pb[:, 0:8], AX.X, ALU.max), inc=(s_rt, 1))
        p.wait("vector", s_rt, p.val(s_rt))
        p.do("vector", lambda e, st=st: e.tensor_scalar(
            st[:, 1:2], st[:, 0:1], -1.0, None, ALU.mult, ALU.bypass),
            inc=(s_rt, 1))
        p.wait("scalar", s_rt, p.val(s_rt))
        p.do("scalar", lambda e, pb=pb, st=st: e.activation(
            st[:, 8:16], pb[:, 0:8], AF.Exp, bias=st[:, 1:2], scale=1.0),
            inc=(s_rt, 1))
        p.wait("vector", s_rt, p.val(s_rt))
        p.do("vector", lambda e, st=st: e.tensor_reduce(
            st[:, 2:3], st[:, 8:16], AX.X, ALU.add), inc=(s_rt, 1))
        p.wait("vector", s_rt, p.val(s_rt))
        p.do("vector", lambda e, st=st: e.reciprocal(st[:, 3:4], st[:, 2:3]),
             inc=(s_rt, 1))
        p.wait("vector", s_rt, p.val(s_rt))
        p.do("vector", lambda e, st=st: e.tensor_scalar(
            st[:, 16:24], st[:, 8:16], st[:, 3:4], None, ALU.mult, ALU.bypass),
            inc=(s_rt, 1))
        p.wait("vector", s_rt, p.val(s_rt))
        p.do("vector", lambda e, st=st: e.tensor_reduce(
            st[:, 4:5], st[:, 16:24], AX.X, ALU.max), inc=(s_rt, 1))
        p.wait("vector", s_rt, p.val(s_rt))
        p.do("vector", lambda e, st=st: e.tensor_scalar(
            st[:, 8:16], st[:, 16:24], st[:, 4:5], None, ALU.is_equal,
            ALU.bypass), inc=(s_rt, 1))
        p.wait("vector", s_rt, p.val(s_rt))
        p.do("vector", lambda e, st=st: e.scalar_tensor_tensor(
            st[:, 8:16], st[:, 8:16], -1e30, st[:, 16:24], ALU.mult, ALU.add),
            inc=(s_rt, 1))
        p.wait("vector", s_rt, p.val(s_rt))
        p.do("vector", lambda e, st=st: e.tensor_reduce(
            st[:, 5:6], st[:, 8:16], AX.X, ALU.max), inc=(s_rt, 1))
        p.wait("vector", s_rt, p.val(s_rt))
        p.do("vector", lambda e, st=st: e.tensor_scalar(
            st[:, 8:16], st[:, 16:24], st[:, 5:6], None, ALU.is_ge,
            ALU.bypass), inc=(s_rt, 1))
        p.wait("vector", s_rt, p.val(s_rt))
        p.do("vector", lambda e, st=st, i=i: e.tensor_tensor(
            wcols[:, i * 8:(i + 1) * 8], st[:, 8:16], st[:, 16:24], ALU.mult),
            inc=rel)
        p.wait("vector", rel[0], p.val(rel[0]))

    emit_proj_tm(p, ln2, wrt, rt_out, banks, (4, 5, 6, 7), pfx + "rt",
                 out_w=8, kdj=DJ)

    NQ = 4
    TPQ = NTT // NQ
    acc = ar.alloc(TPQ * 1024, F32)
    hat_s = [ar.alloc(D, F32) for _ in range(2)]
    m1r = [ar.alloc(1024, BF16) for _ in range(3)]
    m2r = [ar.alloc(512, BF16) for _ in range(2)]
    h1t = ar.alloc(16 * 1024, BF16)
    p.wait("vector", pfx + "rt_pe", p.val(pfx + "rt_pe"))
    p.wait("sync", pfx + "l2_tr", p.val(pfx + "l2_tr"))
    p.wait("sync", pfx + "l2_nm", p.val(pfx + "l2_nm"))
    p.wait("gpsimd", pfx + "l2_tr", p.val(pfx + "l2_tr"))
    s_mm = pfx + "mm"
    m1g = [0, 0, 0]
    m2g = [0, 0]
    m1n = [0]
    m2n = [0]

    def load_m1(e_, u):
        sl = m1n[0] % 3
        s_w = pfx + "ma%d" % sl
        p.wait("gpsimd", s_mm, m1g[sl])
        p.dma("gpsimd", m1r[sl][:, :], io["moe1_%d" % e_][u], s_w)
        m1n[0] += 1
        return sl, (s_w, p.val(s_w))

    def load_m2(e_, k, u):
        sl = m2n[0] % 2
        s_w = pfx + "mc%d" % sl
        p.wait("gpsimd", s_mm, m2g[sl])
        p.dma("gpsimd", m2r[sl][:, :], io["moe2_%d" % e_][k, u], s_w)
        m2n[0] += 1
        return sl, (s_w, p.val(s_w))
    for q in range(NQ):
        if q >= 1:
            p.wait("sync", pfx + "ro", p.val(pfx + "ro"))
        p.dma("sync", acc[:, :].rearrange("p (t c) -> p t c", c=1024),
              io["res_in"][:, :].rearrange("(qq t pp) c -> qq pp t c", pp=128,
                                           t=TPQ)[q], pfx + "al")
        p.wait("vector", pfx + "al", p.val(pfx + "al"))
        for t in range(TPQ):
            hs_ = hat_s[t % 2]
            s_h = pfx + "hq%d" % (t % 2)
            if t >= 2:
                p.wait("sync", pfx + "ai", p.val(pfx + "ai"))
            p.dma("sync", hs_[:, :],
                  io["ha_d"][(q * TPQ + t) * 128:(q * TPQ + t + 1) * 128, :],
                  s_h)
            p.wait("vector", s_h, p.val(s_h))
            p.do("vector", lambda e, hs_=hs_, t=t: e.tensor_tensor(
                acc[:, t * 1024:(t + 1) * 1024], acc[:, t * 1024:(t + 1) * 1024],
                hs_[:, :], ALU.add), inc=(pfx + "ai", 1))
        p.wait("vector", pfx + "ai", p.val(pfx + "ai"))
        c0 = q * TPQ * 128
        for e_ in range(8):
            pend1 = [load_m1(e_, 0), load_m1(e_, 1)]
            pend2 = []
            for u in range(16):
                sl, rdy = pend1.pop(0)
                if u + 2 < 16:
                    pend1.append(load_m1(e_, u + 2))
                elif u == 15:
                    pend2.append(load_m2(e_, 0, 0))
                p.wait("tensor", rdy[0], rdy[1])
                for h in range(2):
                    bid = (2 * u + h) % 4
                    pb = banks[bid]
                    p.bank_acquire("tensor", bid)
                    for j in range(DJ):
                        p.do("tensor", lambda e, pb=pb, sl=sl, j=j, c0=c0, h=h:
                             e.matmul(
                                 pb[:, :], m1r[sl][:, j * 128:(j + 1) * 128],
                                 ln2[:, j * NT + c0 + h * 512:
                                     j * NT + c0 + h * 512 + 512],
                                 start=(j == 0), stop=(j == DJ - 1)),
                             inc=(s_mm, 1) if j == DJ - 1 else None)
                    p.wait("scalar", s_mm, p.val(s_mm))
                    p.do("scalar", lambda e, pb=pb, u=u, h=h: e.activation(
                        h1t[:, u * 1024 + h * 512: u * 1024 + h * 512 + 512],
                        pb[:, :], AF.Gelu_apprx_tanh), inc=(p.bk(bid), 1))
                m1g[sl] = p.val(s_mm)
            for k in range(2):
                for tt in range(8):
                    p.bank_acquire("tensor", tt)
                for u in range(16):
                    sl, rdy = pend2.pop(0)
                    if u + 1 < 16:
                        pend2.append(load_m2(e_, k, u + 1))
                    elif k == 0:
                        pend2.append(load_m2(e_, 1, 0))
                    p.wait("tensor", rdy[0], rdy[1])
                    for tt in range(8):
                        pb = banks[tt]
                        p.do("tensor", lambda e, pb=pb, sl=sl, u=u, tt=tt:
                             e.matmul(
                                 pb[:, :],
                                 h1t[:, u * 1024 + tt * 128:
                                     u * 1024 + tt * 128 + 128],
                                 m2r[sl][:, :], start=(u == 0), stop=(u == 15)),
                             inc=(s_mm, 1) if tt == 7 else None)
                    m2g[sl] = p.val(s_mm)
                for tt in range(8):
                    gt = q * TPQ + tt
                    pb = banks[tt]
                    p.wait("vector", s_mm, p.val(s_mm))
                    p.do("vector", lambda e, pb=pb, tt=tt, k=k, gt=gt, e_=e_:
                         e.scalar_tensor_tensor(
                             acc[:, tt * 1024 + k * 512: tt * 1024 + k * 512 + 512],
                             pb[:, :], wcols[:, gt * 8 + e_:gt * 8 + e_ + 1],
                             acc[:, tt * 1024 + k * 512: tt * 1024 + k * 512 + 512],
                             ALU.mult, ALU.add), inc=(p.bk(tt), 1))
        for b in range(8):
            p.wait("sync", p.bk(b), p.val(p.bk(b)))
        p.dma("sync",
              io["res_out"][:, :].rearrange("(qq t pp) c -> qq pp t c",
                                            pp=128, t=TPQ)[q],
              acc[:, :].rearrange("p (t c) -> p t c", c=1024), pfx + "ro")
    p.wait("sync", pfx + "ro", p.val(pfx + "ro"))
    ar.reset(m0)


def prep_l1_ffn_pre(ins):
    qkv = np.asarray(ins["L1_ffn_qkv_weight"], np.float32)
    Wq = np.ascontiguousarray(qkv[0:1024].T)
    Wk = qkv[1024:1280].T
    Wv = qkv[1280:1536].T
    Wk_rep = np.ascontiguousarray(
        np.repeat(Wk.reshape(1024, 4, 64), 4, axis=1).reshape(1024, 1024))
    Wv_rep = np.ascontiguousarray(
        np.repeat(Wv.reshape(1024, 4, 64), 4, axis=1).reshape(1024, 1024))
    return dict(
        f1q_t=Wq.reshape(DJ, 128, 1024),
        wkrep_t=Wk_rep.reshape(DJ, 128, 1024),
        wvrep_t=Wv_rep.reshape(DJ, 128, 1024),
        ln1w=np.asarray(ins["L1_ffn_ln1_weight"], np.float32).reshape(DJ, 128).T.copy(),
        ln1b=np.asarray(ins["L1_ffn_ln1_bias"], np.float32).reshape(DJ, 128).T.copy(),
    )


def prep_l1_ffn_post(ins, core_id):
    Wproj = np.ascontiguousarray(np.asarray(ins["L1_ffn_proj_weight"],
                                            np.float32).T)
    wrt = np.ascontiguousarray(np.asarray(ins["L1_ffn_router_weight"],
                                          np.float32).T)
    amask = np.zeros((128, 128), np.float32)
    for t in range(8):
        if t > core_id:
            amask[:, t * 16:(t + 1) * 16] = -1e30
    d = dict(
        wproj_t=Wproj.reshape(DJ, 128, 1024),
        wrt_t=wrt.reshape(DJ, 128, 8),
        ln2w=np.asarray(ins["L1_ffn_ln2_weight"], np.float32).reshape(DJ, 128).T.copy(),
        ln2b=np.asarray(ins["L1_ffn_ln2_bias"], np.float32).reshape(DJ, 128).T.copy(),
        amask=amask,
    )
    w1 = np.asarray(ins["L1_ffn_moe_w1"], np.float32)
    w2 = np.asarray(ins["L1_ffn_moe_w2"], np.float32)
    for e_ in range(8):
        d["moe1_%d" % e_] = np.ascontiguousarray(
            w1[e_].reshape(8, 128, 2, 8, 128).transpose(2, 3, 1, 0, 4)
            .reshape(16, 128, 1024))
        d["moe2_%d" % e_] = np.ascontiguousarray(
            w2[e_].reshape(2, 8, 128, 2, 512).transpose(3, 0, 1, 2, 4)
            .reshape(2, 16, 128, 512))
    return d


def prep_l1_ffn(ins, core_id):
    qkv = np.asarray(ins["L1_ffn_qkv_weight"], np.float32)
    Wq = np.ascontiguousarray(qkv[0:1024].T)
    Wk = qkv[1024:1280].T
    Wv = qkv[1280:1536].T
    Wk_rep = np.ascontiguousarray(
        np.repeat(Wk.reshape(1024, 4, 64), 4, axis=1).reshape(1024, 1024))
    Wv_rep = np.ascontiguousarray(
        np.repeat(Wv.reshape(1024, 4, 64), 4, axis=1).reshape(1024, 1024))
    Wproj = np.ascontiguousarray(np.asarray(ins["L1_ffn_proj_weight"],
                                            np.float32).T)
    wrt = np.ascontiguousarray(np.asarray(ins["L1_ffn_router_weight"],
                                          np.float32).T)
    amask = np.zeros((128, 128), np.float32)
    for t in range(8):
        if t > core_id:
            amask[:, t * 16:(t + 1) * 16] = -1e30
    d = dict(
        f1q_t=Wq.reshape(DJ, 128, 1024),
        wkrep_t=Wk_rep.reshape(DJ, 128, 1024),
        wvrep_t=Wv_rep.reshape(DJ, 128, 1024),
        wproj_t=Wproj.reshape(DJ, 128, 1024),
        wrt_t=wrt.reshape(DJ, 128, 8),
        ln1w=np.asarray(ins["L1_ffn_ln1_weight"], np.float32).reshape(DJ, 128).T.copy(),
        ln1b=np.asarray(ins["L1_ffn_ln1_bias"], np.float32).reshape(DJ, 128).T.copy(),
        ln2w=np.asarray(ins["L1_ffn_ln2_weight"], np.float32).reshape(DJ, 128).T.copy(),
        ln2b=np.asarray(ins["L1_ffn_ln2_bias"], np.float32).reshape(DJ, 128).T.copy(),
        amask=amask,
    )
    w1 = np.asarray(ins["L1_ffn_moe_w1"], np.float32)
    w2 = np.asarray(ins["L1_ffn_moe_w2"], np.float32)
    for e_ in range(8):
        d["moe1_%d" % e_] = np.ascontiguousarray(
            w1[e_].reshape(8, 128, 2, 8, 128).transpose(2, 3, 1, 0, 4)
            .reshape(16, 128, 1024))
        d["moe2_%d" % e_] = np.ascontiguousarray(
            w2[e_].reshape(2, 8, 128, 2, 512).transpose(3, 0, 1, 2, 4)
            .reshape(2, 16, 128, 512))
    return d


L1FFN_PRE_SPECS = [("f1q_t", [DJ, 128, 1024]), ("wkrep_t", [DJ, 128, 1024]),
                   ("wvrep_t", [DJ, 128, 1024]),
                   ("ln1w", [128, DJ]), ("ln1b", [128, DJ])]
L1FFN_POST_SPECS = ([("wproj_t", [DJ, 128, 1024]), ("wrt_t", [DJ, 128, 8]),
                     ("ln2w", [128, DJ]), ("ln2b", [128, DJ]),
                     ("amask", [128, 128])] +
                    [("moe1_%d" % e_, [16, 128, 1024]) for e_ in range(8)] +
                    [("moe2_%d" % e_, [2, 16, 128, 512]) for e_ in range(8)])
L1FFN_IO_SPECS = L1FFN_PRE_SPECS + L1FFN_POST_SPECS


def emit_l3_ffn(p, ar, banks, io, pfx):
    m0 = ar.mark()
    xn3 = ar.alloc(DJ * NT, BF16)
    wcols = ar.alloc(NTT * 8, F32)
    mln = ar.mark()
    specs = emit_ln_to_fm(
        p, ar, banks, lambda i: io["res_in"][i * 128:(i + 1) * 128, :],
        lambda i, j: xn3[:, j * NT + i * 128: j * NT + i * 128 + 128],
        pfx=pfx + "ln")
    ar.reset(mln)
    wrt = ar.alloc(DJ * 8, BF16)
    rstt = [ar.alloc(24, F32) for _ in range(2)]
    for sem, vv in specs:
        p.wait("gpsimd", sem, vv)
    p.wait("gpsimd", pfx + "ln_nm", p.val(pfx + "ln_nm"))
    for j in range(DJ):
        p.dma("gpsimd", wrt[:, j * 8:(j + 1) * 8], io["wrt3_t"][j], pfx + "rw")
    for sem, vv in specs:
        p.wait("tensor", sem, vv)
    p.wait("tensor", pfx + "rw", p.val(pfx + "rw"))

    def rt_out(i, pbs, pe_ready):
        (pb, w0, w1, rel) = pbs[0]
        st = rstt[i % 2]
        s_rt = pfx + "rt"
        p.wait("vector", pe_ready[0], pe_ready[1])
        if i >= 2:
            p.wait("vector", s_rt, p.val(s_rt))
        p.do("vector", lambda e, pb=pb, st=st: e.tensor_reduce(
            st[:, 0:1], pb[:, 0:8], AX.X, ALU.max), inc=(s_rt, 1))
        p.wait("vector", s_rt, p.val(s_rt))
        p.do("vector", lambda e, pb=pb, st=st: e.tensor_scalar(
            st[:, 8:16], pb[:, 0:8], st[:, 0:1], None, ALU.is_equal,
            ALU.bypass), inc=(s_rt, 1))
        p.wait("vector", s_rt, p.val(s_rt))
        p.do("vector", lambda e, pb=pb, st=st: e.scalar_tensor_tensor(
            st[:, 16:24], st[:, 8:16], -1e30, pb[:, 0:8], ALU.mult, ALU.add),
            inc=rel)
        p.wait("vector", rel[0], p.val(rel[0]))
        p.do("vector", lambda e, st=st: e.tensor_reduce(
            st[:, 1:2], st[:, 16:24], AX.X, ALU.max), inc=(s_rt, 1))
        p.wait("vector", s_rt, p.val(s_rt))
        p.do("vector", lambda e, st=st: e.tensor_tensor(
            st[:, 2:3], st[:, 0:1], st[:, 1:2], ALU.subtract), inc=(s_rt, 1))
        p.wait("scalar", s_rt, p.val(s_rt))
        p.do("scalar", lambda e, st=st: e.activation(
            st[:, 3:4], st[:, 2:3], AF.Sigmoid), inc=(s_rt, 1))
        p.wait("vector", s_rt, p.val(s_rt))
        p.do("vector", lambda e, st=st: e.tensor_scalar(
            st[:, 4:5], st[:, 3:4], -1.0, 1.0, ALU.mult, ALU.add),
            inc=(s_rt, 1))
        p.wait("vector", s_rt, p.val(s_rt))
        p.do("vector", lambda e, st=st: e.tensor_scalar(
            st[:, 16:24], st[:, 16:24], st[:, 1:2], None, ALU.is_equal,
            ALU.bypass), inc=(s_rt, 1))
        p.wait("vector", s_rt, p.val(s_rt))
        p.do("vector", lambda e, st=st: e.tensor_scalar(
            st[:, 8:16], st[:, 8:16], st[:, 3:4], None, ALU.mult, ALU.bypass),
            inc=(s_rt, 1))
        p.wait("vector", s_rt, p.val(s_rt))
        p.do("vector", lambda e, st=st, i=i: e.scalar_tensor_tensor(
            wcols[:, i * 8:(i + 1) * 8], st[:, 16:24], st[:, 4:5],
            st[:, 8:16], ALU.mult, ALU.add), inc=(s_rt, 1))
        p.wait("vector", s_rt, p.val(s_rt))

    emit_proj_tm(p, xn3, wrt, rt_out, banks, (4, 5, 6, 7), pfx + "rt",
                 out_w=8, kdj=DJ)

    NQ = 4
    TPQ = NTT // NQ
    acc = ar.alloc(TPQ * 1024, F32)
    h1t = ar.alloc(32 * 1024, BF16)
    NS1, NS2 = 3, 2
    w1r = [ar.alloc(1024, BF16) for _ in range(NS1)]
    w2r = [ar.alloc(512, BF16) for _ in range(NS2)]
    b1all = ar.alloc(256, F32)
    b2sl = [ar.alloc(1024, BF16) for _ in range(2)]
    s_mm = pfx + "mm"
    w1g = [0] * NS1
    w2g = [0] * NS2
    b2g = [0, 0]
    w1n = [0]
    w2n = [0]

    def load_w1(e_, u):
        sl = w1n[0] % NS1
        s_w = pfx + "wa%d" % sl
        p.wait("gpsimd", s_mm, w1g[sl])
        p.dma("gpsimd", w1r[sl][:, :], io["w1_%d" % e_][u], s_w)
        w1n[0] += 1
        return sl, (s_w, p.val(s_w))

    def load_w2(e_, k, u):
        sl = w2n[0] % NS2
        s_w = pfx + "wb%d" % sl
        p.wait("gpsimd", s_mm, w2g[sl])
        p.dma("gpsimd", w2r[sl][:, :], io["w2_%d" % e_][k, u], s_w)
        w2n[0] += 1
        return sl, (s_w, p.val(s_w))

    p.dma("gpsimd", b1all[:, :], io["b1all"][:, :], pfx + "b1a")
    p.wait("scalar", pfx + "b1a", p.val(pfx + "b1a"))
    p.wait("vector", pfx + "rt_pe", p.val(pfx + "rt_pe"))
    p.wait("sync", pfx + "ln_tr", p.val(pfx + "ln_tr"))
    p.wait("sync", pfx + "ln_nm", p.val(pfx + "ln_nm"))
    p.wait("gpsimd", pfx + "ln_tr", p.val(pfx + "ln_tr"))
    for q in range(NQ):
        if q >= 1:
            p.wait("sync", pfx + "ro", p.val(pfx + "ro"))
        p.dma("sync", acc[:, :].rearrange("p (t c) -> p t c", c=1024),
              io["res_in"][:, :].rearrange("(qq t pp) c -> qq pp t c", pp=128,
                                           t=TPQ)[q], pfx + "al")
        p.wait("vector", pfx + "al", p.val(pfx + "al"))
        c0 = q * TPQ * 128
        for e_ in range(8):
            sb = e_ % 2
            s_b = pfx + "bb%d" % sb
            p.wait("gpsimd", s_mm, b2g[sb])
            p.dma("gpsimd", b2sl[sb][0:1, :], io["b2_%d" % e_][:, :], s_b)
            pend1 = [load_w1(e_, 0), load_w1(e_, 1)]
            pend2 = []
            for u in range(32):
                sl, rdy = pend1.pop(0)
                if u + 2 < 32:
                    pend1.append(load_w1(e_, u + 2))
                elif u == 31:
                    pend2.append(load_w2(e_, 0, 0))
                p.wait("tensor", rdy[0], rdy[1])
                for h in range(2):
                    bid = (2 * u + h) % 4
                    pb = banks[bid]
                    p.bank_acquire("tensor", bid)
                    for j in range(DJ):
                        p.do("tensor", lambda e, pb=pb, sl=sl, j=j, u=u, h=h,
                             c0=c0: e.matmul(
                                 pb[:, :], w1r[sl][:, j * 128:(j + 1) * 128],
                                 xn3[:, j * NT + c0 + h * 512:
                                     j * NT + c0 + h * 512 + 512],
                                 start=(j == 0), stop=(j == DJ - 1)),
                             inc=(s_mm, 1) if j == DJ - 1 else None)
                    p.wait("scalar", s_mm, p.val(s_mm))
                    p.do("scalar", lambda e, pb=pb, u=u, h=h, e_=e_:
                         e.activation(
                             h1t[:, u * 1024 + h * 512: u * 1024 + h * 512 + 512],
                             pb[:, :], AF.Gelu,
                             bias=b1all[:, e_ * 32 + u: e_ * 32 + u + 1],
                             scale=1.0), inc=(p.bk(bid), 1))
                w1g[sl] = p.val(s_mm)
            p.wait("tensor", s_b, p.val(s_b))
            for k in range(2):
                for tt in range(8):
                    p.bank_acquire("tensor", tt)
                for u in range(32):
                    sl, rdy = pend2.pop(0)
                    if u + 1 < 32:
                        pend2.append(load_w2(e_, k, u + 1))
                    elif k == 0:
                        pend2.append(load_w2(e_, 1, 0))
                    p.wait("tensor", rdy[0], rdy[1])
                    for tt in range(8):
                        pb = banks[tt]
                        p.do("tensor", lambda e, pb=pb, sl=sl, u=u, tt=tt:
                             e.matmul(
                                 pb[:, :],
                                 h1t[:, u * 1024 + tt * 128:
                                     u * 1024 + tt * 128 + 128],
                                 w2r[sl][:, :], start=(u == 0), stop=False),
                             inc=(s_mm, 1) if tt == 7 else None)
                    w2g[sl] = p.val(s_mm)
                for tt in range(8):
                    gt = q * TPQ + tt
                    pb = banks[tt]
                    p.do("tensor", lambda e, pb=pb, sb=sb, k=k: e.matmul(
                        pb[:, :], p.c["ones_bf"][0:1, 0:128],
                        b2sl[sb][0:1, k * 512:(k + 1) * 512], start=False,
                        stop=True), inc=(s_mm, 1))
                    p.wait("vector", s_mm, p.val(s_mm))
                    p.do("vector", lambda e, pb=pb, tt=tt, k=k, gt=gt, e_=e_:
                         e.scalar_tensor_tensor(
                             acc[:, tt * 1024 + k * 512: tt * 1024 + k * 512 + 512],
                             pb[:, :], wcols[:, gt * 8 + e_:gt * 8 + e_ + 1],
                             acc[:, tt * 1024 + k * 512: tt * 1024 + k * 512 + 512],
                             ALU.mult, ALU.add), inc=(p.bk(tt), 1))
            b2g[sb] = p.val(s_mm)
        for b in range(8):
            p.wait("sync", p.bk(b), p.val(p.bk(b)))
        p.dma("sync",
              io["res_out"][:, :].rearrange("(qq t pp) c -> qq pp t c",
                                            pp=128, t=TPQ)[q],
              acc[:, :].rearrange("p (t c) -> p t c", c=1024), pfx + "ro")
    p.wait("sync", pfx + "ro", p.val(pfx + "ro"))
    ar.reset(m0)


def prep_l3_ffn(ins):
    wrt = np.ascontiguousarray(np.asarray(ins["L3_ffn_router_w"], np.float32).T)
    w1 = np.asarray(ins["L3_ffn_w1"], np.float32)
    w2 = np.asarray(ins["L3_ffn_w2"], np.float32)
    b1 = np.asarray(ins["L3_ffn_b1"], np.float32)
    b2 = np.asarray(ins["L3_ffn_b2"], np.float32)
    d = dict(wrt3_t=wrt.reshape(DJ, 128, 8))
    d["b1all"] = np.ascontiguousarray(
        np.concatenate([b1[e_].reshape(32, 128).T for e_ in range(8)], axis=1))
    for e_ in range(8):
        w1t = w1[e_].T
        d["w1_%d" % e_] = np.ascontiguousarray(
            w1t.reshape(8, 128, 4, 8, 128).transpose(2, 3, 1, 0, 4)
            .reshape(32, 128, 1024))
        w2t = w2[e_].T
        d["w2_%d" % e_] = np.ascontiguousarray(
            w2t.reshape(4, 8, 128, 2, 512).transpose(3, 0, 1, 2, 4)
            .reshape(2, 32, 128, 512))
        d["b2_%d" % e_] = b2[e_].reshape(1, 1024).copy()
    return d


L3FFN_IO_SPECS = ([("wrt3_t", [DJ, 128, 8]), ("b1all", [128, 256])] +
                  [("w1_%d" % e_, [32, 128, 1024]) for e_ in range(8)] +
                  [("w2_%d" % e_, [2, 32, 128, 512]) for e_ in range(8)] +
                  [("b2_%d" % e_, [1, 1024]) for e_ in range(8)])


def emit_l3_attn(p, ar, banks, io, pfx):
    EPSA = 1e-6
    H = 16
    m0 = ar.mark()

    mpro = ar.mark()
    post = [ar.alloc(D, F32) for _ in range(2)]
    x3t = [ar.alloc(D, F32) for _ in range(2)]

    def pro_cb(i, xt, st, st_ready):
        po, x3 = post[i % 2], x3t[i % 2]
        s_po = pfx + "po%d" % (i % 2)
        s_xo = pfx + "xo%d" % (i % 2)
        if i >= 2:
            p.wait("sync", pfx + "pv", p.val(pfx + "pv"))
        p.dma("sync", po[:, :], io["pos_d"][i * 128:(i + 1) * 128, :], s_po)
        p.wait("vector", st_ready[0], st_ready[1])
        p.wait("vector", s_po, p.val(s_po))
        if i >= 2:
            p.wait("vector", s_xo, p.val(s_xo))
        p.do("vector", lambda e, st=st: e.tensor_scalar(
            st[:, 5:6], st[:, 2:3], 32.0, None, ALU.mult, ALU.bypass),
            inc=(pfx + "pv", 1))
        p.wait("vector", pfx + "pv", p.val(pfx + "pv"))
        p.do("vector", lambda e, xt=xt, st=st, x3=x3: e.tensor_scalar(
            x3[:, :], xt[:, :], st[:, 0:1], st[:, 5:6], ALU.subtract,
            ALU.mult), inc=(pfx + "pv", 1))
        p.wait("vector", pfx + "pv", p.val(pfx + "pv"))
        p.do("vector", lambda e, x3=x3, po=po: e.tensor_tensor(
            x3[:, :], x3[:, :], po[:, :], ALU.add), inc=(pfx + "pv", 1))
        p.wait("sync", pfx + "pv", p.val(pfx + "pv"))
        p.dma("sync", io["x3_d"][i * 128:(i + 1) * 128, :], x3[:, :], s_xo)
        return [(pfx + "pv", p.val(pfx + "pv"))]

    emit_ln_tiles(p, ar, lambda i: io["res_in"][i * 128:(i + 1) * 128, :],
                  NTT, pro_cb, pfx=pfx + "pl")
    ar.reset(mpro)
    for s in range(2):
        p.wait("sync", pfx + "xo%d" % s, p.val(pfx + "xo%d" % s))
        p.wait("gpsimd", pfx + "xo%d" % s, p.val(pfx + "xo%d" % s))

    fst = dict(w1g=[0, 0, 0], w2g=[0, 0], w1n=0, w2n=0)
    for L in range(4):
        lp = pfx + "Y"
        m1 = ar.mark()
        wc = ar.alloc(DJ, F32)
        bc = ar.alloc(DJ, F32)
        p.wait("gpsimd", pfx + "pv", p.val(pfx + "pv"))
        if L > 0:
            pass
        p.dma("gpsimd", wc[:, :], io["ln1s"][L], lp + "lw")
        p.dma("gpsimd", bc[:, :], io["ln1b"][L], lp + "lw")
        p.wait("vector", lp + "lw", p.val(lp + "lw"))
        ln1 = ar.alloc(DJ * NT, BF16)
        mln = ar.mark()
        specs = emit_ln_to_fm(
            p, ar, banks, lambda i: io["x3_d"][i * 128:(i + 1) * 128, :],
            lambda i, j: ln1[:, j * NT + i * 128: j * NT + i * 128 + 128],
            scale_bias=(wc, bc), pfx=lp + "n1")
        ar.reset(mln)
        wslot = ar.alloc(DJ * 1040, BF16)
        brow = ar.alloc(1040, BF16, parts=1)
        m_stg = ar.mark()
        kst = [ar.alloc(1024, BF16) for _ in range(2)]
        kfa = ar.alloc(1024, F32)
        kfb = ar.alloc(1024, F32)
        for sem, vv in specs:
            p.wait("gpsimd", sem, vv)
        p.wait("gpsimd", lp + "n1_nm", p.val(lp + "n1_nm"))

        for j in range(DJ):
            p.dma("gpsimd", wslot[:, j * 1040:j * 1040 + 1024], io["wk3"][L][j],
                  lp + "wk")
        p.dma("gpsimd", brow[0:1, 0:1024], io["bk3"][L][:, :], lp + "wk")
        p.wait("tensor", lp + "wk", p.val(lp + "wk"))
        for sem, vv in specs:
            p.wait("tensor", sem, vv)

        def k_out(i, pbs, pe_ready, lp=lp):
            ks = kst[i % 2]
            s_ks = lp + "ks%d" % (i % 2)
            if i >= 2:
                p.wait("scalar", s_ks, p.val(s_ks))
            p.wait("scalar", pe_ready[0], pe_ready[1])
            p.wait("vector", pe_ready[0], pe_ready[1])
            for pb, w0, w1, rel in pbs:
                p.wait("scalar", lp + "kv", p.val(lp + "kv"))
                p.do("scalar", lambda e, pb=pb, w0=w0, w1=w1: e.activation(
                    kfa[:, w0:w1], pb[:, 0:w1 - w0], AF.Relu),
                    inc=(lp + "ke", 1))
                p.wait("vector", lp + "ke", p.val(lp + "ke"))
                p.do("vector", lambda e, pb=pb, w0=w0, w1=w1: e.tensor_scalar(
                    kfb[:, w0:w1], pb[:, 0:w1 - w0], 0.0, None, ALU.min,
                    ALU.bypass), inc=rel)
                p.wait("scalar", rel[0], p.val(rel[0]))
                p.wait("scalar", lp + "ke", p.val(lp + "ke"))
                p.do("scalar", lambda e, w0=w0, w1=w1: e.activation(
                    kfb[:, w0:w1], kfb[:, w0:w1], AF.Exp), inc=(lp + "ke", 1))
                p.wait("vector", lp + "ke", p.val(lp + "ke"))
                p.do("vector", lambda e, ks=ks, w0=w0, w1=w1: e.tensor_tensor(
                    ks[:, w0:w1], kfa[:, w0:w1], kfb[:, w0:w1], ALU.add),
                    inc=(lp + "kv", 1))
            p.wait("sync", lp + "kv", p.val(lp + "kv"))
            p.dma("sync", io["k3_d"][i], ks[:, :], s_ks)

        emit_proj_tm(p, ln1, wslot, k_out, banks, (0, 1, 2, 3), lp + "k",
                     out_w=1024, kdj=DJ, erow=brow[0:1, 0:1024], w_stride=1040)
        for s in range(2):
            p.wait("gpsimd", lp + "ks%d" % s, p.val(lp + "ks%d" % s))
        p.wait("gpsimd", lp + "k_pe", p.val(lp + "k_pe"))

        vst = [ar.alloc(1040, BF16) for _ in range(2)]
        for j in range(DJ):
            p.dma("gpsimd", wslot[:, j * 1040:(j + 1) * 1040],
                  io["wv3aug"][L][j], lp + "wv")
        p.dma("gpsimd", brow[:, :], io["bv3erow"][L][:, :], lp + "wv")
        p.wait("tensor", lp + "wv", p.val(lp + "wv"))

        def v_out(i, pbs, pe_ready, lp=lp):
            vs = vst[i % 2]
            s_vs = lp + "vs%d" % (i % 2)
            if i >= 2:
                p.wait("vector", s_vs, p.val(s_vs))
            p.wait("vector", pe_ready[0], pe_ready[1])
            rels = []
            for pb, w0, w1, rel in pbs:
                p.do("vector", lambda e, pb=pb, vs=vs, w0=w0, w1=w1:
                     e.tensor_copy(vs[:, w0:w1], pb[:, 0:w1 - w0]), inc=rel)
                rels.append((rel[0], p.val(rel[0])))
            for sem, vv in rels:
                p.wait("sync", sem, vv)
            p.dma("sync", io["v3_d"][i], vs[:, :], s_vs)

        emit_proj_tm(p, ln1, wslot, v_out, banks, (0, 1, 2, 3, 4, 5), lp + "v",
                     out_w=1040, kdj=DJ, erow=brow, w_stride=1040)
        for s in range(2):
            p.wait("gpsimd", lp + "vs%d" % s, p.val(lp + "vs%d" % s))
        p.wait("gpsimd", lp + "v_pe", p.val(lp + "v_pe"))

        for j in range(DJ):
            p.dma("gpsimd", wslot[:, j * 1040:j * 1040 + 1024], io["wq3"][L][j],
                  lp + "wq")
        p.dma("gpsimd", brow[0:1, 0:1024], io["bq3"][L][:, :], lp + "wq")
        p.wait("tensor", lp + "wq", p.val(lp + "wq"))
        qst = [ar.alloc(512, BF16) for _ in range(3)]
        nq = [0]
        for h in range(H):
            for tc in range(8):
                kq = nq[0]
                bid = kq % 2
                pb = banks[bid]
                qs = qst[kq % 3]
                s_qs = lp + "qs%d" % (kq % 3)
                p.bank_acquire("tensor", bid)
                for j in range(DJ):
                    p.do("tensor", lambda e, pb=pb, j=j, h=h, tc=tc: e.matmul(
                        pb[0:64, :],
                        wslot[:, j * 1040 + h * 64: j * 1040 + h * 64 + 64],
                        ln1[:, j * NT + tc * 512: j * NT + (tc + 1) * 512],
                        start=(j == 0), stop=False))
                p.do("tensor", lambda e, pb=pb, h=h: e.matmul(
                    pb[0:64, :], brow[0:1, h * 64:h * 64 + 64],
                    p.c["ones_bf"][0:1, 0:512], start=False, stop=True),
                    inc=(lp + "qm", 1))
                if kq >= 3:
                    p.wait("scalar", s_qs, p.val(s_qs))
                p.wait("scalar", lp + "qm", p.val(lp + "qm"))
                p.wait("vector", lp + "qm", p.val(lp + "qm"))
                p.wait("scalar", lp + "qv", p.val(lp + "qv"))
                p.do("scalar", lambda e, pb=pb: e.activation(
                    kfa[0:64, 0:512], pb[0:64, :], AF.Relu), inc=(lp + "qe", 1))
                p.wait("vector", lp + "qe", p.val(lp + "qe"))
                p.do("vector", lambda e, pb=pb: e.tensor_scalar(
                    kfb[0:64, 0:512], pb[0:64, :], 0.0, None, ALU.min,
                    ALU.bypass), inc=(p.bk(bid), 1))
                p.wait("scalar", p.bk(bid), p.val(p.bk(bid)))
                p.wait("scalar", lp + "qe", p.val(lp + "qe"))
                p.do("scalar", lambda e: e.activation(
                    kfb[0:64, 0:512], kfb[0:64, 0:512], AF.Exp),
                    inc=(lp + "qe", 1))
                p.wait("vector", lp + "qe", p.val(lp + "qe"))
                p.do("vector", lambda e, qs=qs: e.tensor_tensor(
                    qs[0:64, :], kfa[0:64, 0:512], kfb[0:64, 0:512], ALU.add),
                    inc=(lp + "qv", 1))
                p.wait("sync", lp + "qv", p.val(lp + "qv"))
                p.dma("sync", io["q3_d"][h, :, tc * 512:(tc + 1) * 512],
                      qs[0:64, :], s_qs)
                nq[0] += 1
        for s in range(3):
            p.wait("sync", lp + "qs%d" % s, p.val(lp + "qs%d" % s))

        for s in range(3):
            p.wait("sync", lp + "qs%d" % s, p.val(lp + "qs%d" % s))
        for s in range(2):
            p.wait("sync", lp + "ks%d" % s, p.val(lp + "ks%d" % s))
            p.wait("sync", lp + "vs%d" % s, p.val(lp + "vs%d" % s))
        p.wait("sync", lp + "qv", p.val(lp + "qv"))
        p.wait("sync", lp + "kv", p.val(lp + "kv"))
        p.wait("sync", lp + "qe", p.val(lp + "qe"))
        p.wait("sync", lp + "ke", p.val(lp + "ke"))
        ar.reset(m_stg)
        ar2 = m_stg
        kv_sb = ar.alloc(16 * 65, BF16, parts=64)
        ksl = [ar.alloc(1024, BF16) for _ in range(2)]
        vsl = [ar.alloc(1040, BF16) for _ in range(2)]
        for hp in range(2):
            for b in range(8):
                p.bank_acquire("tensor", b)
            for i in range(NTT):
                kk, vv2 = ksl[i % 2], vsl[i % 2]
                s_sl = lp + "c%d" % (i % 2)
                if i >= 2 or hp == 1:
                    p.wait("sync", lp + "cm", p.val(lp + "cm"))
                p.dma("sync", kk[:, :], io["k3_d"][i], s_sl)
                p.dma("sync", vv2[:, :], io["v3_d"][i], s_sl)
                p.wait("tensor", s_sl, p.val(s_sl))
                for h8 in range(8):
                    h = hp * 8 + h8
                    p.do("tensor", lambda e, kk=kk, vv2=vv2, h=h, i=i, h8=h8:
                         e.matmul(
                             banks[h8][0:64, 0:65],
                             kk[:, h * 64:(h + 1) * 64],
                             vv2[:, h * 65:(h + 1) * 65],
                             start=(i == 0), stop=(i == NTT - 1)),
                         inc=(lp + "cm", 1) if h8 == 7 else None)
            p.wait("vector", lp + "cm", p.val(lp + "cm"))
            for h8 in range(8):
                h = hp * 8 + h8
                p.do("vector", lambda e, h=h, h8=h8: e.tensor_copy(
                    kv_sb[:, h * 65:(h + 1) * 65], banks[h8][0:64, 0:65]),
                    inc=(p.bk(h8), 1))
        for b in range(8):
            p.wait("tensor", p.bk(b), p.val(p.bk(b)))

        at3 = ln1
        p.wait("vector", lp + "qm", p.val(lp + "qm"))
        p.wait("sync", lp + "qm", p.val(lp + "qm"))
        qh2 = [ar.alloc(NT, BF16, parts=64) for _ in range(2)]
        tmp64 = ar.alloc(512, BF16, parts=64)
        rl = ar.alloc(512, F32, parts=1)
        bl = ar.alloc(512, F32, parts=64)
        for h in range(H):
            qh = qh2[h % 2]
            s_qh = lp + "qh%d" % (h % 2)
            if h >= 2:
                p.wait("sync", lp + "am", p.val(lp + "am"))
            p.dma("sync", qh[:, :], io["q3_d"][h], s_qh)
            p.wait("tensor", s_qh, p.val(s_qh))
            for qc in range(8):
                p.bank_acquire("tensor", 6)
                p.do("tensor", lambda e, qh=qh, h=h, qc=qc: e.matmul(
                    banks[6][0:65, :], kv_sb[:, h * 65:(h + 1) * 65],
                    qh[:, qc * 512:(qc + 1) * 512], start=True, stop=True),
                    inc=(lp + "am", 1))
                p.wait("vector", lp + "am", p.val(lp + "am"))
                p.wait("vector", lp + "bm", p.val(lp + "bm"))
                p.do("vector", lambda e: e.tensor_scalar(
                    rl[:, :], banks[6][64:65, :], float(EPSA), None, ALU.add,
                    ALU.bypass), inc=(lp + "rl", 1))
                p.wait("vector", lp + "rl", p.val(lp + "rl"))
                p.do("vector", lambda e: e.reciprocal(rl[:, :], rl[:, :]),
                     inc=(lp + "rl", 1))
                p.wait("tensor", lp + "rl", p.val(lp + "rl"))
                p.bank_acquire("tensor", 7)
                p.do("tensor", lambda e: e.matmul(
                    banks[7][0:64, :], p.c["ones_f"][0:1, 0:64], rl[:, :],
                    start=True, stop=True), inc=(lp + "bm", 1))
                p.wait("scalar", lp + "bm", p.val(lp + "bm"))
                p.wait("scalar", p.bk(6), p.val(p.bk(6)))
                p.do("scalar", lambda e: e.copy(bl[:, :], banks[7][0:64, :]),
                     inc=(p.bk(7), 1))
                p.wait("vector", p.bk(7), p.val(p.bk(7)))
                col0 = (h // 2) * NT + qc * 512
                if h % 2 == 0:
                    p.do("vector", lambda e, col0=col0: e.tensor_tensor(
                        at3[0:64, col0:col0 + 512], banks[6][0:64, :],
                        bl[:, :], ALU.mult), inc=(p.bk(6), 1))
                else:
                    if p.val(lp + "od"):
                        p.wait("vector", lp + "od", p.val(lp + "od"))
                    p.do("vector", lambda e: e.tensor_tensor(
                        tmp64[:, :], banks[6][0:64, :], bl[:, :], ALU.mult),
                        inc=(p.bk(6), 1))
                    p.wait("sync", p.bk(6), p.val(p.bk(6)))
                    p.dma("sync", at3[64:128, col0:col0 + 512], tmp64[:, :],
                          lp + "od")
        at_fin = [(p.bk(6), p.val(p.bk(6))), (lp + "od", p.val(lp + "od"))]

        s_w2 = lp + "wo"
        p.wait("gpsimd", lp + "am", p.val(lp + "am"))
        p.wait("gpsimd", lp + "qm", p.val(lp + "qm"))
        for j in range(DJ):
            p.dma("gpsimd", wslot[:, j * 1040:j * 1040 + 1024], io["wo3"][L][j],
                  s_w2)
        p.dma("gpsimd", brow[0:1, 0:1024], io["bo3"][L][:, :], s_w2)
        p.wait("tensor", s_w2, p.val(s_w2))
        for sem, vv in at_fin:
            p.wait("tensor", sem, vv)
        x3a = [ar.alloc(D, F32) for _ in range(2)]
        x3o = [ar.alloc(D, F32) for _ in range(2)]
        rg = {}
        for i in range(NTT):
            xi, xo = x3a[i % 2], x3o[i % 2]
            s_xl = lp + "xl%d" % (i % 2)
            s_xs = lp + "xs%d" % (i % 2)
            if i >= 2:
                for sem, vv in rg[i - 2]:
                    p.wait("sync", sem, vv)
            p.dma("sync", xi[:, :], io["x3_d"][i * 128:(i + 1) * 128, :], s_xl)
            g4 = 4 * (i % 2)
            for k in range(2):
                bid_e = g4 + k
                pbe = banks[bid_e]
                p.bank_acquire("tensor", bid_e)
                for n, h in enumerate(range(0, H, 2)):
                    p.do("tensor", lambda e, pbe=pbe, h=h, i=i, k=k, n=n:
                         e.matmul(
                             pbe[:, :],
                             at3[0:64, (h // 2) * NT + i * 128:(h // 2) * NT + i * 128 + 128],
                             wslot[0:64, (h // 2) * 1040 + k * 512:(h // 2) * 1040 + k * 512 + 512],
                             start=(n == 0), stop=False))
                p.do("tensor", lambda e, pbe=pbe, k=k: e.matmul(
                    pbe[:, :], p.c["ones_bf"][0:1, 0:128],
                    brow[0:1, k * 512:(k + 1) * 512], start=False, stop=True),
                    inc=(lp + "pw", 1))
                bid_o = g4 + 2 + k
                pbo = banks[bid_o]
                p.bank_acquire("tensor", bid_o)
                for n, h in enumerate(range(1, H, 2)):
                    p.do("tensor", lambda e, pbo=pbo, h=h, i=i, k=k, n=n:
                         e.matmul(
                             pbo[:, :],
                             at3[64:128, (h // 2) * NT + i * 128:(h // 2) * NT + i * 128 + 128],
                             wslot[64:128, (h // 2) * 1040 + k * 512:(h // 2) * 1040 + k * 512 + 512],
                             start=(n == 0), stop=(n == H // 2 - 1)),
                         inc=(lp + "pw", 1) if n == H // 2 - 1 else None)
            p.wait("vector", lp + "pw", p.val(lp + "pw"))
            p.wait("vector", s_xl, p.val(s_xl))
            if i >= 2:
                p.wait("vector", s_xs, p.val(s_xs))
            specs_i = []
            for k in range(2):
                bid_e, bid_o = g4 + k, g4 + 2 + k
                p.do("vector", lambda e, pb=banks[bid_e], xi=xi, xo=xo, k=k:
                     e.tensor_tensor(xo[:, k * 512:(k + 1) * 512], pb[:, :],
                                     xi[:, k * 512:(k + 1) * 512], ALU.add),
                     inc=(p.bk(bid_e), 1))
                specs_i.append((p.bk(bid_e), p.val(p.bk(bid_e))))
                p.wait("vector", p.bk(bid_e), p.val(p.bk(bid_e)))
                p.do("vector", lambda e, pb=banks[bid_o], xo=xo, k=k:
                     e.tensor_tensor(xo[:, k * 512:(k + 1) * 512],
                                     xo[:, k * 512:(k + 1) * 512], pb[:, :],
                                     ALU.add), inc=(p.bk(bid_o), 1))
                specs_i.append((p.bk(bid_o), p.val(p.bk(bid_o))))
            rg[i] = specs_i
            for sem, vv in specs_i:
                p.wait("sync", sem, vv)
            p.dma("sync", io["x3_d"][i * 128:(i + 1) * 128, :], xo[:, :], s_xs)
        for s in range(2):
            p.wait("sync", lp + "xs%d" % s, p.val(lp + "xs%d" % s))

        ar.reset(ar2)
        p.wait("gpsimd", lp + "pw", p.val(lp + "pw"))
        for s in range(2):
            p.wait("gpsimd", lp + "xs%d" % s, p.val(lp + "xs%d" % s))
            p.wait("sync", lp + "xs%d" % s, p.val(lp + "xs%d" % s))
        w2c2 = ar.alloc(DJ, F32)
        b2c2 = ar.alloc(DJ, F32)
        p.dma("gpsimd", w2c2[:, :], io["ln2s"][L], lp + "l2w")
        p.dma("gpsimd", b2c2[:, :], io["l3n2b"][L], lp + "l2w")
        p.wait("vector", lp + "l2w", p.val(lp + "l2w"))
        ln2 = ln1
        p.wait("sync", lp + "qm", p.val(lp + "qm"))
        mln2 = ar.mark()
        specs2 = emit_ln_to_fm(
            p, ar, banks, lambda i: io["x3_d"][i * 128:(i + 1) * 128, :],
            lambda i, j: ln2[:, j * NT + i * 128: j * NT + i * 128 + 128],
            scale_bias=(w2c2, b2c2), pfx=lp + "n2")
        ar.reset(mln2)
        w1r = [ar.alloc(1024, BF16) for _ in range(3)]
        w2r = [ar.alloc(512, BF16) for _ in range(2)]
        h1t = ar.alloc(32 * 1024, BF16)
        b1sb = ar.alloc(32, F32)
        b2row = ar.alloc(1024, BF16, parts=1)
        xf = [ar.alloc(512, F32) for _ in range(2)]
        xg = [ar.alloc(512, F32) for _ in range(2)]
        for sem, vv in specs2:
            p.wait("gpsimd", sem, vv)
        p.wait("gpsimd", lp + "n2_nm", p.val(lp + "n2_nm"))
        p.dma("gpsimd", b1sb[:, :], io["b13"][L], lp + "fb")
        p.dma("gpsimd", b2row[:, :], io["b23"][L][:, :], lp + "fb")
        p.wait("scalar", lp + "fb", p.val(lp + "fb"))
        p.wait("tensor", lp + "fb", p.val(lp + "fb"))
        for sem, vv in specs2:
            p.wait("tensor", sem, vv)
        s_mm = pfx + "fm"

        def load_w1f(u):
            sl = fst["w1n"] % 3
            s_w = pfx + "fa%d" % sl
            p.wait("gpsimd", s_mm, fst["w1g"][sl])
            p.dma("gpsimd", w1r[sl][:, :], io["w13"][L][u], s_w)
            fst["w1n"] += 1
            return sl, (s_w, p.val(s_w))

        def load_w2f(k, u):
            sl = fst["w2n"] % 2
            s_w = pfx + "fc%d" % sl
            p.wait("gpsimd", s_mm, fst["w2g"][sl])
            p.dma("gpsimd", w2r[sl][:, :], io["w23"][L][k, u], s_w)
            fst["w2n"] += 1
            return sl, (s_w, p.val(s_w))

        for ch in range(4):
            c0 = ch * 1024
            pend1 = [load_w1f(0), load_w1f(1)]
            pend2 = []
            for u in range(32):
                sl, rdy = pend1.pop(0)
                if u + 2 < 32:
                    pend1.append(load_w1f(u + 2))
                elif u == 31:
                    pend2.append(load_w2f(0, 0))
                p.wait("tensor", rdy[0], rdy[1])
                for h in range(2):
                    bid = (2 * u + h) % 4
                    pb = banks[bid]
                    p.bank_acquire("tensor", bid)
                    for j in range(DJ):
                        p.do("tensor", lambda e, pb=pb, sl=sl, j=j, c0=c0, h=h:
                             e.matmul(
                                 pb[:, :], w1r[sl][:, j * 128:(j + 1) * 128],
                                 ln2[:, j * NT + c0 + h * 512:
                                     j * NT + c0 + h * 512 + 512],
                                 start=(j == 0), stop=(j == DJ - 1)),
                             inc=(s_mm, 1) if j == DJ - 1 else None)
                    p.wait("scalar", s_mm, p.val(s_mm))
                    p.do("scalar", lambda e, pb=pb, u=u, h=h: e.activation(
                        h1t[:, u * 1024 + h * 512: u * 1024 + h * 512 + 512],
                        pb[:, :], AF.Relu, bias=b1sb[:, u:u + 1], scale=1.0),
                        inc=(p.bk(bid), 1))
                fst["w1g"][sl] = p.val(s_mm)
            for k in range(2):
                for tt in range(8):
                    p.bank_acquire("tensor", tt)
                for u in range(32):
                    sl, rdy = pend2.pop(0)
                    if u + 1 < 32:
                        pend2.append(load_w2f(k, u + 1))
                    elif k == 0:
                        pend2.append(load_w2f(1, 0))
                    p.wait("tensor", rdy[0], rdy[1])
                    for tt in range(8):
                        pb = banks[tt]
                        p.do("tensor", lambda e, pb=pb, sl=sl, u=u, tt=tt:
                             e.matmul(
                                 pb[:, :],
                                 h1t[:, u * 1024 + tt * 128:
                                     u * 1024 + tt * 128 + 128],
                                 w2r[sl][:, :], start=(u == 0), stop=False),
                             inc=(s_mm, 1) if tt == 7 else None)
                    fst["w2g"][sl] = p.val(s_mm)
                for tt in range(8):
                    i = ch * 8 + tt
                    sx = tt % 2
                    xi, xo = xf[sx], xg[sx]
                    s_xl = lp + "fl%d" % sx
                    s_xs = lp + "fs%d" % sx
                    if tt >= 2:
                        p.wait("sync", p.bk(tt - 2), p.val(p.bk(tt - 2)))
                    p.dma("sync", xi[:, :],
                          io["x3_d"][i * 128:(i + 1) * 128,
                                     k * 512:(k + 1) * 512], s_xl)
                    pb = banks[tt]
                    p.do("tensor", lambda e, pb=pb, k=k: e.matmul(
                        pb[:, :], p.c["ones_bf"][0:1, 0:128],
                        b2row[0:1, k * 512:(k + 1) * 512], start=False,
                        stop=True), inc=(s_mm, 1))
                    p.wait("vector", s_mm, p.val(s_mm))
                    p.wait("vector", s_xl, p.val(s_xl))
                    p.wait("vector", s_xs, p.val(s_xs))
                    p.do("vector", lambda e, pb=pb, xi=xi, xo=xo:
                         e.tensor_tensor(xo[:, :], pb[:, :], xi[:, :],
                                         ALU.add), inc=(p.bk(tt), 1))
                    p.wait("sync", p.bk(tt), p.val(p.bk(tt)))
                    p.dma("sync",
                          io["x3_d"][i * 128:(i + 1) * 128,
                                     k * 512:(k + 1) * 512], xo[:, :], s_xs)
        for s in range(2):
            p.wait("sync", lp + "fs%d" % s, p.val(lp + "fs%d" % s))
            p.wait("gpsimd", lp + "fs%d" % s, p.val(lp + "fs%d" % s))
        ar.reset(m1)
        p.wait("sync", s_mm, p.val(s_mm))
        p.wait("gpsimd", s_mm, p.val(s_mm))

    sbc = ar.alloc(D, F32)
    bbc = ar.alloc(D, F32)
    srow = ar.alloc(1024, F32, parts=1)
    brw = ar.alloc(1024, F32, parts=1)
    p.dma("gpsimd", srow[:, :], io["lnfs"][:, :], pfx + "fw")
    p.dma("gpsimd", brw[:, :], io["lnfb"][:, :], pfx + "fw")
    p.wait("tensor", pfx + "fw", p.val(pfx + "fw"))
    for k in range(2):
        p.bank_acquire("tensor", 4 + k)
        p.do("tensor", lambda e, k=k: e.matmul(
            banks[4 + k][:, :], p.c["ones_f"][0:1, 0:128],
            srow[0:1, k * 512:(k + 1) * 512], start=True, stop=True),
            inc=(pfx + "bm", 1))
        p.wait("vector", pfx + "bm", p.val(pfx + "bm"))
        p.do("vector", lambda e, k=k: e.tensor_copy(
            sbc[:, k * 512:(k + 1) * 512], banks[4 + k][:, :]),
            inc=(p.bk(4 + k), 1))
        p.bank_acquire("tensor", 4 + k)
        p.do("tensor", lambda e, k=k: e.matmul(
            banks[4 + k][:, :], p.c["ones_f"][0:1, 0:128],
            brw[0:1, k * 512:(k + 1) * 512], start=True, stop=True),
            inc=(pfx + "bm", 1))
        p.wait("vector", pfx + "bm", p.val(pfx + "bm"))
        p.do("vector", lambda e, k=k: e.tensor_copy(
            bbc[:, k * 512:(k + 1) * 512], banks[4 + k][:, :]),
            inc=(p.bk(4 + k), 1))
    p.wait("vector", p.bk(4), p.val(p.bk(4)))
    p.wait("vector", p.bk(5), p.val(p.bk(5)))
    rsl = [ar.alloc(D, F32) for _ in range(2)]
    rso = [ar.alloc(D, F32) for _ in range(2)]
    mfin = ar.mark()

    def fin_cb(i, xt, st, st_ready):
        rs, ro = rsl[i % 2], rso[i % 2]
        s_rl = pfx + "gr%d" % (i % 2)
        s_ro = pfx + "go%d" % (i % 2)
        if i >= 2:
            p.wait("sync", pfx + "gv", p.val(pfx + "gv"))
        p.dma("sync", rs[:, :], io["res_in"][i * 128:(i + 1) * 128, :], s_rl)
        p.wait("vector", st_ready[0], st_ready[1])
        p.wait("vector", s_rl, p.val(s_rl))
        if i >= 2:
            p.wait("vector", s_ro, p.val(s_ro))
        p.do("vector", lambda e, xt=xt, st=st, ro=ro: e.tensor_scalar(
            ro[:, :], xt[:, :], st[:, 0:1], st[:, 2:3], ALU.subtract,
            ALU.mult), inc=(pfx + "gv", 1))
        p.wait("vector", pfx + "gv", p.val(pfx + "gv"))
        p.do("vector", lambda e, ro=ro: e.tensor_tensor(
            ro[:, :], ro[:, :], sbc[:, :], ALU.mult), inc=(pfx + "gv", 1))
        p.wait("vector", pfx + "gv", p.val(pfx + "gv"))
        p.do("vector", lambda e, ro=ro: e.tensor_tensor(
            ro[:, :], ro[:, :], bbc[:, :], ALU.add), inc=(pfx + "gv", 1))
        p.wait("vector", pfx + "gv", p.val(pfx + "gv"))
        p.do("vector", lambda e, ro=ro, rs=rs: e.tensor_tensor(
            ro[:, :], ro[:, :], rs[:, :], ALU.add), inc=(pfx + "gv", 1))
        p.wait("sync", pfx + "gv", p.val(pfx + "gv"))
        p.dma("sync", io["res_out"][i * 128:(i + 1) * 128, :], ro[:, :], s_ro)
        return [(pfx + "gv", p.val(pfx + "gv"))]

    emit_ln_tiles(p, ar, lambda i: io["x3_d"][i * 128:(i + 1) * 128, :],
                  NTT, fin_cb, pfx=pfx + "gl")
    ar.reset(mfin)
    for s in range(2):
        p.wait("sync", pfx + "go%d" % s, p.val(pfx + "go%d" % s))
    ar.reset(m0)


def prep_l3_attn(ins):
    import math
    d = {}
    for nm in ("Wq", "Wk", "Wv", "Wo"):
        W = np.asarray(ins["L3_attn_" + nm], np.float32)
        if nm == "Wv":
            d["wv3aug"] = np.stack([make_vaug(np.ascontiguousarray(W[i]))
                                    for i in range(4)])
        else:
            key = {"Wq": "wq3", "Wk": "wk3", "Wo": "wo3"}[nm]
            d[key] = np.ascontiguousarray(W.reshape(4, DJ, 128, 1024))
    d["bv3erow"] = np.stack([make_erow(np.asarray(ins["L3_attn_bv"][i], np.float32))
                             for i in range(4)])
    d["bq3"] = np.asarray(ins["L3_attn_bq"], np.float32).reshape(4, 1, 1024)
    d["bk3"] = np.asarray(ins["L3_attn_bk"], np.float32).reshape(4, 1, 1024)
    d["bo3"] = np.asarray(ins["L3_attn_bo"], np.float32).reshape(4, 1, 1024)
    d["w13"] = np.ascontiguousarray(
        np.asarray(ins["L3_attn_W1"], np.float32)
        .reshape(4, 8, 128, 4, 8, 128).transpose(0, 3, 4, 2, 1, 5)
        .reshape(4, 32, 128, 1024))
    d["w23"] = np.ascontiguousarray(
        np.asarray(ins["L3_attn_W2"], np.float32)
        .reshape(4, 4, 8, 128, 2, 512).transpose(0, 4, 1, 2, 3, 5)
        .reshape(4, 2, 32, 128, 512))
    d["b13"] = np.ascontiguousarray(
        np.asarray(ins["L3_attn_b1"], np.float32).reshape(4, 32, 128)
        .transpose(0, 2, 1))
    d["b23"] = np.asarray(ins["L3_attn_b2"], np.float32).reshape(4, 1, 1024)
    for nm, key in (("ln1_s", "ln1s"), ("ln1_b", "ln1b"),
                    ("ln2_s", "ln2s"), ("ln2_b", "l3n2b")):
        d[key] = np.ascontiguousarray(
            np.asarray(ins["L3_attn_" + nm], np.float32).reshape(4, DJ, 128)
            .transpose(0, 2, 1))
    d["lnfs"] = np.asarray(ins["L3_attn_lnf_s"], np.float32).reshape(1, 1024)
    d["lnfb"] = np.asarray(ins["L3_attn_lnf_b"], np.float32).reshape(1, 1024)
    half = 512
    freq = np.exp(np.arange(half, dtype=np.float32) * (-math.log(10000.0) / (half - 1)))
    pos = (np.arange(NT, dtype=np.float32) + 1.0)[:, None] * freq[None, :]
    d["pos_d"] = np.concatenate([np.sin(pos), np.cos(pos)], axis=-1).astype(np.float32)
    return d


L3ATTN_IO_SPECS = [("wq3", [4, DJ, 128, 1024]), ("wk3", [4, DJ, 128, 1024]),
                   ("wv3aug", [4, DJ, 128, 1040]), ("wo3", [4, DJ, 128, 1024]),
                   ("bv3erow", [4, 1, 1040]),
                   ("bq3", [4, 1, 1024]), ("bk3", [4, 1, 1024]),
                   ("bo3", [4, 1, 1024]),
                   ("w13", [4, 32, 128, 1024]), ("w23", [4, 2, 32, 128, 512]),
                   ("b13", [4, 128, 32]), ("b23", [4, 1, 1024]),
                   ("ln1s", [4, 128, DJ]), ("ln1b", [4, 128, DJ]),
                   ("ln2s", [4, 128, DJ]), ("l3n2b", [4, 128, DJ]),
                   ("lnfs", [1, 1024]), ("lnfb", [1, 1024]),
                   ("pos_d", [NT, D])]


def _dram_in(nc, name, shp, dt=F32):
    return nc.dram_tensor(name, list(shp), dt, kind="ExternalInput")


def _dram_out(nc, name, shp, dt=F32):
    return nc.dram_tensor(name, list(shp), dt, kind="ExternalOutput")


def _specs_in(nc, io, specs, dt=F32):
    for nm, shp in specs:
        io[nm] = _dram_in(nc, nm, shp, dt)


def _cd(nc):
    return dict(
        ident=_dram_in(nc, "ident", [128, 128], BF16),
        ident_f=_dram_in(nc, "ident_f", [128, 128], F32),
    )


def build_launch_A():
    nc = bass.Bass("TRN2", target_bir_lowering=False)
    stack = ExitStack()
    io = dict(
        res_in=_dram_in(nc, "res_in", [NT, D]),
        res_out=_dram_out(nc, "res_out", [NT, D]),
        qt=nc.dram_tensor("qt_scr", [16, 65, NT], BF16),
        kt=nc.dram_tensor("kt_scr", [16, 65, NT], BF16),
        counts_out=_dram_out(nc, "counts_out", [8, 1]),
    )
    _specs_in(nc, io, ATTN_IO_SPECS)
    _specs_in(nc, io, L0FFN_IO_SPECS)
    cd = _cd(nc)
    p = Prog(nc, stack)
    with stack:
        banks = make_banks(p)
        ar = Arena(p, 176 * 1024)
        emit_const_setup(p, cd)
        emit_attn_causal(p, ar, banks, io, "a0", scale=0.125)
        io2 = dict(io)
        io2["res_in"] = io["res_out"]
        for nm in ("a0rout0", "a0rout1", "a0pw", "a0ra"):
            p.wait("gpsimd", nm, p.val(nm))
        emit_l0_ffn(p, ar, banks, io2, "f0", mode="counts")
        p.finalize()
    return nc


def build_launch_B():
    nc = bass.Bass("TRN2", target_bir_lowering=False)
    stack = ExitStack()
    io = dict(
        res_in=_dram_in(nc, "res_in", [NT, D]),
        counts_all=_dram_in(nc, "counts_all", [8, 8]),
        res_mid=nc.dram_tensor("res_mid", [NT, D], F32),
        res_out=_dram_out(nc, "res_out", [NT, D]),
        k_d=_dram_out(nc, "k_d", [NTT, 128, 1024], BF16),
        v_d=_dram_out(nc, "v_d", [NTT, 128, 1024], BF16),
        q_d=_dram_out(nc, "q_d", [NTT, 128, 1024], BF16),
        hs_d=_dram_out(nc, "hs_d", [NT, D]),
        k1_d=nc.dram_tensor("k1_scr", [NTT, 128, 1024], BF16),
        v1_d=nc.dram_tensor("v1_scr", [NTT, 128, 1024], BF16),
        q1_d=nc.dram_tensor("q1_scr", [DJ, 128, NT], BF16),
    )
    _specs_in(nc, io, L0FFN_IO_SPECS)
    _specs_in(nc, io, L1ATTN_IO_SPECS)
    _specs_in(nc, io, L1FFN_PRE_SPECS)
    cd = _cd(nc)
    p = Prog(nc, stack)
    with stack:
        banks = make_banks(p)
        ar = Arena(p, 176 * 1024)
        emit_const_setup(p, cd)
        iof = dict(io)
        iof["res_out"] = io["res_mid"]
        emit_l0_ffn(p, ar, banks, iof, "f0", mode="full")
        ioa = dict(io)
        ioa["res_in"] = io["res_mid"]
        ioa["res_out"] = io["res_out"]
        ioa["k_d"] = io["k1_d"]
        ioa["v_d"] = io["v1_d"]
        ioa["q_d"] = io["q1_d"]
        for nm in ("f0ro", "f0em"):
            p.wait("gpsimd", nm, p.val(nm))
        emit_l1_attn(p, ar, banks, ioa, "a1")
        iop = dict(io)
        iop["res_in"] = io["res_out"]
        for nm in ("a1rout0", "a1rout1", "a1o_pe", "a1am", "a1ra"):
            p.wait("gpsimd", nm, p.val(nm))
        emit_l1_ffn_pre(p, ar, banks, iop, "p1")
        p.finalize()
    return nc


def build_launch_C():
    nc = bass.Bass("TRN2", target_bir_lowering=False)
    stack = ExitStack()
    io = dict(
        res_in=_dram_in(nc, "res_in", [NT, D]),
        q_d=_dram_in(nc, "q_d", [NTT, 128, 1024], BF16),
        hs_d=_dram_in(nc, "hs_d", [NT, D]),
        kall=_dram_in(nc, "kall", [8, NTT, 128, 1024], BF16),
        vall=_dram_in(nc, "vall", [8, NTT, 128, 1024], BF16),
        ha_d=nc.dram_tensor("ha_d", [NT, D], F32),
        res_mid=nc.dram_tensor("res_mid", [NT, D], F32),
        res_out=_dram_out(nc, "res_out", [NT, D]),
        qt=nc.dram_tensor("qt_scr", [16, 65, NT], BF16),
        kt=nc.dram_tensor("kt_scr", [16, 65, NT], BF16),
    )
    _specs_in(nc, io, L1FFN_POST_SPECS)
    _specs_in(nc, io, ATTN_IO_SPECS)
    cd = _cd(nc)
    p = Prog(nc, stack)
    with stack:
        banks = make_banks(p)
        ar = Arena(p, 176 * 1024)
        emit_const_setup(p, cd)
        iop = dict(io)
        iop["res_out"] = io["res_mid"]
        emit_l1_ffn_post(p, ar, banks, iop, "q1")
        ioa = dict(io)
        ioa["res_in"] = io["res_mid"]
        p.wait("sync", "q1ro", p.val("q1ro"))
        for nm in ("q1ro", "q1mm"):
            p.wait("gpsimd", nm, p.val(nm))
        emit_attn_causal(p, ar, banks, ioa, "a2", scale=0.125)
        p.finalize()
    return nc


def build_launch_D1():
    nc = bass.Bass("TRN2", target_bir_lowering=False)
    stack = ExitStack()
    io = dict(
        res_in=_dram_in(nc, "res_in", [NT, D]),
        res_out=_dram_out(nc, "res_out", [NT, D]),
    )
    _specs_in(nc, io, L2FFN_IO_SPECS)
    cd = _cd(nc)
    p = Prog(nc, stack)
    with stack:
        banks = make_banks(p)
        ar = Arena(p, 176 * 1024)
        emit_const_setup(p, cd)
        emit_l2_ffn(p, ar, banks, io, "f2")
        p.finalize()
    return nc


def build_launch_D2():
    nc = bass.Bass("TRN2", target_bir_lowering=False)
    stack = ExitStack()
    io = dict(
        res_in=_dram_in(nc, "res_in", [NT, D]),
        res_out=_dram_out(nc, "res_out", [NT, D]),
        x3_d=nc.dram_tensor("x3_d", [NT, D], F32),
        k3_d=nc.dram_tensor("k3_d", [NTT, 128, 1024], BF16),
        v3_d=nc.dram_tensor("v3_d", [NTT, 128, 1040], BF16),
        q3_d=nc.dram_tensor("q3_d", [16, 64, NT], BF16),
    )
    _specs_in(nc, io, L3ATTN_IO_SPECS)
    cd = _cd(nc)
    p = Prog(nc, stack)
    with stack:
        banks = make_banks(p)
        ar = Arena(p, 176 * 1024)
        emit_const_setup(p, cd)
        emit_l3_attn(p, ar, banks, io, "a3")
        p.finalize()
    return nc


def build_launch_E():
    nc = bass.Bass("TRN2", target_bir_lowering=False)
    stack = ExitStack()
    io = dict(
        res_in=_dram_in(nc, "res_in", [NT, D]),
        res_out=_dram_out(nc, "res_out", [NT, D]),
    )
    _specs_in(nc, io, L3FFN_IO_SPECS)
    cd = _cd(nc)
    p = Prog(nc, stack)
    with stack:
        banks = make_banks(p)
        ar = Arena(p, 176 * 1024)
        emit_const_setup(p, cd)
        emit_l3_ffn(p, ar, banks, io, "f3")
        p.finalize()
    return nc


def build_launch_CDE(parts=("C", "D1", "D2", "E")):
    nc = bass.Bass("TRN2", target_bir_lowering=False)
    stack = ExitStack()
    io = dict(
        res_in=_dram_in(nc, "res_in", [NT, D]),
        q_d=_dram_in(nc, "q_d", [NTT, 128, 1024], BF16),
        hs_d=_dram_in(nc, "hs_d", [NT, D]),
        kall=_dram_in(nc, "kall", [8, NTT, 128, 1024], BF16),
        vall=_dram_in(nc, "vall", [8, NTT, 128, 1024], BF16),
        ha_d=nc.dram_tensor("ha_d", [NT, D], F32),
        res_mid=nc.dram_tensor("res_mid", [NT, D], F32),
        r5=nc.dram_tensor("r5", [NT, D], F32),
        r6=nc.dram_tensor("r6", [NT, D], F32),
        r7=nc.dram_tensor("r7", [NT, D], F32),
        res_out=_dram_out(nc, "res_out", [NT, D]),
        qt=nc.dram_tensor("qt_scr", [16, 65, NT], BF16),
        kt=nc.dram_tensor("kt_scr", [16, 65, NT], BF16),
        x3_d=nc.dram_tensor("x3_d", [NT, D], F32),
        k3_d=nc.dram_tensor("k3_d", [NTT, 128, 1024], BF16),
        v3_d=nc.dram_tensor("v3_d", [NTT, 128, 1040], BF16),
        q3_d=nc.dram_tensor("q3_d", [16, 64, NT], BF16),
    )
    for specs in (L1FFN_POST_SPECS, ATTN_IO_SPECS, L2FFN_IO_SPECS,
                  L3ATTN_IO_SPECS, L3FFN_IO_SPECS):
        _specs_in(nc, io, specs)
    cd = _cd(nc)
    p = Prog(nc, stack)
    with stack:
        banks = make_banks(p)
        ar = Arena(p, 176 * 1024)
        emit_const_setup(p, cd)
        for b in range(8):
            p._phys(p.bk(b))
        for nm in ("cst", "cst2", "dma_sync", "dma_gpsimd"):
            p._phys(nm)
        p.pool_start()
        chain = {"C": io["res_mid"], "D1": io["r6"], "D2": io["r7"],
                 "E": io["res_out"]}
        cur_in = io["res_in"]
        first = True
        for part in parts:
            if not first:
                p.new_epoch()
            first = False
            out = chain[part] if part != parts[-1] else io["res_out"]
            if part == "C":
                iop = dict(io)
                iop["res_in"] = cur_in
                iop["res_out"] = io["res_mid"]
                emit_l1_ffn_post(p, ar, banks, iop, "q1")
                ioa = dict(io)
                ioa["res_in"] = io["res_mid"]
                ioa["res_out"] = out if out is not io["res_mid"] else io["r5"]
                p.wait("sync", "q1ro", p.val("q1ro"))
                for nm in ("q1ro", "q1mm"):
                    p.wait("gpsimd", nm, p.val(nm))
                emit_attn_causal(p, ar, banks, ioa, "a2", scale=0.125)
                cur_in = ioa["res_out"]
            elif part == "D1":
                iof = dict(io)
                iof["res_in"] = cur_in
                iof["res_out"] = out
                emit_l2_ffn(p, ar, banks, iof, "f2")
                cur_in = out
            elif part == "D2":
                iog = dict(io)
                iog["res_in"] = cur_in
                iog["res_out"] = out
                emit_l3_attn(p, ar, banks, iog, "a3")
                cur_in = out
            elif part == "E":
                ioh = dict(io)
                ioh["res_in"] = cur_in
                emit_l3_ffn(p, ar, banks, ioh, "f3")
        p.finalize()
    return nc


_BUILD_CACHE = {}


def _get(name, fn):
    if name not in _BUILD_CACHE:
        _BUILD_CACHE[name] = fn()
    return _BUILD_CACHE[name]


def kernel(**inputs):
    ins = {k: np.asarray(v) for k, v in inputs.items()}
    x = np.asarray(ins["L0_attn_x"], np.float32)
    cst = const_inputs()
    w_l0a = prep_l0_attn(ins)
    w_l0f = [prep_l0_ffn(ins, c) for c in range(NCORES)]
    w_l1a = prep_l1_attn(ins)
    w_l1fpre = prep_l1_ffn_pre(ins)
    w_l1fpost = [prep_l1_ffn_post(ins, c) for c in range(NCORES)]
    w_l2a = prep_l2_attn(ins)
    w_l2f = prep_l2_ffn(ins)
    w_l3a = prep_l3_attn(ins)
    w_l3f = prep_l3_ffn(ins)

    ncA = _get("A", build_launch_A)
    in_maps = [dict(res_in=np.ascontiguousarray(x[c]), **w_l0a, **w_l0f[c],
                    **cst) for c in range(NCORES)]
    rA = run_bass_kernel_spmd(ncA, in_maps, core_ids=list(range(NCORES)))
    res1 = [np.asarray(rA.results[c]["res_out"]) for c in range(NCORES)]
    counts = [np.asarray(rA.results[c]["counts_out"]).reshape(8)
              for c in range(NCORES)]
    counts_all = np.ascontiguousarray(np.stack(counts)).astype(np.float32)

    ncB = _get("B", build_launch_B)
    in_maps = [dict(res_in=res1[c], counts_all=counts_all, **w_l0f[c],
                    **w_l1a, **w_l1fpre, **cst) for c in range(NCORES)]
    rB = run_bass_kernel_spmd(ncB, in_maps, core_ids=list(range(NCORES)))
    res3 = [np.asarray(rB.results[c]["res_out"]) for c in range(NCORES)]
    hs = [np.asarray(rB.results[c]["hs_d"]) for c in range(NCORES)]
    qd = [np.asarray(rB.results[c]["q_d"]) for c in range(NCORES)]
    kall = np.stack([np.asarray(rB.results[c]["k_d"]) for c in range(NCORES)])
    vall = np.stack([np.asarray(rB.results[c]["v_d"]) for c in range(NCORES)])

    ncC = _get("CDE", build_launch_CDE)
    in_maps = [dict(res_in=res3[c], q_d=qd[c], hs_d=hs[c], kall=kall,
                    vall=vall, **w_l1fpost[c], **w_l2a, **w_l2f, **w_l3a,
                    **w_l3f, **cst) for c in range(NCORES)]
    rC = run_bass_kernel_spmd(ncC, in_maps, core_ids=list(range(NCORES)))
    out = np.stack([np.asarray(rC.results[c]["res_out"], np.float32)
                    for c in range(NCORES)])
    return out


# version 05 (final)
# speedup vs baseline: 1.0297x; 1.0297x over previous
import numpy as np
import ml_dtypes
from contextlib import ExitStack

import concourse.bass as bass
import concourse.mybir as mybir
from concourse.bass_utils import run_bass_kernel_spmd
from concourse.alu_op_type import AluOpType as ALU

F32 = mybir.dt.float32
BF16 = mybir.dt.bfloat16
U8 = mybir.dt.uint8
AF = mybir.ActivationFunctionType
AX = mybir.AxisListType
BF = ml_dtypes.bfloat16

NCORES = 8
NT = 4096
D = 1024
NTT = NT // 128
DJ = D // 128

ENGS = ("tensor", "vector", "scalar", "gpsimd", "sync")


class Prog:
    def __init__(self, nc, stack):
        self.nc = nc
        self.stack = stack
        self.stream = {e: [] for e in ENGS}
        self.sems = {}
        self.semval = {}
        self.uid = 0
        self.c = {}
        self._consts = {}
        self._cpool = None
        self.alias = {}
        self.base = {}
        self._pool = {"c": [], "s": [], "g": []}
        self._pool_i = {"c": 0, "s": 0, "g": 0}
        self._pooling = False

    def _phys(self, name, cls=None):
        ph = self.alias.get(name)
        if ph is not None:
            return ph
        if not self._pooling:
            self.alias[name] = name
            self.base[name] = 0
            return name
        if cls is None:
            return None
        pool, i = self._pool[cls], self._pool_i[cls]
        if i < len(pool):
            ph = pool[i]
        else:
            ph = "z%s%d" % (cls, len(pool))
            pool.append(ph)
            self.alias[ph] = ph
            self.base[ph] = 0
        self._pool_i[cls] += 1
        self.alias[name] = ph
        self.base[name] = self.semval.get(ph, 0)
        return ph

    def pool_start(self):
        self._pooling = True

    def new_epoch(self):
        for e in ENGS:
            for name in sorted(self.semval):
                v = self.semval[name]
                if v > 0:
                    h = self.sems[name]
                    self.stream[e].append(
                        lambda en, h=h, v=v: en.wait_ge(h, v))
        self.alias = {k: v for k, v in self.alias.items() if k == v}
        self.base = {k: v for k, v in self.base.items() if k in self.alias}
        self._pool_i = {"c": 0, "s": 0, "g": 0}

    def sem(self, name, cls="c"):
        name = self._phys(name, cls)
        if name not in self.sems:
            self.sems[name] = self.stack.enter_context(self.nc.semaphore(name))
            self.semval[name] = 0
        return self.sems[name]

    def val(self, name):
        ph = self.alias.get(name)
        if ph is None:
            return 0
        return self.semval.get(ph, 0) - self.base[name]

    def do(self, eng, fn, inc=None):
        if inc is None:
            self.stream[eng].append(fn)
            return None
        name, k = inc
        ph = self._phys(name, "c")
        h = self.sem(ph)
        self.semval[ph] += k
        v = self.semval[ph] - self.base[name]

        def wrapped(e, fn=fn, h=h, k=k):
            fn(e).then_inc(h, k)

        self.stream[eng].append(wrapped)
        return v

    def wait(self, eng, name, val):
        if val <= 0:
            return
        ph = self.alias.get(name)
        assert ph is not None, f"wait on unbound pooled sem {name}"
        val = val + self.base[name]
        assert ph in self.semval and val <= self.semval[ph], \
            f"wait on future {name}={val} (posted {self.semval.get(ph)})"
        h = self.sems[ph]
        self.stream[eng].append(lambda e, h=h, v=val: e.wait_ge(h, v))

    def bk(self, b):
        return "bk%d" % b

    def bank_acquire(self, eng, b):
        self.wait(eng, self.bk(b), self.val(self.bk(b)))

    def dma(self, q, out, in_, sem=None):
        sem = sem or ("dma_" + q)
        self.sem(sem, "g" if q == "gpsimd" else "s")
        return self.do(q, lambda e: e.dma_start(out=out, in_=in_), inc=(sem, 16))

    def sb(self, name, shape, dtype):
        self.uid += 1
        return self.stack.enter_context(
            self.nc.sbuf_tensor(f"{name}_{self.uid}", shape, dtype))

    def ps(self, name, shape, dtype=F32):
        self.uid += 1
        return self.stack.enter_context(
            self.nc.psum_tensor(f"{name}_{self.uid}", shape, dtype))

    def constf(self, value, dtype=F32):
        key = (float(value), dtype)
        if key not in self._consts:
            if self._cpool is None:
                self._cpool = self.sb("cpool", [128, 64], F32)
            idx = len(self._consts)
            assert idx < 64
            ap = self._cpool[:, idx:idx + 1]
            if dtype != F32:
                ap = self._cpool[:, idx:idx + 1].bitcast(dtype)[:, 0:1]
            self._consts[key] = (ap, float(value))
        return self._consts[key][0]

    def finalize(self):
        if self._consts:
            h = self.sem("cst2")
            items = list(self._consts.values())
            pre = []
            for n, (ap, val) in enumerate(items):
                if n == len(items) - 1:
                    pre.append(lambda e, ap=ap, val=val, h=h:
                               e.memset(ap, val).then_inc(h, 1))
                else:
                    pre.append(lambda e, ap=ap, val=val: e.memset(ap, val))
            self.stream["gpsimd"] = pre + self.stream["gpsimd"]
            for e in ENGS:
                if e != "gpsimd" and self.stream[e]:
                    self.stream[e] = [lambda en, h=h: en.wait_ge(h, 1)] + self.stream[e]
        with self.nc.Block() as block:
            for e in ENGS:
                ops = self.stream[e]
                if ops:
                    getattr(block, e)(lambda eng, ops=ops: [op(eng) for op in ops])


class Arena:

    def __init__(self, p, bytes_pp):
        self.buf = p.sb("arena", [128, bytes_pp], U8)
        self.cap = bytes_pp
        self.off = 0
        self.peak = 0

    def alloc(self, cols, dtype, parts=128):
        nb = cols * mybir.dt.size(dtype)
        nbal = (nb + 31) // 32 * 32
        assert self.off + nbal <= self.cap, \
            f"arena overflow {self.off}+{nbal}>{self.cap}"
        o = self.off
        self.off += nbal
        self.peak = max(self.peak, self.off)
        v = self.buf[:, o:o + nb].bitcast(dtype)
        if parts != 128:
            v = v[0:parts, :]
        return v

    def mark(self):
        return self.off

    def reset(self, mark):
        self.off = mark


def cdiv(a, b):
    return -(-a // b)


def make_banks(p):
    return [p.ps(f"bank{i}", [128, 512], F32) for i in range(8)]


def emit_const_setup(p, cd):
    p.c["ident"] = p.sb("ident", [128, 128], BF16)
    p.c["ident_f"] = p.sb("identf", [128, 128], F32)
    p.dma("sync", p.c["ident"][:, :], cd["ident"][:, :], "cst")
    p.dma("sync", p.c["ident_f"][:, :], cd["ident_f"][:, :], "cst")
    ob = p.sb("ones_bf", [1, 512], BF16)
    of = p.sb("ones_f", [1, 512], F32)
    oc = p.sb("ones_col", [128, 1], BF16)
    p.do("gpsimd", lambda e: e.memset(ob[:, :], 1.0), inc=("cst", 1))
    p.do("gpsimd", lambda e: e.memset(of[:, :], 1.0), inc=("cst", 1))
    p.do("gpsimd", lambda e: e.memset(oc[:, :], 1.0), inc=("cst", 1))
    p.c["ones_bf"] = ob
    p.c["ones_f"] = of
    p.c["ones_col"] = oc
    v = p.val("cst")
    for e in ENGS:
        p.wait(e, "cst", v)


def emit_ln_tiles(p, ar, src_ap_fn, n_tiles, consume_cb, eps=1e-5, pfx="ln"):
    s_dv = pfx + "_dv"
    xts = [ar.alloc(D, F32) for _ in range(3)]
    sts = [ar.alloc(16, F32) for _ in range(3)]
    dv0 = p.val(s_dv)
    fins = {}
    for i in range(n_tiles):
        xt, st = xts[i % 3], sts[i % 3]
        s_ld = pfx + "_ld%d" % (i % 3)
        if i >= 3:
            for sem, val in fins[i - 3]:
                p.wait("sync", sem, val)
        p.dma("sync", xt[:, :], src_ap_fn(i), s_ld)
        p.wait("vector", s_ld, p.val(s_ld))
        if i >= 3:
            for sem, val in fins[i - 3]:
                p.wait("vector", sem, val)
        p.do("vector", lambda e, xt=xt, st=st: e.bn_stats(st[:, 4:10], xt[:, 0:512]),
             inc=(s_dv, 1))
        p.do("vector", lambda e, xt=xt, st=st: e.bn_stats(st[:, 10:16], xt[:, 512:1024]),
             inc=(s_dv, 1))
        p.wait("vector", s_dv, p.val(s_dv))
        p.do("vector", lambda e, st=st: e.bn_aggr(st[:, 0:2], st[:, 4:16]),
             inc=(s_dv, 1))
        p.wait("scalar", s_dv, p.val(s_dv))
        epsc = p.constf(eps)
        p.do("scalar", lambda e, st=st, epsc=epsc: e.activation(
            st[:, 2:3], st[:, 1:2], AF.Sqrt, bias=epsc, scale=1.0),
            inc=(s_dv, 1))
        p.wait("vector", s_dv, p.val(s_dv))
        p.do("vector", lambda e, st=st: e.reciprocal(st[:, 2:3], st[:, 2:3]),
             inc=(s_dv, 1))
        p.wait("vector", s_dv, p.val(s_dv))
        p.do("vector", lambda e, st=st: e.scalar_tensor_tensor(
            st[:, 3:4], st[:, 0:1], -1.0, st[:, 2:3], ALU.mult, ALU.mult),
            inc=(s_dv, 1))
        fins[i] = consume_cb(i, xt, st, (s_dv, p.val(s_dv)))


def emit_ln_to_fm(p, ar, banks, src_ap_fn, dst_cb, n_tiles=NTT, eps=1e-5,
                  scale_bias=None, extra_cb=None, pfx="lnt", bank_ids=(6, 7)):
    nbs = [ar.alloc(D, BF16) for _ in range(2)]
    pts = [banks[b][:, 0:64].bitcast(BF16) for b in bank_ids]
    s_nm, s_tr, s_cp = pfx + "_nm", pfx + "_tr", pfx + "_cp"
    nm0, tr0, cp0 = p.val(s_nm), p.val(s_tr), p.val(s_cp)
    p.sem(s_cp)

    def consume2(i, xt, st, st_ready):
        nbi = nbs[i % 2]
        p.wait("vector", st_ready[0], st_ready[1])
        if i >= 2:
            p.wait("vector", s_tr, tr0 + DJ * (i - 1))
        p.do("vector", lambda e, xt=xt, st=st, nbi=nbi: e.tensor_scalar(
            nbi[:, :], xt[:, :], st[:, 0:1], st[:, 2:3], ALU.subtract, ALU.mult),
            inc=(s_nm, 1))
        fins = [(s_nm, p.val(s_nm))]
        if extra_cb is not None:
            fins += extra_cb(i, xt, st, st_ready)
        p.wait("tensor", s_nm, p.val(s_nm))
        for j in range(DJ):
            idx = i * DJ + j
            bid = bank_ids[idx % 2]
            pb = pts[idx % 2]
            p.bank_acquire("tensor", bid)
            p.do("tensor", lambda e, pb=pb, nbi=nbi, j=j: e.transpose(
                pb, nbi[:, j * 128:(j + 1) * 128], p.c["ident"][:, :]),
                inc=(s_tr, 1))
            p.wait("vector", s_tr, tr0 + idx + 1)
            dst = dst_cb(i, j)
            rel = (p.bk(bid), 1)
            if scale_bias is None:
                p.do("vector", lambda e, pb=pb, dst=dst: e.tensor_copy(dst, pb),
                     inc=rel)
            else:
                w_sb, b_sb = scale_bias
                p.do("vector", lambda e, pb=pb, dst=dst, j=j:
                     e.tensor_scalar(dst, pb, w_sb[:, j:j + 1], b_sb[:, j:j + 1],
                                     ALU.mult, ALU.add), inc=rel)
        return fins

    emit_ln_tiles(p, ar, src_ap_fn, n_tiles, consume2, eps=eps, pfx=pfx)
    return [(p.bk(bank_ids[0]), p.val(p.bk(bank_ids[0]))),
            (p.bk(bank_ids[1]), p.val(p.bk(bank_ids[1])))]


def emit_proj_fm(p, xn_t, w_sb, out_cb, banks, bank_ids, pfx,
                 n_out_tiles=8, tok_chunks=8, brow=None, kdj=DJ, tok_w=512,
                 xn_nt=NT, w_stride=None):
    s_pe = pfx + "_pe"
    W = w_stride if w_stride is not None else n_out_tiles * 128
    n = 0
    for oc in range(n_out_tiles):
        for tc in range(tok_chunks):
            bid = bank_ids[n % len(bank_ids)]
            pb = banks[bid]
            p.bank_acquire("tensor", bid)
            for j in range(kdj):
                last = (j == kdj - 1 and brow is None)
                p.do("tensor", lambda e, pb=pb, j=j, oc=oc, tc=tc, last=last:
                     e.matmul(
                         pb[:, 0:tok_w],
                         w_sb[:, j * W + oc * 128: j * W + oc * 128 + 128],
                         xn_t[:, j * xn_nt + tc * tok_w: j * xn_nt + (tc + 1) * tok_w],
                         start=(j == 0), stop=last),
                     inc=(s_pe, 1) if last else None)
            if brow is not None:
                p.do("tensor", lambda e, pb=pb, oc=oc: e.matmul(
                    pb[:, 0:tok_w], brow[0:1, oc * 128:oc * 128 + 128],
                    p.c["ones_bf"][0:1, 0:tok_w], start=False, stop=True),
                    inc=(s_pe, 1))
            v = p.val(s_pe)
            out_cb(oc, tc, pb, (s_pe, v), (p.bk(bid), 1))
            n += 1


def emit_proj_tm(p, xn_t, w_sb, out_cb, banks, bank_ids, pfx,
                 out_w=1024, erow=None, kdj=DJ, n_tok_tiles=NTT, xn_nt=NT,
                 tiles=None, w_stride=None):
    s_pe = pfx + "_pe"
    wst = w_stride if w_stride is not None else out_w
    ncol = cdiv(out_w, 512)
    ngrp = len(bank_ids) // ncol
    assert ngrp >= 2
    if tiles is None:
        tiles = range(n_tok_tiles)
    for idx, i in enumerate(tiles):
        g = idx % ngrp
        pbs = []
        for k in range(ncol):
            w0, w1 = k * 512, min(out_w, (k + 1) * 512)
            bid = bank_ids[g * ncol + k]
            pb = banks[bid]
            p.bank_acquire("tensor", bid)
            pbs.append((pb, w0, w1, (p.bk(bid), 1)))
            for j in range(kdj):
                last = (j == kdj - 1 and erow is None and k == ncol - 1)
                p.do("tensor", lambda e, pb=pb, j=j, i=i, w0=w0, w1=w1,
                     last2=(j == kdj - 1 and erow is None): e.matmul(
                         pb[:, 0:w1 - w0],
                         xn_t[:, j * xn_nt + i * 128: j * xn_nt + i * 128 + 128],
                         w_sb[:, j * wst + w0: j * wst + w1],
                         start=(j == 0), stop=last2),
                     inc=(s_pe, 1) if last else None)
            if erow is not None:
                p.do("tensor", lambda e, pb=pb, w0=w0, w1=w1: e.matmul(
                    pb[:, 0:w1 - w0], p.c["ones_bf"][0:1, 0:128],
                    erow[0:1, w0:w1], start=False, stop=True),
                    inc=(s_pe, 1) if k == ncol - 1 else None)
        v = p.val(s_pe)
        out_cb(i, pbs, (s_pe, v))


def emit_attn_causal(p, ar, banks, io, pfx, scale=0.125, stop_after=None):
    H = 16
    m0 = ar.mark()
    v_sb = ar.alloc(NTT * 1040, BF16)
    msk = ar.alloc(4 * 512, BF16)
    for m in range(4):
        p.dma("gpsimd", msk[:, m * 512:(m + 1) * 512], io["masks"][m], pfx + "wm")

    m1 = ar.mark()
    xn_t = ar.alloc(DJ * NT, BF16)
    m_ln = ar.mark()
    xn_specs = emit_ln_to_fm(
        p, ar, banks, lambda i: io["res_in"][i * 128:(i + 1) * 128, :],
        lambda i, j: xn_t[:, j * NT + i * 128: j * NT + i * 128 + 128],
        pfx=pfx + "ln")
    ar.reset(m_ln)
    wqk = ar.alloc(DJ * 1024, BF16)
    wv = ar.alloc(DJ * 1040, BF16)
    ero = ar.alloc(1040, BF16, parts=1)
    bro = ar.alloc(1024, BF16, parts=1)
    bq4 = [ar.alloc(512, BF16) for _ in range(4)]

    for sem, vv in xn_specs:
        p.wait("gpsimd", sem, vv)
    for j in range(DJ):
        p.dma("gpsimd", wqk[:, j * 1024:(j + 1) * 1024], io["wq_t"][j], pfx + "wq")
    p.dma("gpsimd", bro[:, :], io["qb_row"][:, :], pfx + "wq")
    wq_rdy = p.val(pfx + "wq")

    for sem, vv in xn_specs:
        p.wait("tensor", sem, vv)
    p.wait("tensor", pfx + "wq", wq_rdy)

    nst = [0]

    def qk_out(dst):
        def cb(oc, tc, pb, pe_ready, rel):
            k = nst[0]
            sl = bq4[k % 4]
            s_sl = pfx + "qs%d" % (k % 4)
            if k >= 4:
                p.wait("scalar", s_sl, p.val(s_sl))
            p.wait("scalar", pe_ready[0], pe_ready[1])
            p.do("scalar", lambda e, pb=pb, sl=sl: e.copy(sl[:, :], pb[:, :]),
                 inc=rel)
            p.wait("sync", rel[0], p.val(rel[0]))
            p.dma("sync", dst[2 * oc, 0:64, tc * 512:(tc + 1) * 512],
                  sl[0:64, :], s_sl)
            p.dma("sync", dst[2 * oc + 1, 0:64, tc * 512:(tc + 1) * 512],
                  sl[64:128, :], s_sl)
            nst[0] += 1
        return cb

    emit_proj_fm(p, xn_t, wqk, qk_out(io["qt"]), banks, (0, 1), pfx + "q",
                 brow=bro)
    p.wait("gpsimd", pfx + "q_pe", p.val(pfx + "q_pe"))
    for j in range(DJ):
        p.dma("gpsimd", wqk[:, j * 1024:(j + 1) * 1024], io["wk_t"][j], pfx + "wk")
    p.dma("gpsimd", bro[:, :], io["kb_row"][:, :], pfx + "wk")
    wk_rdy = p.val(pfx + "wk")
    p.wait("tensor", pfx + "wk", wk_rdy)
    emit_proj_fm(p, xn_t, wqk, qk_out(io["kt"]), banks, (0, 1), pfx + "k",
                 brow=bro)
    for h in range(H):
        for cch in range(8):
            p.dma("sync", io["kt"][h, 64:65, cch * 512:(cch + 1) * 512],
                  p.c["ones_bf"][0:1, 0:512], pfx + "qso")

    p.wait("gpsimd", pfx + "k_pe", p.val(pfx + "k_pe"))
    for j in range(DJ):
        p.dma("gpsimd", wv[:, j * 1040:(j + 1) * 1040], io["wv_aug"][j], pfx + "wv")
    p.dma("gpsimd", ero[:, :], io["erow_v"][:, :], pfx + "wv")
    wv_rdy = p.val(pfx + "wv")
    p.wait("tensor", pfx + "wv", wv_rdy)

    def v_out(i, pbs, pe_ready):
        p.wait("vector", pe_ready[0], pe_ready[1])
        for n, (pb, w0, w1, rel) in enumerate(pbs):
            p.do("vector", lambda e, pb=pb, i=i, w0=w0, w1=w1: e.tensor_copy(
                v_sb[:, i * 1040 + w0: i * 1040 + w1], pb[:, 0:w1 - w0]),
                inc=rel)

    emit_proj_tm(p, xn_t, wv, v_out, banks, (2, 3, 4, 5, 6, 7), pfx + "v",
                 out_w=1040, erow=ero)
    v_fin = [(p.bk(b), p.val(p.bk(b))) for b in (2, 3, 4, 5, 6, 7)]
    if stop_after == "qkv":
        for kk in range(4):
            sname = pfx + "qs%d" % kk
            p.wait("sync", sname, p.val(sname))
        p.wait("sync", pfx + "qso", p.val(pfx + "qso"))
        for sem, vv in v_fin:
            p.wait("sync", sem, vv)
        return

    ar.reset(m1)
    m2 = ar.mark()
    qh = [ar.alloc(NT, BF16) for _ in range(2)]
    kh = [ar.alloc(NT, BF16) for _ in range(2)]
    mx = ar.alloc(NTT, F32)
    mxc = ar.alloc(8, F32)
    mxs = ar.alloc(128, BF16)
    s_pa = pfx + "pa"
    s_pam = pfx + "pam"
    s_mx = pfx + "mxv"
    s_mt = pfx + "mtr"
    s_mc = pfx + "mcp"
    s_pad = pfx + "pad"
    p.sem(s_pad)
    for kk in range(4):
        sname = pfx + "qs%d" % kk
        p.wait("sync", sname, p.val(sname))
    p.wait("sync", pfx + "qso", p.val(pfx + "qso"))
    p.wait("sync", pfx + "v_pe", p.val(pfx + "v_pe"))
    for sem, vv in v_fin:
        p.wait("sync", sem, vv)
    for h in range(H):
        qsl, ksl = qh[h % 2], kh[h % 2]
        s_hl = pfx + "hl%d" % (h % 2)
        if h >= 2:
            p.wait("sync", s_pad, p.val(s_pad))
        p.dma("sync", qsl[0:64, :], io["qt"][h, 0:64, :], s_hl)
        p.dma("sync", ksl[0:64, :], io["kt"][h, 0:64, :], s_hl)
        p.wait("tensor", s_hl, p.val(s_hl))
        nk = [0]
        s_cmb = pfx + "cmb"
        for qt_i in range(NTT):
            kcmax = (qt_i * 128 + 127) // 512 + 1
            p.wait("vector", s_cmb, p.val(s_cmb))
            for kc in range(kcmax):
                bid = nk[0] % 2
                pb = banks[bid]
                p.bank_acquire("tensor", bid)
                p.do("tensor", lambda e, pb=pb, qsl=qsl, ksl=ksl, qt_i=qt_i,
                     kc=kc: e.matmul(
                         pb[:, :], qsl[0:64, qt_i * 128:qt_i * 128 + 128],
                         ksl[0:64, kc * 512:(kc + 1) * 512],
                         start=True, stop=True), inc=(s_pam, 1))
                p.wait("vector", s_pam, p.val(s_pam))
                p.do("vector", lambda e, pb=pb, kc=kc: e.tensor_reduce(
                    mxc[:, kc:kc + 1], pb[:, :], AX.X, ALU.max),
                    inc=(p.bk(bid), 1))
                nk[0] += 1
            p.wait("vector", p.bk(0), p.val(p.bk(0)))
            p.wait("vector", p.bk(1), p.val(p.bk(1)))
            p.do("vector", lambda e, qt_i=qt_i, kcmax=kcmax: e.tensor_reduce(
                mx[:, qt_i:qt_i + 1], mxc[:, 0:kcmax], AX.X, ALU.max),
                inc=(s_cmb, 1))
        p.wait("vector", s_cmb, p.val(s_cmb))
        p.do("vector", lambda e: e.tensor_scalar_mul(mx[:, :], mx[:, :], -1.0),
             inc=(s_mx, 1))
        p.wait("tensor", s_mx, p.val(s_mx))
        p.bank_acquire("tensor", 2)
        p.do("tensor", lambda e: e.transpose(
            banks[2][0:32, 0:128], mx[:, :], p.c["ident_f"][:, :]),
            inc=(s_mt, 1))
        p.wait("scalar", s_mt, p.val(s_mt))
        p.wait("scalar", s_pad, p.val(s_pad))
        p.do("scalar", lambda e: e.copy(mxs[0:32, :], banks[2][0:32, 0:128]),
             inc=(p.bk(2), 1))
        p.wait("sync", p.bk(2), p.val(p.bk(2)))
        p.dma("sync",
              io["qt"][h, 64:65, :].rearrange("o (t q) -> (o t) q", q=128),
              mxs[0:32, :], s_pad)
    passa_done = p.val(s_pad)
    if stop_after == "passa":
        p.wait("sync", s_pad, passa_done)
        return

    ar.reset(m2)
    ctx_sb = ar.alloc(DJ * NT, BF16)
    m3 = ar.mark()
    qh2 = [ar.alloc(NT, BF16) for _ in range(2)]
    kh2 = [ar.alloc(NT, BF16) for _ in range(2)]
    psl = [ar.alloc(512, BF16) for _ in range(3)]
    tmp64 = ar.alloc(512, BF16, parts=64)
    rl = ar.alloc(512, F32, parts=1)
    bl = ar.alloc(512, F32, parts=64)
    s_smm = pfx + "smm"
    s_sx = pfx + "sx"
    s_cm = pfx + "cm"
    s_ce = pfx + "ce"
    s_od = pfx + "od"
    p.sem(s_ce)
    p.sem(s_od)
    p.wait("sync", s_pad, passa_done)
    for sem, vv in v_fin:
        p.wait("tensor", sem, vv)
    p.wait("vector", pfx + "wm", p.val(pfx + "wm"))
    for h in range(H):
        qsl, ksl = qh2[h % 2], kh2[h % 2]
        s_h2 = pfx + "h2%d" % (h % 2)
        if h >= 2:
            p.wait("sync", s_cm, p.val(s_cm))
        p.dma("sync", qsl[0:65, :], io["qt"][h, 0:65, :], s_h2)
        p.dma("sync", ksl[0:65, :], io["kt"][h, 0:65, :], s_h2)
        p.wait("tensor", s_h2, p.val(s_h2))
        for qc in range(8):
            nkb = 4 * qc + 4
            exp_done = {}
            p.bank_acquire("tensor", 4)
            for kb in range(nkb + 1):
                if kb < nkb:
                    bid = kb % 2
                    pb = banks[bid]
                    p.bank_acquire("tensor", bid)
                    p.do("tensor", lambda e, pb=pb, qsl=qsl, ksl=ksl, kb=kb,
                         qc=qc: e.matmul(
                             pb[:, :], ksl[0:65, kb * 128:kb * 128 + 128],
                             qsl[0:65, qc * 512:(qc + 1) * 512],
                             start=True, stop=True), inc=(s_smm, 1))
                    sl = psl[kb % 3]
                    p.wait("scalar", s_smm, p.val(s_smm))
                    if kb >= 3:
                        p.wait("scalar", s_cm, p.val(s_cm))
                    scc = p.constf(scale)
                    p.do("scalar", lambda e, pb=pb, sl=sl, scc=scc: e.activation(
                        sl[:, :], pb[:, :], AF.Exp, scale=scc),
                        inc=(p.bk(bid), 1))
                    dj = kb - 4 * qc
                    if dj >= 0:
                        p.wait("vector", p.bk(bid), p.val(p.bk(bid)))
                        p.do("vector", lambda e, sl=sl, dj=dj: e.tensor_tensor(
                            sl[:, :], sl[:, :], msk[:, dj * 512:(dj + 1) * 512],
                            ALU.mult), inc=(s_sx, 1))
                        exp_done[kb] = (s_sx, p.val(s_sx))
                    else:
                        exp_done[kb] = (p.bk(bid), p.val(p.bk(bid)))
                if kb >= 1:
                    kbp = kb - 1
                    slp = psl[kbp % 3]
                    p.wait("tensor", exp_done[kbp][0], exp_done[kbp][1])
                    p.do("tensor", lambda e, slp=slp, kbp=kbp, h=h, qc=qc,
                         nkb=nkb: e.matmul(
                             banks[4][0:65, :],
                             v_sb[:, kbp * 1040 + h * 65: kbp * 1040 + h * 65 + 65],
                             slp[:, :], start=(kbp == 0), stop=(kbp == nkb - 1)),
                         inc=(s_cm, 1))
            p.wait("vector", s_cm, p.val(s_cm))
            p.wait("vector", pfx + "blm", p.val(pfx + "blm"))
            p.do("vector", lambda e: e.reciprocal(rl[:, :], banks[4][64:65, :]),
                 inc=(pfx + "rl", 1))
            p.wait("tensor", pfx + "rl", p.val(pfx + "rl"))
            p.bank_acquire("tensor", 5)
            p.do("tensor", lambda e: e.matmul(
                banks[5][0:64, :], p.c["ones_f"][0:1, 0:64], rl[:, :],
                start=True, stop=True), inc=(pfx + "blm", 1))
            p.wait("scalar", pfx + "blm", p.val(pfx + "blm"))
            p.wait("scalar", p.bk(4), p.val(p.bk(4)))
            p.do("scalar", lambda e: e.copy(bl[:, :], banks[5][0:64, :]),
                 inc=(p.bk(5), 1))
            p.wait("vector", p.bk(5), p.val(p.bk(5)))
            col0 = (h // 2) * NT + qc * 512
            if h % 2 == 0:
                p.do("vector", lambda e, col0=col0: e.tensor_tensor(
                    ctx_sb[0:64, col0:col0 + 512], banks[4][0:64, :],
                    bl[:, :], ALU.mult), inc=(p.bk(4), 1))
            else:
                if qc >= 1 or h >= 2:
                    p.wait("vector", s_od, p.val(s_od))
                p.do("vector", lambda e: e.tensor_tensor(
                    tmp64[:, :], banks[4][0:64, :], bl[:, :], ALU.mult),
                    inc=(p.bk(4), 1))
                p.wait("sync", p.bk(4), p.val(p.bk(4)))
                p.dma("sync", ctx_sb[64:128, col0:col0 + 512], tmp64[:, :],
                      s_od)
    ctx_fin = [(p.bk(4), p.val(p.bk(4))), (s_od, p.val(s_od))]
    if stop_after == "passb":
        for sem, vv in ctx_fin:
            p.wait("sync", sem, vv)
        return

    ar.reset(m3)
    wo = ar.alloc(DJ * 1024, BF16)
    bor = ar.alloc(1024, BF16, parts=1)
    rts = [ar.alloc(D, F32) for _ in range(2)]
    ros = [ar.alloc(D, F32) for _ in range(2)]
    s_w2 = pfx + "w2"
    p.wait("gpsimd", s_cm, p.val(s_cm))
    for sem, vv in ctx_fin:
        p.wait("gpsimd", sem, vv)
    p.wait("gpsimd", pfx + "blm", p.val(pfx + "blm"))
    p.wait("gpsimd", p.bk(5), p.val(p.bk(5)))
    for j in range(DJ):
        p.dma("gpsimd", wo[:, j * 1024:(j + 1) * 1024], io["wo_t"][j], s_w2)
    p.dma("gpsimd", bor[:, :], io["bo_row"][:, :], s_w2)
    w2_rdy = p.val(s_w2)
    s_pw = pfx + "pw"
    p.wait("tensor", s_w2, w2_rdy)
    for sem, vv in ctx_fin:
        p.wait("tensor", sem, vv)
    rt_guard = {}
    for i in range(NTT):
        rti, roi = rts[i % 2], ros[i % 2]
        s_rl2 = pfx + "rld%d" % (i % 2)
        s_ro = pfx + "rout%d" % (i % 2)
        if i >= 2:
            for sem, vv in rt_guard[i - 2]:
                p.wait("sync", sem, vv)
        p.dma("sync", rti[:, :], io["res_in"][i * 128:(i + 1) * 128, :], s_rl2)
        g4 = 4 * (i % 2)
        for k in range(2):
            bid_e = g4 + k
            pbe = banks[bid_e]
            p.bank_acquire("tensor", bid_e)
            for n, h in enumerate(range(0, H, 2)):
                p.do("tensor", lambda e, pbe=pbe, h=h, i=i, k=k, n=n: e.matmul(
                    pbe[:, :],
                    ctx_sb[0:64,
                           (h // 2) * NT + i * 128:(h // 2) * NT + i * 128 + 128],
                    wo[0:64,
                       (h // 2) * 1024 + k * 512:(h // 2) * 1024 + (k + 1) * 512],
                    start=(n == 0), stop=False))
            p.do("tensor", lambda e, pbe=pbe, k=k: e.matmul(
                pbe[:, :], p.c["ones_bf"][0:1, 0:128],
                bor[0:1, k * 512:(k + 1) * 512], start=False, stop=True),
                inc=(s_pw, 1))
            bid_o = g4 + 2 + k
            pbo = banks[bid_o]
            p.bank_acquire("tensor", bid_o)
            for n, h in enumerate(range(1, H, 2)):
                p.do("tensor", lambda e, pbo=pbo, h=h, i=i, k=k, n=n: e.matmul(
                    pbo[:, :],
                    ctx_sb[64:128,
                           (h // 2) * NT + i * 128:(h // 2) * NT + i * 128 + 128],
                    wo[64:128,
                       (h // 2) * 1024 + k * 512:(h // 2) * 1024 + (k + 1) * 512],
                    start=(n == 0), stop=(n == H // 2 - 1)),
                    inc=(s_pw, 1) if n == H // 2 - 1 else None)
        p.wait("vector", s_pw, p.val(s_pw))
        p.wait("vector", s_rl2, p.val(s_rl2))
        if i >= 2:
            p.wait("vector", s_ro, p.val(s_ro))
        specs = []
        for k in range(2):
            bid_e, bid_o = g4 + k, g4 + 2 + k
            p.do("vector", lambda e, pb=banks[bid_e], rti=rti, roi=roi, k=k:
                 e.tensor_tensor(roi[:, k * 512:(k + 1) * 512], pb[:, :],
                                 rti[:, k * 512:(k + 1) * 512], ALU.add),
                 inc=(p.bk(bid_e), 1))
            specs.append((p.bk(bid_e), p.val(p.bk(bid_e))))
            p.wait("vector", p.bk(bid_e), p.val(p.bk(bid_e)))
            p.do("vector", lambda e, pb=banks[bid_o], roi=roi, k=k:
                 e.tensor_tensor(roi[:, k * 512:(k + 1) * 512],
                                 roi[:, k * 512:(k + 1) * 512], pb[:, :],
                                 ALU.add),
                 inc=(p.bk(bid_o), 1))
            specs.append((p.bk(bid_o), p.val(p.bk(bid_o))))
        rt_guard[i] = specs
        for sem, vv in specs:
            p.wait("sync", sem, vv)
        p.dma("sync", io["res_out"][i * 128:(i + 1) * 128, :], roi[:, :], s_ro)
    p.wait("sync", pfx + "rout0", p.val(pfx + "rout0"))
    p.wait("sync", pfx + "rout1", p.val(pfx + "rout1"))
    ar.reset(m0)


def make_vaug(Wv_in_out):
    w = Wv_in_out.reshape(1024, 16, 64)
    aug = np.zeros((1024, 16, 65), np.float32)
    aug[:, :, 0:64] = w
    return np.ascontiguousarray(aug.reshape(1024, 1040)).reshape(DJ, 128, 1040)


def make_erow(bv):
    e = np.zeros((16, 65), np.float32)
    e[:, 0:64] = bv.reshape(16, 64)
    e[:, 64] = 1.0
    return e.reshape(1, 1040).copy()


def make_masks():
    r = np.arange(128)[:, None]
    c = np.arange(512)[None, :]
    return np.stack([(c >= r + 128 * j).astype(np.float32) for j in range(4)])


def prep_l0_attn(ins):
    W_qkv = np.asarray(ins["L0_attn_W_qkv"], np.float32)
    Wq = np.ascontiguousarray(W_qkv[0].reshape(1024, 1024).T)
    Wk = np.ascontiguousarray(W_qkv[1].reshape(1024, 1024).T)
    Wv = np.ascontiguousarray(W_qkv[2].reshape(1024, 1024).T)
    WoT = np.ascontiguousarray(np.asarray(ins["L0_attn_W_o"], np.float32).T)
    z = np.zeros((1, 1024), np.float32)
    return dict(
        wq_t=np.ascontiguousarray(Wq.reshape(DJ, 128, 1024)),
        wk_t=np.ascontiguousarray(Wk.reshape(DJ, 128, 1024)),
        wv_aug=np.ascontiguousarray(make_vaug(Wv)),
        erow_v=make_erow(np.zeros((1024,), np.float32)),
        qb_row=z.copy(), kb_row=z.copy(),
        wo_t=np.ascontiguousarray(WoT.reshape(DJ, 128, 1024)),
        bo_row=z.copy(),
        masks=make_masks(),
    )


def prep_l2_attn(ins):
    f = lambda k: np.ascontiguousarray(np.asarray(ins[k], np.float32))
    return dict(
        wq_t=f("L2_attn_Wq").reshape(DJ, 128, 1024),
        wk_t=f("L2_attn_Wk").reshape(DJ, 128, 1024),
        wv_aug=np.ascontiguousarray(make_vaug(f("L2_attn_Wv"))),
        erow_v=make_erow(np.asarray(ins["L2_attn_bv"], np.float32)),
        qb_row=np.asarray(ins["L2_attn_bq"], np.float32).reshape(1, 1024),
        kb_row=np.asarray(ins["L2_attn_bk"], np.float32).reshape(1, 1024),
        wo_t=f("L2_attn_Wo").reshape(DJ, 128, 1024),
        bo_row=np.asarray(ins["L2_attn_bo"], np.float32).reshape(1, 1024),
        masks=make_masks(),
    )


def const_inputs():
    return dict(ident=np.eye(128, dtype=BF), ident_f=np.eye(128, dtype=np.float32))


ATTN_IO_SPECS = [("wq_t", [DJ, 128, 1024]), ("wk_t", [DJ, 128, 1024]),
                 ("wv_aug", [DJ, 128, 1040]), ("erow_v", [1, 1040]),
                 ("qb_row", [1, 1024]), ("kb_row", [1, 1024]),
                 ("wo_t", [DJ, 128, 1024]), ("bo_row", [1, 1024]),
                 ("masks", [4, 128, 512])]


def build_attn_launch(pfx="a0", scale=0.125, stop_after=None):
    nc = bass.Bass("TRN2", target_bir_lowering=False)
    stack = ExitStack()
    io = dict(
        res_in=nc.dram_tensor("res_in", [NT, D], F32, kind="ExternalInput"),
        res_out=nc.dram_tensor("res_out", [NT, D], F32, kind="ExternalOutput"),
        qt=nc.dram_tensor("qt_scr", [16, 65, NT], BF16),
        kt=nc.dram_tensor("kt_scr", [16, 65, NT], BF16),
    )
    for nm, shp in ATTN_IO_SPECS:
        io[nm] = nc.dram_tensor(nm, shp, F32, kind="ExternalInput")
    cd = dict(
        ident=nc.dram_tensor("ident", [128, 128], BF16, kind="ExternalInput"),
        ident_f=nc.dram_tensor("ident_f", [128, 128], F32, kind="ExternalInput"),
    )
    p = Prog(nc, stack)
    with stack:
        banks = make_banks(p)
        ar = Arena(p, 176 * 1024)
        emit_const_setup(p, cd)
        emit_attn_causal(p, ar, banks, io, pfx, scale=scale, stop_after=stop_after)
        p.finalize()
    return nc




def emit_l2_ffn(p, ar, banks, io, pfx):
    CH = 512
    NCH = NT // CH
    TPC = CH // 128
    m0 = ar.mark()
    wr = ar.alloc(DJ * 1024, BF16)
    wk = ar.alloc(DJ * 1024, BF16)
    wv = ar.alloc(DJ * 1024, BF16)
    mu = ar.alloc(8, F32)
    mur = ar.alloc(DJ, F32)
    murn = ar.alloc(DJ, F32)
    muk = ar.alloc(DJ, F32)
    mukn = ar.alloc(DJ, F32)
    s_w = pfx + "w"
    for j in range(DJ):
        p.dma("gpsimd", wr[:, j * 1024:(j + 1) * 1024], io["wr_t"][j], s_w)
        p.dma("gpsimd", wk[:, j * 1024:(j + 1) * 1024], io["f2k_t"][j], s_w)
        p.dma("gpsimd", wv[:, j * 1024:(j + 1) * 1024], io["wv_t"][j], s_w)
    p.dma("gpsimd", mur[:, :], io["mu_r"][:, :], s_w)
    p.dma("gpsimd", muk[:, :], io["mu_k"][:, :], s_w)
    w_rdy = p.val(s_w)
    p.wait("vector", s_w, w_rdy)
    p.do("vector", lambda e: e.tensor_scalar(murn[:, :], mur[:, :], -1.0, 1.0,
                                             ALU.mult, ALU.add), inc=(pfx + "mu", 1))
    p.do("vector", lambda e: e.tensor_scalar(mukn[:, :], muk[:, :], -1.0, 1.0,
                                             ALU.mult, ALU.add), inc=(pfx + "mu", 1))
    p.wait("vector", pfx + "mu", p.val(pfx + "mu"))
    p.wait("tensor", s_w, w_rdy)

    XW = CH + 1
    xns = [ar.alloc(DJ * XW, BF16) for _ in range(2)]
    xrt = ar.alloc(DJ * CH, BF16)
    xkt = ar.alloc(DJ * CH, BF16)
    k2t = ar.alloc(DJ * CH, BF16)
    rtk = ar.alloc(TPC * 1024, BF16)
    rst = [ar.alloc(D, F32) for _ in range(2)]
    rso = [ar.alloc(D, F32) for _ in range(2)]
    tmpv = ar.alloc(1024, F32)
    p.do("gpsimd", lambda e: e.memset(xns[0][:, :], 0.0), inc=(pfx + "hz", 1))
    p.wait("vector", pfx + "hz", 1)
    s_mv = pfx + "mv"
    s_xr = pfx + "xr"
    guards = {"r_pe": None, "k_pe": None, "v_pe": None, "mv": 0}

    for c in range(NCH):
        xn = xns[c % 2]
        if c >= 2:
            p.wait("vector", s_xr, guards["xr%d" % (c - 2)])
            p.wait("vector", pfx + "hc", p.val(pfx + "hc"))
        if c >= 1:
            prevnm = pfx + "ln_nm"
            p.wait("sync", prevnm, p.val(prevnm))
        mln = ar.mark()
        specs = emit_ln_to_fm(
            p, ar, banks,
            lambda i, c=c: io["res_in"][c * CH + i * 128: c * CH + (i + 1) * 128, :],
            lambda i, j, xn=xn: xn[:, j * XW + 1 + i * 128: j * XW + 1 + i * 128 + 128],
            n_tiles=TPC, pfx=pfx + "ln")
        ar.reset(mln)
        for sem, vv in specs:
            p.wait("vector", sem, vv)
        if c > 0:
            prev = xns[(c - 1) % 2]
            p.do("vector", lambda e, xn=xn, prev=prev: e.tensor_copy(
                xn[:, 0:DJ * XW].rearrange("p (j w) -> p j w", w=XW)[:, :, 0:1],
                prev[:, 0:DJ * XW].rearrange("p (j w) -> p j w", w=XW)[:, :, CH:CH + 1]),
                inc=(pfx + "hc", 1))
            p.wait("vector", pfx + "hc", p.val(pfx + "hc"))
        if guards["r_pe"] is not None:
            p.wait("vector", guards["r_pe"][0], guards["r_pe"][1])
            p.wait("vector", guards["k_pe"][0], guards["k_pe"][1])
        for j in range(DJ):
            cur = xn[:, j * XW + 1: j * XW + 1 + CH]
            prv = xn[:, j * XW: j * XW + CH]
            for dst, m0_, m1_ in ((xrt, murn, mur), (xkt, mukn, muk)):
                p.do("vector", lambda e, j=j, dst=dst, prv=prv, m0_=m0_:
                     e.tensor_scalar(dst[:, j * CH:(j + 1) * CH], prv,
                                     m0_[:, j:j + 1], None, ALU.mult, ALU.bypass),
                     inc=(s_xr, 1))
                p.wait("vector", s_xr, p.val(s_xr))
                p.do("vector", lambda e, j=j, dst=dst, cur=cur, m1_=m1_:
                     e.scalar_tensor_tensor(dst[:, j * CH:(j + 1) * CH], cur,
                                            m1_[:, j:j + 1],
                                            dst[:, j * CH:(j + 1) * CH],
                                            ALU.mult, ALU.add), inc=(s_xr, 1))
        guards["xr%d" % c] = p.val(s_xr)
        p.wait("tensor", s_xr, p.val(s_xr))

        mv_guard = guards["mv"]
        first_r = [True]

        def r_out(i, pbs, pe_ready, c=c, mv_guard=mv_guard, first_r=first_r):
            p.wait("scalar", pe_ready[0], pe_ready[1])
            if first_r[0] and mv_guard:
                p.wait("scalar", s_mv, mv_guard)
                first_r[0] = False
            for pb, w0, w1, rel in pbs:
                p.do("scalar", lambda e, pb=pb, i=i, w0=w0, w1=w1: e.activation(
                    rtk[:, i * 1024 + w0: i * 1024 + w1], pb[:, 0:w1 - w0],
                    AF.Sigmoid), inc=rel)
        emit_proj_tm(p, xrt, wr, r_out, banks, (0, 1, 2, 3), pfx + "r",
                     out_w=1024, kdj=DJ, xn_nt=CH, n_tok_tiles=TPC)
        guards["r_pe"] = (pfx + "r_pe", p.val(pfx + "r_pe"))

        if guards["v_pe"] is not None:
            p.wait("scalar", guards["v_pe"][0], guards["v_pe"][1])

        def k_out(oc, tc, pb, pe_ready, rel, c=c):
            p.wait("scalar", pe_ready[0], pe_ready[1])
            dst = k2t[:, oc * CH + tc * 512: oc * CH + tc * 512 + 512]
            p.do("scalar", lambda e, pb=pb, dst=dst: e.activation(
                dst, pb[:, :], AF.Relu), inc=rel)
            p.wait("vector", rel[0], p.val(rel[0]))
            p.do("vector", lambda e, dst=dst: e.tensor_tensor(
                dst, dst, dst, ALU.mult), inc=(pfx + "k2", 1))
        emit_proj_fm(p, xkt, wk, k_out, banks, (4, 5), pfx + "k",
                     n_out_tiles=8, tok_chunks=1, kdj=DJ, tok_w=512, xn_nt=CH)
        guards["k_pe"] = (pfx + "k_pe", p.val(pfx + "k_pe"))
        p.wait("tensor", pfx + "k2", p.val(pfx + "k2"))

        def v_out(i, pbs, pe_ready, c=c):
            ii = c * TPC + i
            rti, roi = rst[ii % 2], rso[ii % 2]
            s_rl = pfx + "rld%d" % (ii % 2)
            s_ro = pfx + "rout%d" % (ii % 2)
            if ii >= 2:
                p.wait("sync", s_mv, p.val(s_mv))
            p.dma("sync", rti[:, :], io["res_in"][ii * 128:(ii + 1) * 128, :],
                  s_rl)
            p.wait("vector", pe_ready[0], pe_ready[1])
            p.wait("vector", s_rl, p.val(s_rl))
            if ii >= 2:
                p.wait("vector", s_ro, p.val(s_ro))
            for pb, w0, w1, rel in pbs:
                p.wait("vector", s_mv, p.val(s_mv))
                p.do("vector", lambda e, pb=pb, i=i, w0=w0, w1=w1: e.tensor_tensor(
                    tmpv[:, w0:w1], pb[:, 0:w1 - w0],
                    rtk[:, i * 1024 + w0: i * 1024 + w1], ALU.mult), inc=rel)
                p.wait("vector", rel[0], p.val(rel[0]))
                p.do("vector", lambda e, rti=rti, roi=roi, w0=w0, w1=w1:
                     e.tensor_tensor(roi[:, w0:w1], tmpv[:, w0:w1],
                                     rti[:, w0:w1], ALU.add), inc=(s_mv, 1))
            p.wait("sync", s_mv, p.val(s_mv))
            p.dma("sync", io["res_out"][ii * 128:(ii + 1) * 128, :], roi[:, :],
                  s_ro)
        emit_proj_tm(p, k2t, wv, v_out, banks, (0, 1, 2, 3), pfx + "v",
                     out_w=1024, kdj=DJ, xn_nt=CH, n_tok_tiles=TPC)
        guards["v_pe"] = (pfx + "v_pe", p.val(pfx + "v_pe"))
        guards["mv"] = p.val(s_mv)
    p.wait("sync", pfx + "rout0", p.val(pfx + "rout0"))
    p.wait("sync", pfx + "rout1", p.val(pfx + "rout1"))
    ar.reset(m0)


def prep_l2_ffn(ins):
    f = lambda k: np.ascontiguousarray(np.asarray(ins[k], np.float32))
    return dict(
        wr_t=np.ascontiguousarray(f("L2_ffn_W_r").T.reshape(DJ, 128, 1024)),
        f2k_t=np.ascontiguousarray(f("L2_ffn_W_k").T.reshape(DJ, 128, 1024)),
        wv_t=np.ascontiguousarray(f("L2_ffn_W_v").T.reshape(DJ, 128, 1024)),
        mu_r=np.ascontiguousarray(f("L2_ffn_mu_r").reshape(DJ, 128).T),
        mu_k=np.ascontiguousarray(f("L2_ffn_mu_k").reshape(DJ, 128).T),
    )


L2FFN_IO_SPECS = [("wr_t", [DJ, 128, 1024]), ("f2k_t", [DJ, 128, 1024]),
                  ("wv_t", [DJ, 128, 1024]), ("mu_r", [128, DJ]),
                  ("mu_k", [128, DJ])]


def emit_l1_attn(p, ar, banks, io, pfx):
    EPS = 1e-6
    m0 = ar.mark()
    kv_sb = ar.alloc(DJ * 1024, BF16)
    ksum = ar.alloc(8, BF16)
    ksf = ar.alloc(8, F32)
    wslot = ar.alloc(DJ * 2048, BF16)
    brow = ar.alloc(1024, BF16, parts=1)

    m1 = ar.mark()
    xn_t = ar.alloc(DJ * NT, BF16)
    mln = ar.mark()
    xn_specs = emit_ln_to_fm(
        p, ar, banks, lambda i: io["res_in"][i * 128:(i + 1) * 128, :],
        lambda i, j: xn_t[:, j * NT + i * 128: j * NT + i * 128 + 128],
        pfx=pfx + "ln")
    ar.reset(mln)
    kst = [ar.alloc(1024, BF16) for _ in range(2)]
    vst = [ar.alloc(1024, BF16) for _ in range(2)]
    qst = [ar.alloc(512, BF16) for _ in range(3)]

    s_wv = pfx + "wkv"
    for sem, vv in xn_specs:
        p.wait("gpsimd", sem, vv)
    for j in range(DJ):
        p.dma("gpsimd", wslot[:, j * 2048:(j + 1) * 2048], io["wkv_t"][j], s_wv)
    p.wait("tensor", s_wv, p.val(s_wv))
    for sem, vv in xn_specs:
        p.wait("tensor", sem, vv)

    def kv_out(i, pbs, pe_ready):
        ks, vs = kst[i % 2], vst[i % 2]
        s_ks = pfx + "ks%d" % (i % 2)
        s_vs = pfx + "vs%d" % (i % 2)
        if i >= 2:
            p.wait("scalar", s_ks, p.val(s_ks))
            p.wait("vector", s_vs, p.val(s_vs))
        p.wait("scalar", pe_ready[0], pe_ready[1])
        p.wait("vector", pe_ready[0], pe_ready[1])
        vrel = []
        for pb, w0, w1, rel in pbs:
            if w0 < 1024:
                p.do("scalar", lambda e, pb=pb, ks=ks, w0=w0, w1=w1:
                     e.activation(ks[:, w0:w1], pb[:, 0:w1 - w0], AF.Relu),
                     inc=rel)
                p.wait("vector", rel[0], p.val(rel[0]))
                p.do("vector", lambda e, ks=ks, w0=w0, w1=w1: e.tensor_scalar(
                    ks[:, w0:w1], ks[:, w0:w1], float(EPS), None, ALU.add,
                    ALU.bypass), inc=(pfx + "ke", 1))
            else:
                p.do("vector", lambda e, pb=pb, vs=vs, w0=w0, w1=w1:
                     e.tensor_copy(vs[:, w0 - 1024:w1 - 1024], pb[:, 0:w1 - w0]),
                     inc=rel)
                vrel.append((rel[0], p.val(rel[0])))
        p.wait("sync", pfx + "ke", p.val(pfx + "ke"))
        p.dma("sync", io["k_d"][i], ks[:, :], s_ks)
        for sem, vv in vrel:
            p.wait("sync", sem, vv)
        p.dma("sync", io["v_d"][i], vs[:, :], s_vs)

    emit_proj_tm(p, xn_t, wslot, kv_out, banks, (0, 1, 2, 3, 4, 5, 6, 7),
                 pfx + "kv", out_w=2048, kdj=DJ)

    s_wq = pfx + "wq"
    p.wait("gpsimd", pfx + "kv_pe", p.val(pfx + "kv_pe"))
    for j in range(DJ):
        p.dma("gpsimd", wslot[:, j * 2048:j * 2048 + 1024], io["wq_t"][j], s_wq)
    p.wait("tensor", s_wq, p.val(s_wq))
    nq = [0]

    def q_out(oc, tc, pb, pe_ready, rel):
        k = nq[0]
        qs = qst[k % 3]
        s_qs = pfx + "qs%d" % (k % 3)
        if k >= 3:
            p.wait("scalar", s_qs, p.val(s_qs))
        p.wait("scalar", pe_ready[0], pe_ready[1])
        p.do("scalar", lambda e, pb=pb, qs=qs: e.activation(
            qs[:, :], pb[:, :], AF.Relu), inc=rel)
        p.wait("vector", rel[0], p.val(rel[0]))
        p.do("vector", lambda e, qs=qs: e.tensor_scalar(
            qs[:, :], qs[:, :], float(EPS), None, ALU.add, ALU.bypass),
            inc=(pfx + "qe", 1))
        p.wait("sync", pfx + "qe", p.val(pfx + "qe"))
        p.dma("sync", io["q_d"][oc, :, tc * 512:(tc + 1) * 512], qs[:, :], s_qs)
        nq[0] += 1

    emit_proj_fm(p, xn_t, wslot, q_out, banks, (0, 1), pfx + "q",
                 n_out_tiles=8, tok_chunks=8, kdj=DJ, tok_w=512, w_stride=2048)

    ar.reset(m1)
    at_t = ar.alloc(DJ * NT, BF16)
    zb = ar.alloc(NT, F32)
    rz = zb[0:1, :]
    ksl = [ar.alloc(1024, BF16) for _ in range(2)]
    vsl = [ar.alloc(512, BF16) for _ in range(2)]
    qpc = [ar.alloc(DJ * 512, BF16) for _ in range(2)]
    rst = [ar.alloc(D, F32) for _ in range(2)]
    rso = [ar.alloc(D, F32) for _ in range(2)]
    p.wait("sync", pfx + "q_pe", p.val(pfx + "q_pe"))
    for s in range(2):
        p.wait("sync", pfx + "ks%d" % s, p.val(pfx + "ks%d" % s))
        p.wait("sync", pfx + "vs%d" % s, p.val(pfx + "vs%d" % s))
    for s in range(3):
        p.wait("sync", pfx + "qs%d" % s, p.val(pfx + "qs%d" % s))
    p.bank_acquire("tensor", 0)
    for i in range(NTT):
        sl = ksl[i % 2]
        s_sl = pfx + "kl%d" % (i % 2)
        if i >= 2:
            p.wait("sync", pfx + "km", p.val(pfx + "km"))
        p.dma("sync", sl[:, :], io["k_d"][i], s_sl)
        p.wait("tensor", s_sl, p.val(s_sl))
        for j in range(DJ):
            p.do("tensor", lambda e, sl=sl, j=j, i=i: e.matmul(
                banks[0][:, j:j + 1], sl[:, j * 128:(j + 1) * 128],
                p.c["ones_col"][:, 0:1],
                start=(i == 0 and j == 0), stop=(i == NTT - 1 and j == DJ - 1)),
                inc=(pfx + "km", 1) if j == DJ - 1 else None)
    p.wait("vector", pfx + "km", p.val(pfx + "km"))
    p.do("vector", lambda e: e.tensor_copy(ksf[:, :], banks[0][:, 0:8]),
         inc=(pfx + "ksv", 1))
    p.wait("vector", pfx + "ksv", p.val(pfx + "ksv"))
    p.do("vector", lambda e: e.tensor_copy(ksum[:, :], ksf[:, :]),
         inc=(p.bk(0), 1))
    p.wait("tensor", p.bk(0), p.val(p.bk(0)))
    p.wait("tensor", pfx + "ksv", p.val(pfx + "ksv"))

    for eh in range(2):
        for i in range(NTT):
            ksl_i, vsl_i = ksl[i % 2], vsl[i % 2]
            s_kl = pfx + "cl%d" % (i % 2)
            if i >= 2 or eh == 1:
                p.wait("sync", pfx + "cm", p.val(pfx + "cm"))
            elif eh == 0 and i < 2:
                p.wait("sync", pfx + "km", p.val(pfx + "km"))
            p.dma("sync", ksl_i[:, :], io["k_d"][i], s_kl)
            p.dma("sync", vsl_i[:, :], io["v_d"][i, :, eh * 512:(eh + 1) * 512],
                  s_kl)
            p.wait("tensor", s_kl, p.val(s_kl))
            if i == 0:
                for j in range(DJ):
                    p.bank_acquire("tensor", j)
            for j in range(DJ):
                p.do("tensor", lambda e, ksl_i=ksl_i, vsl_i=vsl_i, j=j, i=i:
                     e.matmul(banks[j][:, :], ksl_i[:, j * 128:(j + 1) * 128],
                              vsl_i[:, :], start=(i == 0),
                              stop=(i == NTT - 1)),
                     inc=(pfx + "cm", 1) if j == DJ - 1 else None)
        p.wait("vector", pfx + "cm", p.val(pfx + "cm"))
        for j in range(DJ):
            p.do("vector", lambda e, j=j, eh=eh: e.tensor_copy(
                kv_sb[:, j * 1024 + eh * 512: j * 1024 + eh * 512 + 512],
                banks[j][:, :]), inc=(p.bk(j), 1))
    kv_done = [(p.bk(j), p.val(p.bk(j))) for j in range(DJ)]

    zm_guard = {}
    for nc_ in range(8):
        sl = qpc[nc_ % 2]
        s_sl = pfx + "ql%d" % (nc_ % 2)
        if nc_ >= 2:
            p.wait("sync", pfx + "zm", zm_guard[nc_ - 2])
        for j in range(DJ):
            p.dma("sync", sl[:, j * 512:(j + 1) * 512],
                  io["q_d"][j, :, nc_ * 512:(nc_ + 1) * 512], s_sl)
        p.wait("tensor", s_sl, p.val(s_sl))
        bz = 2 + nc_ % 2
        p.bank_acquire("tensor", bz)
        for j in range(DJ):
            p.do("tensor", lambda e, sl=sl, j=j, bz=bz: e.matmul(
                banks[bz][0:1, :], ksum[:, j:j + 1], sl[:, j * 512:(j + 1) * 512],
                start=(j == 0), stop=(j == DJ - 1)),
                inc=(pfx + "zm", 1) if j == DJ - 1 else None)
        zm_guard[nc_] = p.val(pfx + "zm")
        p.wait("vector", pfx + "zm", p.val(pfx + "zm"))
        p.do("vector", lambda e, nc_=nc_, bz=bz: e.tensor_scalar(
            rz[0:1, nc_ * 512:(nc_ + 1) * 512], banks[bz][0:1, :], float(EPS),
            None, ALU.add, ALU.bypass), inc=(p.bk(bz), 1))
    p.wait("vector", p.bk(2), p.val(p.bk(2)))
    p.wait("vector", p.bk(3), p.val(p.bk(3)))
    p.do("vector", lambda e: e.reciprocal(rz[0:1, :], rz[0:1, :]),
         inc=(pfx + "rz", 1))
    p.wait("tensor", pfx + "rz", p.val(pfx + "rz"))
    for nc_ in range(8):
        bz = 2 + nc_ % 2
        p.bank_acquire("tensor", bz)
        p.do("tensor", lambda e, nc_=nc_, bz=bz: e.matmul(
            banks[bz][:, :], p.c["ones_f"][0:1, 0:128],
            rz[0:1, nc_ * 512:(nc_ + 1) * 512], start=True, stop=True),
            inc=(pfx + "zb", 1))
        p.wait("scalar", pfx + "zb", p.val(pfx + "zb"))
        p.do("scalar", lambda e, nc_=nc_, bz=bz: e.copy(
            zb[:, nc_ * 512:(nc_ + 1) * 512], banks[bz][:, :]),
            inc=(p.bk(bz), 1))
    p.wait("vector", p.bk(2), p.val(p.bk(2)))
    p.wait("vector", p.bk(3), p.val(p.bk(3)))

    for sem, vv in kv_done:
        p.wait("tensor", sem, vv)
    am_guard = {}
    for nc_ in range(8):
        sl = qpc[nc_ % 2]
        s_sl = pfx + "fl%d" % (nc_ % 2)
        if nc_ >= 2:
            p.wait("sync", pfx + "am", am_guard[nc_ - 2])
        elif nc_ == 0:
            p.wait("sync", pfx + "zm", p.val(pfx + "zm"))
        for j in range(DJ):
            p.dma("sync", sl[:, j * 512:(j + 1) * 512],
                  io["q_d"][j, :, nc_ * 512:(nc_ + 1) * 512], s_sl)
        p.wait("tensor", s_sl, p.val(s_sl))
        for oc in range(8):
            bb = (2 * nc_ + oc) % 2 + 4
            p.bank_acquire("tensor", bb)
            for j in range(DJ):
                p.do("tensor", lambda e, sl=sl, j=j, oc=oc, bb=bb: e.matmul(
                    banks[bb][:, :],
                    kv_sb[:, j * 1024 + oc * 128: j * 1024 + oc * 128 + 128],
                    sl[:, j * 512:(j + 1) * 512], start=(j == 0),
                    stop=(j == DJ - 1)),
                    inc=(pfx + "am", 1) if j == DJ - 1 else None)
            p.wait("vector", pfx + "am", p.val(pfx + "am"))
            p.do("vector", lambda e, oc=oc, nc_=nc_, bb=bb: e.tensor_tensor(
                at_t[:, oc * NT + nc_ * 512: oc * NT + (nc_ + 1) * 512],
                banks[bb][:, :], zb[:, nc_ * 512:(nc_ + 1) * 512], ALU.mult),
                inc=(p.bk(bb), 1))
        am_guard[nc_] = p.val(pfx + "am")
    at_done = [(p.bk(4), p.val(p.bk(4))), (p.bk(5), p.val(p.bk(5)))]

    s_wo = pfx + "wo"
    p.wait("gpsimd", pfx + "am", p.val(pfx + "am"))
    for j in range(DJ):
        p.dma("gpsimd", wslot[:, j * 2048:j * 2048 + 1024], io["wout_t"][j],
              s_wo)
    p.dma("gpsimd", brow[:, :], io["bout_row"][:, :], s_wo)
    p.wait("tensor", s_wo, p.val(s_wo))
    for sem, vv in at_done:
        p.wait("tensor", sem, vv)
    ro_guard = {}

    def o_out(i, pbs, pe_ready):
        rti, roi = rst[i % 2], rso[i % 2]
        s_rl = pfx + "rld%d" % (i % 2)
        s_ro = pfx + "rout%d" % (i % 2)
        if i >= 2:
            for sem, vv in ro_guard[i - 2]:
                p.wait("sync", sem, vv)
        p.dma("sync", rti[:, :], io["res_in"][i * 128:(i + 1) * 128, :], s_rl)
        p.wait("vector", pe_ready[0], pe_ready[1])
        p.wait("vector", s_rl, p.val(s_rl))
        if i >= 2:
            p.wait("vector", s_ro, p.val(s_ro))
        specs = []
        for pb, w0, w1, rel in pbs:
            p.do("vector", lambda e, pb=pb, rti=rti, roi=roi, w0=w0, w1=w1:
                 e.tensor_tensor(roi[:, w0:w1], pb[:, 0:w1 - w0],
                                 rti[:, w0:w1], ALU.add), inc=rel)
            specs.append((rel[0], p.val(rel[0])))
        ro_guard[i] = specs
        for sem, vv in specs:
            p.wait("sync", sem, vv)
        p.dma("sync", io["res_out"][i * 128:(i + 1) * 128, :], roi[:, :], s_ro)

    emit_proj_tm(p, at_t, wslot, o_out, banks, (0, 1, 2, 3), pfx + "o",
                 out_w=1024, kdj=DJ, erow=brow, w_stride=2048)
    p.wait("sync", pfx + "rout0", p.val(pfx + "rout0"))
    p.wait("sync", pfx + "rout1", p.val(pfx + "rout1"))
    ar.reset(m0)


def prep_l1_attn(ins):
    W = np.asarray(ins["L1_attn_W_qkv"], np.float32)
    Wq = np.ascontiguousarray(W[0:1024].T)
    Wk = np.ascontiguousarray(W[1024:2048].T)
    Wv = np.ascontiguousarray(W[2048:3072].T)
    wkv = np.concatenate([Wk, Wv], axis=1)
    Wout = np.ascontiguousarray(np.asarray(ins["L1_attn_W_out"], np.float32).T)
    return dict(
        wkv_t=np.ascontiguousarray(wkv.reshape(DJ, 128, 2048)),
        wq_t=np.ascontiguousarray(Wq.reshape(DJ, 128, 1024)),
        wout_t=np.ascontiguousarray(Wout.reshape(DJ, 128, 1024)),
        bout_row=np.asarray(ins["L1_attn_b_out"], np.float32).reshape(1, 1024),
    )


L1ATTN_IO_SPECS = [("wkv_t", [DJ, 128, 2048]), ("wq_t", [DJ, 128, 1024]),
                   ("wout_t", [DJ, 128, 1024]), ("bout_row", [1, 1024])]
L1ATTN_SCRATCH = [("k_d", [NTT, 128, 1024]),
                  ("v_d", [NTT, 128, 1024]), ("q_d", [DJ, 128, NT])]


def emit_l0_ffn(p, ar, banks, io, pfx, mode="full"):
    m0 = ar.mark()
    wg = ar.alloc(DJ * 8, BF16)
    OH = ar.alloc(NTT * 8, BF16)
    gate = ar.alloc(NTT, F32)
    rstt = [ar.alloc(16, F32) for _ in range(2)]
    s_wg = pfx + "wg"
    for j in range(DJ):
        p.dma("gpsimd", wg[:, j * 8:(j + 1) * 8], io["wg_t"][j], s_wg)
    wg_rdy = p.val(s_wg)

    m1 = ar.mark()
    xn0 = ar.alloc(DJ * NT, BF16)
    mln = ar.mark()
    xn_specs = emit_ln_to_fm(
        p, ar, banks, lambda i: io["res_in"][i * 128:(i + 1) * 128, :],
        lambda i, j: xn0[:, j * NT + i * 128: j * NT + i * 128 + 128],
        pfx=pfx + "ln")
    ar.reset(mln)
    for sem, vv in xn_specs:
        p.wait("tensor", sem, vv)
    p.wait("tensor", s_wg, wg_rdy)

    def route_out(i, pbs, pe_ready):
        (pb, w0, w1, rel) = pbs[0]
        st = rstt[i % 2]
        s_rt = pfx + "rt"
        p.wait("vector", pe_ready[0], pe_ready[1])
        if i >= 2:
            p.wait("vector", s_rt, p.val(s_rt))
        p.do("vector", lambda e, pb=pb, st=st: e.tensor_reduce(
            st[:, 0:1], pb[:, 0:8], AX.X, ALU.max), inc=(s_rt, 1))
        p.wait("vector", s_rt, p.val(s_rt))
        if mode == "full":
            p.do("vector", lambda e, pb=pb, st=st, i=i: e.tensor_scalar(
                OH[:, i * 8:(i + 1) * 8], pb[:, 0:8], st[:, 0:1], None,
                ALU.is_equal, ALU.bypass), inc=(s_rt, 1))
            p.do("vector", lambda e, st=st: e.tensor_scalar(
                st[:, 1:2], st[:, 0:1], -1.0, None, ALU.mult, ALU.bypass),
                inc=(s_rt, 1))
            p.wait("scalar", s_rt, p.val(s_rt))
            p.do("scalar", lambda e, pb=pb, st=st: e.activation(
                st[:, 8:16], pb[:, 0:8], AF.Exp, bias=st[:, 1:2], scale=1.0),
                inc=rel)
            p.wait("vector", rel[0], p.val(rel[0]))
            p.do("vector", lambda e, st=st: e.tensor_reduce(
                st[:, 2:3], st[:, 8:16], AX.X, ALU.add), inc=(s_rt, 1))
            p.wait("vector", s_rt, p.val(s_rt))
            p.do("vector", lambda e, st=st, i=i: e.reciprocal(
                gate[:, i:i + 1], st[:, 2:3]), inc=(s_rt, 1))
        else:
            p.do("vector", lambda e, pb=pb, st=st, i=i: e.tensor_scalar(
                OH[:, i * 8:(i + 1) * 8], pb[:, 0:8], st[:, 0:1], None,
                ALU.is_equal, ALU.bypass), inc=rel)
            p.wait("vector", rel[0], p.val(rel[0]))
            p.do("vector", lambda e: e.nop(), inc=(s_rt, 1))

    emit_proj_tm(p, xn0, wg, route_out, banks, (0, 1, 2, 3), pfx + "lg",
                 out_w=8, kdj=DJ)

    if mode == "counts":
        p.bank_acquire("tensor", 4)
        p.wait("tensor", pfx + "rt", p.val(pfx + "rt"))
        for i in range(NTT):
            p.do("tensor", lambda e, i=i: e.matmul(
                banks[4][0:8, 0:1], OH[:, i * 8:(i + 1) * 8],
                p.c["ones_col"][:, 0:1], start=(i == 0), stop=(i == NTT - 1)),
                inc=(pfx + "cm", 1) if i == NTT - 1 else None)
        p.wait("vector", pfx + "cm", p.val(pfx + "cm"))
        cnt = ar.alloc(1, F32, parts=8)
        p.do("vector", lambda e, cnt=cnt: e.tensor_copy(cnt[:, :],
                                                        banks[4][0:8, 0:1]),
             inc=(p.bk(4), 1))
        p.wait("sync", p.bk(4), p.val(p.bk(4)))
        p.dma("sync", io["counts_out"][:, :], cnt[:, :], pfx + "co")
        p.wait("sync", pfx + "co", p.val(pfx + "co"))
        ar.reset(m0)
        return

    ca = ar.alloc(8, F32, parts=8)
    sel = ar.alloc(1, F32, parts=8)
    offs = ar.alloc(1, F32, parts=8)
    oh_t = ar.alloc(NT, BF16, parts=8)
    loc = ar.alloc(NT, F32, parts=8)
    wfin = ar.alloc(NT, BF16, parts=8)
    wcols = ar.alloc(NTT * 8, F32)
    s_ca = pfx + "ca"
    for sem, vv in xn_specs:
        p.wait("gpsimd", sem, vv)
    p.dma("gpsimd", ca[:, :], io["counts_all"][:, :], s_ca)
    p.dma("gpsimd", sel[:, :], io["sel"][:, :], s_ca)
    ca_rdy = p.val(s_ca)
    p.wait("tensor", s_ca, ca_rdy)
    p.bank_acquire("tensor", 4)
    p.do("tensor", lambda e: e.matmul(banks[4][0:8, 0:1], ca[:, :], sel[:, :],
                                      start=True, stop=True),
         inc=(pfx + "om", 1))
    p.wait("vector", pfx + "om", p.val(pfx + "om"))
    p.do("vector", lambda e: e.tensor_copy(offs[:, :], banks[4][0:8, 0:1]),
         inc=(p.bk(4), 1))

    p.wait("tensor", pfx + "rt", p.val(pfx + "rt"))
    for i in range(NTT):
        bid = 4 + i % 2
        pb = banks[bid][:, 0:64].bitcast(BF16)[0:8, 0:128]
        p.bank_acquire("tensor", bid)
        p.do("tensor", lambda e, pb=pb, i=i: e.transpose(
            pb, OH[:, i * 8:(i + 1) * 8], p.c["ident"][:, :]),
            inc=(pfx + "ot", 1))
        p.wait("vector", pfx + "ot", p.val(pfx + "ot"))
        p.do("vector", lambda e, pb=pb, i=i: e.tensor_copy(
            oh_t[:, i * 128:(i + 1) * 128], pb), inc=(p.bk(bid), 1))
    p.wait("vector", p.bk(4), p.val(p.bk(4)))
    p.wait("vector", p.bk(5), p.val(p.bk(5)))
    p.do("vector", lambda e: e.tensor_tensor_scan(
        loc[:, :], oh_t[:, :], oh_t[:, :], 0.0, ALU.add, ALU.bypass),
        inc=(pfx + "sc", 1))
    p.wait("vector", pfx + "sc", p.val(pfx + "sc"))
    p.do("vector", lambda e: e.tensor_tensor(loc[:, :], loc[:, :], oh_t[:, :],
                                             ALU.subtract), inc=(pfx + "sc", 1))
    p.wait("vector", pfx + "sc", p.val(pfx + "sc"))
    p.do("vector", lambda e: e.tensor_scalar(
        loc[:, :], loc[:, :], offs[:, 0:1], float(NT), ALU.add, ALU.is_lt),
        inc=(pfx + "sc", 1))
    p.wait("vector", pfx + "sc", p.val(pfx + "sc"))
    p.do("vector", lambda e: e.tensor_tensor(wfin[:, :], loc[:, :], oh_t[:, :],
                                             ALU.mult), inc=(pfx + "sc", 1))
    p.wait("tensor", pfx + "sc", p.val(pfx + "sc"))
    for i in range(NTT):
        bid = 4 + i % 2
        pbb = banks[bid][:, 0:64].bitcast(BF16)[0:128, 0:8]
        p.bank_acquire("tensor", bid)
        p.do("tensor", lambda e, pbb=pbb, i=i: e.transpose(
            pbb, wfin[:, i * 128:(i + 1) * 128], p.c["ident"][0:8, 0:8]),
            inc=(pfx + "wt", 1))
        p.wait("vector", pfx + "wt", p.val(pfx + "wt"))
        p.do("vector", lambda e, pbb=pbb, i=i: e.tensor_scalar(
            wcols[:, i * 8:(i + 1) * 8], pbb, gate[:, i:i + 1], None,
            ALU.mult, ALU.bypass), inc=(p.bk(bid), 1))
    wc_done = p.val(p.bk(4)), p.val(p.bk(5))
    p.wait("vector", p.bk(4), wc_done[0])
    p.wait("vector", p.bk(5), wc_done[1])

    NQ = 8
    TPQ = NTT // NQ
    acc = ar.alloc(TPQ * 1024, F32)
    wex = [ar.alloc(DJ * 1024, BF16) for _ in range(2)]
    ber = [ar.alloc(1024, BF16, parts=1) for _ in range(2)]
    s_ac = pfx + "ac"
    for q in range(NQ):
        if q >= 1:
            p.wait("sync", pfx + "ro", p.val(pfx + "ro"))
        p.dma("sync", acc[:, :].rearrange("p (t c) -> p t c", c=1024),
              io["res_in"][:, :].rearrange("(qq t pp) c -> qq pp t c", pp=128,
                                           t=TPQ)[q], s_ac)
        p.wait("vector", s_ac, p.val(s_ac))
        for e_ in range(8):
            ge = q * 8 + e_
            sl, bl_ = wex[ge % 2], ber[ge % 2]
            s_we = pfx + "we%d" % (ge % 2)
            if ge >= 2:
                p.wait("gpsimd", pfx + "em", p.val(pfx + "em"))
            for j in range(DJ):
                p.dma("gpsimd", sl[:, j * 1024:(j + 1) * 1024],
                      io["w_ex%d" % e_][j], s_we)
            p.dma("gpsimd", bl_[:, :], io["b_ex%d" % e_][:, :], s_we)
            p.wait("tensor", s_we, p.val(s_we))
            for t in range(TPQ):
                gt = q * TPQ + t
                for k in range(2):
                    bid = 2 * (gt % 2) + k
                    pb = banks[bid]
                    p.bank_acquire("tensor", bid)
                    for j in range(DJ):
                        p.do("tensor", lambda e, pb=pb, sl=sl, j=j, gt=gt, k=k:
                             e.matmul(
                                 pb[:, :],
                                 xn0[:, j * NT + gt * 128: j * NT + gt * 128 + 128],
                                 sl[:, j * 1024 + k * 512: j * 1024 + (k + 1) * 512],
                                 start=(j == 0), stop=False))
                    p.do("tensor", lambda e, pb=pb, bl_=bl_, k=k: e.matmul(
                        pb[:, :], p.c["ones_bf"][0:1, 0:128],
                        bl_[0:1, k * 512:(k + 1) * 512], start=False, stop=True),
                        inc=(pfx + "em", 1))
                    p.wait("vector", pfx + "em", p.val(pfx + "em"))
                    p.do("vector", lambda e, pb=pb, t=t, k=k, gt=gt, e_=e_:
                         e.scalar_tensor_tensor(
                             acc[:, t * 1024 + k * 512: t * 1024 + (k + 1) * 512],
                             pb[:, :], wcols[:, gt * 8 + e_:gt * 8 + e_ + 1],
                             acc[:, t * 1024 + k * 512: t * 1024 + (k + 1) * 512],
                             ALU.mult, ALU.add), inc=(p.bk(bid), 1))
        p.wait("sync", p.bk(0), p.val(p.bk(0)))
        p.wait("sync", p.bk(1), p.val(p.bk(1)))
        p.wait("sync", p.bk(2), p.val(p.bk(2)))
        p.wait("sync", p.bk(3), p.val(p.bk(3)))
        p.dma("sync",
              io["res_out"][:, :].rearrange("(qq t pp) c -> qq pp t c",
                                            pp=128, t=TPQ)[q],
              acc[:, :].rearrange("p (t c) -> p t c", c=1024), pfx + "ro")
    p.wait("sync", pfx + "ro", p.val(pfx + "ro"))
    ar.reset(m0)


def prep_l0_ffn(ins, core_id):
    wg = np.ascontiguousarray(np.asarray(ins["L0_ffn_wg"], np.float32))
    W = np.asarray(ins["L0_ffn_W"], np.float32)
    b = np.asarray(ins["L0_ffn_b"], np.float32)
    sel = (np.arange(8) < core_id).astype(np.float32).reshape(8, 1)
    d = dict(
        wg_t=wg.reshape(DJ, 128, 8),
        sel=sel,
    )
    for e_ in range(8):
        d["w_ex%d" % e_] = np.ascontiguousarray(W[e_].T.reshape(DJ, 128, 1024))
        d["b_ex%d" % e_] = b[e_].reshape(1, 1024).copy()
    return d


L0FFN_IO_SPECS = ([("wg_t", [DJ, 128, 8]), ("sel", [8, 1])] +
                  [("w_ex%d" % e_, [DJ, 128, 1024]) for e_ in range(8)] +
                  [("b_ex%d" % e_, [1, 1024]) for e_ in range(8)])


def emit_l1_ffn_pre(p, ar, banks, io, pfx):
    m0 = ar.mark()
    hst = [ar.alloc(D, F32) for _ in range(3)]

    def hs_cb(i, xt, st, st_ready):
        h = hst[i % 3]
        s_hs = pfx + "hs%d" % (i % 3)
        p.wait("vector", st_ready[0], st_ready[1])
        if i >= 3:
            p.wait("vector", s_hs, p.val(s_hs))
        p.do("vector", lambda e, xt=xt, st=st, h=h: e.tensor_scalar(
            h[:, :], xt[:, :], st[:, 0:1], st[:, 2:3], ALU.subtract, ALU.mult),
            inc=(pfx + "hv", 1))
        p.wait("sync", pfx + "hv", p.val(pfx + "hv"))
        p.dma("sync", io["hs_d"][i * 128:(i + 1) * 128, :], h[:, :], s_hs)
        return [(pfx + "hv", p.val(pfx + "hv"))]

    mln = ar.mark()
    emit_ln_tiles(p, ar, lambda i: io["res_in"][i * 128:(i + 1) * 128, :],
                  NTT, hs_cb, pfx=pfx + "l0")
    ar.reset(mln)
    for s in range(3):
        p.wait("sync", pfx + "hs%d" % s, p.val(pfx + "hs%d" % s))

    w1c = ar.alloc(DJ, F32)
    b1c = ar.alloc(DJ, F32)
    p.wait("gpsimd", pfx + "hv", p.val(pfx + "hv"))
    p.dma("gpsimd", w1c[:, :], io["ln1w"][:, :], pfx + "lw")
    p.dma("gpsimd", b1c[:, :], io["ln1b"][:, :], pfx + "lw")
    p.wait("vector", pfx + "lw", p.val(pfx + "lw"))
    ln1 = ar.alloc(DJ * NT, BF16)
    mln2 = ar.mark()
    specs = emit_ln_to_fm(
        p, ar, banks, lambda i: io["hs_d"][i * 128:(i + 1) * 128, :],
        lambda i, j: ln1[:, j * NT + i * 128: j * NT + i * 128 + 128],
        scale_bias=(w1c, b1c), pfx=pfx + "l1")
    ar.reset(mln2)
    wslot = ar.alloc(DJ * 1024, BF16)
    stg = [ar.alloc(1024, BF16) for _ in range(2)]
    for sem, vv in specs:
        p.wait("tensor", sem, vv)

    for which, wkey, dkey in (("q", "f1q_t", "q_d"), ("k", "wkrep_t", "k_d"),
                              ("v", "wvrep_t", "v_d")):
        s_w = pfx + "w" + which
        if which != "q":
            prev = {"k": "q", "v": "k"}[which]
            p.wait("gpsimd", pfx + prev + "_pe", p.val(pfx + prev + "_pe"))
            for s in range(2):
                nm = pfx + prev + "s%d" % s
                p.wait("vector", nm, p.val(nm))
        for sem, vv in specs:
            p.wait("gpsimd", sem, vv)
        for j in range(DJ):
            p.dma("gpsimd", wslot[:, j * 1024:(j + 1) * 1024], io[wkey][j], s_w)
        p.wait("tensor", s_w, p.val(s_w))

        def out_cb(i, pbs, pe_ready, which=which, dkey=dkey):
            sl = stg[i % 2]
            s_st = pfx + which + "s%d" % (i % 2)
            if i >= 2:
                p.wait("vector", s_st, p.val(s_st))
            p.wait("vector", pe_ready[0], pe_ready[1])
            vrel = []
            for pb, w0, w1, rel in pbs:
                p.do("vector", lambda e, pb=pb, sl=sl, w0=w0, w1=w1:
                     e.tensor_copy(sl[:, w0:w1], pb[:, 0:w1 - w0]), inc=rel)
                vrel.append((rel[0], p.val(rel[0])))
            for sem, vv in vrel:
                p.wait("sync", sem, vv)
            p.dma("sync", io[dkey][i], sl[:, :], s_st)

        emit_proj_tm(p, ln1, wslot, out_cb, banks, (0, 1, 2, 3), pfx + which,
                     out_w=1024, kdj=DJ)
    done = []
    for which in ("q", "k", "v"):
        for s in range(2):
            nm = pfx + which + "s%d" % s
            p.wait("sync", nm, p.val(nm))
            done.append((nm, p.val(nm)))
        done.append((pfx + which + "_pe", p.val(pfx + which + "_pe")))
    ar.reset(m0)
    return done


def emit_l1_ffn_post(p, ar, banks, io, pfx, prev_done=(), dbg_tiles=None):
    m0 = ar.mark()
    for sem, vv in prev_done:
        p.wait("gpsimd", sem, vv)
    amask = ar.alloc(128, F32)
    p.dma("gpsimd", amask[:, :], io["amask"][:, :], pfx + "am")
    p.wait("vector", pfx + "am", p.val(pfx + "am"))

    m1 = ar.mark()
    at_t = ar.alloc(DJ * NT, BF16)
    qsl = [ar.alloc(1024, BF16) for _ in range(2)]
    ksl = [ar.alloc(1024, BF16) for _ in range(2)]
    vsl = [ar.alloc(1024, BF16) for _ in range(2)]
    tmp = ar.alloc(1024, BF16)
    sc = ar.alloc(128, F32)
    pr = ar.alloc(128, F32)
    mx = ar.alloc(64, F32)
    accv = ar.alloc(1024, F32)
    attn = ar.alloc(1024, BF16)
    s_sc = pfx + "sc"
    NTILES = NTT if dbg_tiles is None else dbg_tiles
    for buf in (tmp, sc, pr, mx, accv, attn):
        p.do("gpsimd", lambda e, buf=buf: e.memset(buf, 0.0), inc=(s_sc, 1))
    p.wait("vector", s_sc, p.val(s_sc))
    p.wait("scalar", s_sc, p.val(s_sc))
    p.wait("tensor", s_sc, p.val(s_sc))
    qg = [0, 0]
    kg = [0, 0]
    vg = [0, 0]
    for i in range(NTILES):
        qs = qsl[i % 2]
        s_q = pfx + "ql%d" % (i % 2)
        p.wait("sync", s_sc, qg[i % 2])
        p.dma("sync", qs[:, :], io["q_d"][i], s_q)
        p.wait("vector", s_q, p.val(s_q))
        for t in range(8):
            ks = ksl[t % 2]
            s_k = pfx + "kl%d" % (t % 2)
            p.wait("sync", s_sc, kg[t % 2])
            p.dma("sync", ks[:, :], io["kall"][t, i], s_k)
            p.wait("vector", s_k, p.val(s_k))
            p.wait("vector", s_sc, p.val(s_sc))
            p.do("vector", lambda e, qs=qs, ks=ks: e.tensor_tensor(
                tmp[:, :], qs[:, :], ks[:, :], ALU.mult), inc=(s_sc, 1))
            kg[t % 2] = p.val(s_sc)
            p.wait("vector", s_sc, p.val(s_sc))
            p.do("vector", lambda e, t=t: e.tensor_reduce(
                sc[:, t * 16:(t + 1) * 16],
                tmp[:, :].rearrange("p (h d) -> p h d", d=64), AX.X, ALU.add),
                inc=(s_sc, 1))
        qg[i % 2] = p.val(s_sc)
        p.wait("vector", s_sc, p.val(s_sc))
        p.do("vector", lambda e: e.tensor_tensor(sc[:, :], sc[:, :],
                                                 amask[:, :], ALU.add),
             inc=(s_sc, 1))
        p.wait("vector", s_sc, p.val(s_sc))
        scv = sc[:, :].rearrange("p (t h) -> p h t", h=16)
        p.do("vector", lambda e, scv=scv: e.tensor_reduce(
            mx[:, 0:16], scv, AX.X, ALU.max), inc=(s_sc, 1))
        p.wait("vector", s_sc, p.val(s_sc))
        mxb = mx[:, 0:16].rearrange("p h -> p () h").broadcast_to((128, 8, 16))
        p.do("vector", lambda e, mxb=mxb: e.tensor_tensor(
            sc[:, :].rearrange("p (t h) -> p t h", h=16),
            sc[:, :].rearrange("p (t h) -> p t h", h=16), mxb, ALU.subtract),
            inc=(s_sc, 1))
        p.wait("scalar", s_sc, p.val(s_sc))
        scc = p.constf(0.125)
        p.do("scalar", lambda e, scc=scc: e.activation(
            pr[:, :], sc[:, :], AF.Exp, scale=scc), inc=(s_sc, 1))
        p.wait("vector", s_sc, p.val(s_sc))
        prv = pr[:, :].rearrange("p (t h) -> p h t", h=16)
        p.do("vector", lambda e, prv=prv: e.tensor_reduce(
            mx[:, 16:32], prv, AX.X, ALU.add), inc=(s_sc, 1))
        p.wait("vector", s_sc, p.val(s_sc))
        p.do("vector", lambda e: e.reciprocal(mx[:, 32:48], mx[:, 16:32]),
             inc=(s_sc, 1))
        for t in range(8):
            vs = vsl[t % 2]
            s_v = pfx + "vl%d" % (t % 2)
            p.wait("sync", s_sc, vg[t % 2])
            p.dma("sync", vs[:, :], io["vall"][t, i], s_v)
            p.wait("vector", s_v, p.val(s_v))
            p.wait("vector", s_sc, p.val(s_sc))
            prb = pr[:, t * 16:(t + 1) * 16].rearrange(
                "p h -> p h ()").broadcast_to((128, 16, 64))
            if t == 0:
                p.do("vector", lambda e, vs=vs, prb=prb: e.tensor_tensor(
                    accv[:, :].rearrange("p (h d) -> p h d", d=64),
                    vs[:, :].rearrange("p (h d) -> p h d", d=64), prb,
                    ALU.mult), inc=(s_sc, 1))
            else:
                p.do("vector", lambda e, vs=vs, prb=prb: e.tensor_tensor(
                    tmp[:, :].rearrange("p (h d) -> p h d", d=64),
                    vs[:, :].rearrange("p (h d) -> p h d", d=64), prb,
                    ALU.mult), inc=(s_sc, 1))
                p.wait("vector", s_sc, p.val(s_sc))
                p.do("vector", lambda e: e.tensor_tensor(
                    accv[:, :], accv[:, :], tmp[:, :].bitcast(F32) if False
                    else tmp[:, :], ALU.add), inc=(s_sc, 1))
            vg[t % 2] = p.val(s_sc)
            p.wait("vector", s_sc, p.val(s_sc))
        rzb = mx[:, 32:48].rearrange("p h -> p h ()").broadcast_to((128, 16, 64))
        p.do("vector", lambda e, rzb=rzb: e.tensor_tensor(
            attn[:, :].rearrange("p (h d) -> p h d", d=64),
            accv[:, :].rearrange("p (h d) -> p h d", d=64), rzb, ALU.mult),
            inc=(s_sc, 1))
        p.wait("tensor", s_sc, p.val(s_sc))
        for j in range(DJ):
            bid = 6 + (i * DJ + j) % 2
            pb = banks[bid][:, 0:64].bitcast(BF16)
            p.bank_acquire("tensor", bid)
            p.do("tensor", lambda e, pb=pb, j=j: e.transpose(
                pb, attn[:, j * 128:(j + 1) * 128], p.c["ident"][:, :]),
                inc=(pfx + "tr", 1))
            p.wait("vector", pfx + "tr", p.val(pfx + "tr"))
            p.do("vector", lambda e, pb=pb, i=i, j=j: e.tensor_copy(
                at_t[:, j * NT + i * 128: j * NT + i * 128 + 128], pb),
                inc=(p.bk(bid), 1))

    wslot = ar.alloc(DJ * 1024, BF16)
    hst = [ar.alloc(D, F32) for _ in range(2)]
    hao = [ar.alloc(D, F32) for _ in range(2)]
    s_wp = pfx + "wp"
    p.wait("gpsimd", pfx + "tr", p.val(pfx + "tr"))
    for j in range(DJ):
        p.dma("gpsimd", wslot[:, j * 1024:(j + 1) * 1024], io["wproj_t"][j],
              s_wp)
    p.wait("tensor", s_wp, p.val(s_wp))
    p.wait("tensor", p.bk(6), p.val(p.bk(6)))
    p.wait("tensor", p.bk(7), p.val(p.bk(7)))
    ha_guard = {}

    def proj_out(i, pbs, pe_ready):
        ht, ho = hst[i % 2], hao[i % 2]
        s_h = pfx + "hl%d" % (i % 2)
        s_o = pfx + "ho%d" % (i % 2)
        if i >= 2:
            for sem, vv in ha_guard[i - 2]:
                p.wait("sync", sem, vv)
        p.dma("sync", ht[:, :], io["hs_d"][i * 128:(i + 1) * 128, :], s_h)
        p.wait("vector", pe_ready[0], pe_ready[1])
        p.wait("vector", s_h, p.val(s_h))
        if i >= 2:
            p.wait("vector", s_o, p.val(s_o))
        specs = []
        for pb, w0, w1, rel in pbs:
            p.do("vector", lambda e, pb=pb, ht=ht, ho=ho, w0=w0, w1=w1:
                 e.tensor_tensor(ho[:, w0:w1], pb[:, 0:w1 - w0], ht[:, w0:w1],
                                 ALU.add), inc=rel)
            specs.append((rel[0], p.val(rel[0])))
        ha_guard[i] = specs
        for sem, vv in specs:
            p.wait("sync", sem, vv)
        p.dma("sync", io["ha_d"][i * 128:(i + 1) * 128, :], ho[:, :], s_o)

    emit_proj_tm(p, at_t, wslot, proj_out, banks, (0, 1, 2, 3), pfx + "pj",
                 out_w=1024, kdj=DJ)
    for s in range(2):
        p.wait("sync", pfx + "ho%d" % s, p.val(pfx + "ho%d" % s))

    ar.reset(m1)
    w2c = ar.alloc(DJ, F32)
    b2c = ar.alloc(DJ, F32)
    p.wait("gpsimd", pfx + "pj_pe", p.val(pfx + "pj_pe"))
    p.wait("sync", pfx + "pj_pe", p.val(pfx + "pj_pe"))
    for s in range(2):
        p.wait("sync", pfx + "ho%d" % s, p.val(pfx + "ho%d" % s))
        p.wait("gpsimd", pfx + "ho%d" % s, p.val(pfx + "ho%d" % s))
    p.dma("gpsimd", w2c[:, :], io["ln2w"][:, :], pfx + "l2w")
    p.dma("gpsimd", b2c[:, :], io["ln2b"][:, :], pfx + "l2w")
    p.wait("vector", pfx + "l2w", p.val(pfx + "l2w"))
    ln2 = ar.alloc(DJ * NT, BF16)
    mln = ar.mark()
    specs2 = emit_ln_to_fm(
        p, ar, banks, lambda i: io["ha_d"][i * 128:(i + 1) * 128, :],
        lambda i, j: ln2[:, j * NT + i * 128: j * NT + i * 128 + 128],
        scale_bias=(w2c, b2c), pfx=pfx + "l2")
    ar.reset(mln)
    for sem, vv in specs2:
        p.wait("tensor", sem, vv)

    wrt = ar.alloc(DJ * 8, BF16)
    wcols = ar.alloc(NTT * 8, F32)
    rstt = [ar.alloc(24, F32) for _ in range(2)]
    for sem, vv in specs2:
        p.wait("gpsimd", sem, vv)
    p.wait("gpsimd", pfx + "l2_nm", p.val(pfx + "l2_nm"))
    for j in range(DJ):
        p.dma("gpsimd", wrt[:, j * 8:(j + 1) * 8], io["wrt_t"][j], pfx + "rw")
    p.wait("tensor", pfx + "rw", p.val(pfx + "rw"))

    def rt_out(i, pbs, pe_ready):
        (pb, w0, w1, rel) = pbs[0]
        st = rstt[i % 2]
        s_rt = pfx + "rt"
        p.wait("vector", pe_ready[0], pe_ready[1])
        if i >= 2:
            p.wait("vector", s_rt, p.val(s_rt))
        p.do("vector", lambda e, pb=pb, st=st: e.tensor_reduce(
            st[:, 0:1], pb[:, 0:8], AX.X, ALU.max), inc=(s_rt, 1))
        p.wait("vector", s_rt, p.val(s_rt))
        p.do("vector", lambda e, st=st: e.tensor_scalar(
            st[:, 1:2], st[:, 0:1], -1.0, None, ALU.mult, ALU.bypass),
            inc=(s_rt, 1))
        p.wait("scalar", s_rt, p.val(s_rt))
        p.do("scalar", lambda e, pb=pb, st=st: e.activation(
            st[:, 8:16], pb[:, 0:8], AF.Exp, bias=st[:, 1:2], scale=1.0),
            inc=(s_rt, 1))
        p.wait("vector", s_rt, p.val(s_rt))
        p.do("vector", lambda e, st=st: e.tensor_reduce(
            st[:, 2:3], st[:, 8:16], AX.X, ALU.add), inc=(s_rt, 1))
        p.wait("vector", s_rt, p.val(s_rt))
        p.do("vector", lambda e, st=st: e.reciprocal(st[:, 3:4], st[:, 2:3]),
             inc=(s_rt, 1))
        p.wait("vector", s_rt, p.val(s_rt))
        p.do("vector", lambda e, st=st: e.tensor_scalar(
            st[:, 16:24], st[:, 8:16], st[:, 3:4], None, ALU.mult, ALU.bypass),
            inc=(s_rt, 1))
        p.wait("vector", s_rt, p.val(s_rt))
        p.do("vector", lambda e, st=st: e.tensor_reduce(
            st[:, 4:5], st[:, 16:24], AX.X, ALU.max), inc=(s_rt, 1))
        p.wait("vector", s_rt, p.val(s_rt))
        p.do("vector", lambda e, st=st: e.tensor_scalar(
            st[:, 8:16], st[:, 16:24], st[:, 4:5], None, ALU.is_equal,
            ALU.bypass), inc=(s_rt, 1))
        p.wait("vector", s_rt, p.val(s_rt))
        p.do("vector", lambda e, st=st: e.scalar_tensor_tensor(
            st[:, 8:16], st[:, 8:16], -1e30, st[:, 16:24], ALU.mult, ALU.add),
            inc=(s_rt, 1))
        p.wait("vector", s_rt, p.val(s_rt))
        p.do("vector", lambda e, st=st: e.tensor_reduce(
            st[:, 5:6], st[:, 8:16], AX.X, ALU.max), inc=(s_rt, 1))
        p.wait("vector", s_rt, p.val(s_rt))
        p.do("vector", lambda e, st=st: e.tensor_scalar(
            st[:, 8:16], st[:, 16:24], st[:, 5:6], None, ALU.is_ge,
            ALU.bypass), inc=(s_rt, 1))
        p.wait("vector", s_rt, p.val(s_rt))
        p.do("vector", lambda e, st=st, i=i: e.tensor_tensor(
            wcols[:, i * 8:(i + 1) * 8], st[:, 8:16], st[:, 16:24], ALU.mult),
            inc=rel)
        p.wait("vector", rel[0], p.val(rel[0]))

    emit_proj_tm(p, ln2, wrt, rt_out, banks, (4, 5, 6, 7), pfx + "rt",
                 out_w=8, kdj=DJ)

    NQ = 4
    TPQ = NTT // NQ
    acc = ar.alloc(TPQ * 1024, F32)
    hat_s = [ar.alloc(D, F32) for _ in range(2)]
    m1r = [ar.alloc(1024, BF16) for _ in range(3)]
    m2r = [ar.alloc(512, BF16) for _ in range(2)]
    h1t = ar.alloc(16 * 1024, BF16)
    p.wait("vector", pfx + "rt_pe", p.val(pfx + "rt_pe"))
    p.wait("sync", pfx + "l2_tr", p.val(pfx + "l2_tr"))
    p.wait("sync", pfx + "l2_nm", p.val(pfx + "l2_nm"))
    p.wait("gpsimd", pfx + "l2_tr", p.val(pfx + "l2_tr"))
    s_mm = pfx + "mm"
    m1g = [0, 0, 0]
    m2g = [0, 0]
    m1n = [0]
    m2n = [0]

    def load_m1(e_, u):
        sl = m1n[0] % 3
        s_w = pfx + "ma%d" % sl
        p.wait("gpsimd", s_mm, m1g[sl])
        p.dma("gpsimd", m1r[sl][:, :], io["moe1_%d" % e_][u], s_w)
        m1n[0] += 1
        return sl, (s_w, p.val(s_w))

    def load_m2(e_, k, u):
        sl = m2n[0] % 2
        s_w = pfx + "mc%d" % sl
        p.wait("gpsimd", s_mm, m2g[sl])
        p.dma("gpsimd", m2r[sl][:, :], io["moe2_%d" % e_][k, u], s_w)
        m2n[0] += 1
        return sl, (s_w, p.val(s_w))
    for q in range(NQ):
        if q >= 1:
            p.wait("sync", pfx + "ro", p.val(pfx + "ro"))
        p.dma("sync", acc[:, :].rearrange("p (t c) -> p t c", c=1024),
              io["res_in"][:, :].rearrange("(qq t pp) c -> qq pp t c", pp=128,
                                           t=TPQ)[q], pfx + "al")
        p.wait("vector", pfx + "al", p.val(pfx + "al"))
        for t in range(TPQ):
            hs_ = hat_s[t % 2]
            s_h = pfx + "hq%d" % (t % 2)
            if t >= 2:
                p.wait("sync", pfx + "ai", p.val(pfx + "ai"))
            p.dma("sync", hs_[:, :],
                  io["ha_d"][(q * TPQ + t) * 128:(q * TPQ + t + 1) * 128, :],
                  s_h)
            p.wait("vector", s_h, p.val(s_h))
            p.do("vector", lambda e, hs_=hs_, t=t: e.tensor_tensor(
                acc[:, t * 1024:(t + 1) * 1024], acc[:, t * 1024:(t + 1) * 1024],
                hs_[:, :], ALU.add), inc=(pfx + "ai", 1))
        p.wait("vector", pfx + "ai", p.val(pfx + "ai"))
        c0 = q * TPQ * 128
        for e_ in range(8):
            pend1 = [load_m1(e_, 0), load_m1(e_, 1)]
            pend2 = []
            for u in range(16):
                sl, rdy = pend1.pop(0)
                if u + 2 < 16:
                    pend1.append(load_m1(e_, u + 2))
                elif u == 15:
                    pend2.append(load_m2(e_, 0, 0))
                p.wait("tensor", rdy[0], rdy[1])
                for h in range(2):
                    bid = (2 * u + h) % 4
                    pb = banks[bid]
                    p.bank_acquire("tensor", bid)
                    for j in range(DJ):
                        p.do("tensor", lambda e, pb=pb, sl=sl, j=j, c0=c0, h=h:
                             e.matmul(
                                 pb[:, :], m1r[sl][:, j * 128:(j + 1) * 128],
                                 ln2[:, j * NT + c0 + h * 512:
                                     j * NT + c0 + h * 512 + 512],
                                 start=(j == 0), stop=(j == DJ - 1)),
                             inc=(s_mm, 1) if j == DJ - 1 else None)
                    p.wait("scalar", s_mm, p.val(s_mm))
                    p.do("scalar", lambda e, pb=pb, u=u, h=h: e.activation(
                        h1t[:, u * 1024 + h * 512: u * 1024 + h * 512 + 512],
                        pb[:, :], AF.Gelu_apprx_tanh), inc=(p.bk(bid), 1))
                m1g[sl] = p.val(s_mm)
            for k in range(2):
                for tt in range(8):
                    p.bank_acquire("tensor", tt)
                for u in range(16):
                    sl, rdy = pend2.pop(0)
                    if u + 1 < 16:
                        pend2.append(load_m2(e_, k, u + 1))
                    elif k == 0:
                        pend2.append(load_m2(e_, 1, 0))
                    p.wait("tensor", rdy[0], rdy[1])
                    for tt in range(8):
                        pb = banks[tt]
                        p.do("tensor", lambda e, pb=pb, sl=sl, u=u, tt=tt:
                             e.matmul(
                                 pb[:, :],
                                 h1t[:, u * 1024 + tt * 128:
                                     u * 1024 + tt * 128 + 128],
                                 m2r[sl][:, :], start=(u == 0), stop=(u == 15)),
                             inc=(s_mm, 1) if tt == 7 else None)
                    m2g[sl] = p.val(s_mm)
                for tt in range(8):
                    gt = q * TPQ + tt
                    pb = banks[tt]
                    p.wait("vector", s_mm, p.val(s_mm))
                    p.do("vector", lambda e, pb=pb, tt=tt, k=k, gt=gt, e_=e_:
                         e.scalar_tensor_tensor(
                             acc[:, tt * 1024 + k * 512: tt * 1024 + k * 512 + 512],
                             pb[:, :], wcols[:, gt * 8 + e_:gt * 8 + e_ + 1],
                             acc[:, tt * 1024 + k * 512: tt * 1024 + k * 512 + 512],
                             ALU.mult, ALU.add), inc=(p.bk(tt), 1))
        for b in range(8):
            p.wait("sync", p.bk(b), p.val(p.bk(b)))
        p.dma("sync",
              io["res_out"][:, :].rearrange("(qq t pp) c -> qq pp t c",
                                            pp=128, t=TPQ)[q],
              acc[:, :].rearrange("p (t c) -> p t c", c=1024), pfx + "ro")
    p.wait("sync", pfx + "ro", p.val(pfx + "ro"))
    ar.reset(m0)


def prep_l1_ffn_pre(ins):
    qkv = np.asarray(ins["L1_ffn_qkv_weight"], np.float32)
    Wq = np.ascontiguousarray(qkv[0:1024].T)
    Wk = qkv[1024:1280].T
    Wv = qkv[1280:1536].T
    Wk_rep = np.ascontiguousarray(
        np.repeat(Wk.reshape(1024, 4, 64), 4, axis=1).reshape(1024, 1024))
    Wv_rep = np.ascontiguousarray(
        np.repeat(Wv.reshape(1024, 4, 64), 4, axis=1).reshape(1024, 1024))
    return dict(
        f1q_t=Wq.reshape(DJ, 128, 1024),
        wkrep_t=Wk_rep.reshape(DJ, 128, 1024),
        wvrep_t=Wv_rep.reshape(DJ, 128, 1024),
        ln1w=np.asarray(ins["L1_ffn_ln1_weight"], np.float32).reshape(DJ, 128).T.copy(),
        ln1b=np.asarray(ins["L1_ffn_ln1_bias"], np.float32).reshape(DJ, 128).T.copy(),
    )


def prep_l1_ffn_post(ins, core_id):
    Wproj = np.ascontiguousarray(np.asarray(ins["L1_ffn_proj_weight"],
                                            np.float32).T)
    wrt = np.ascontiguousarray(np.asarray(ins["L1_ffn_router_weight"],
                                          np.float32).T)
    amask = np.zeros((128, 128), np.float32)
    for t in range(8):
        if t > core_id:
            amask[:, t * 16:(t + 1) * 16] = -1e30
    d = dict(
        wproj_t=Wproj.reshape(DJ, 128, 1024),
        wrt_t=wrt.reshape(DJ, 128, 8),
        ln2w=np.asarray(ins["L1_ffn_ln2_weight"], np.float32).reshape(DJ, 128).T.copy(),
        ln2b=np.asarray(ins["L1_ffn_ln2_bias"], np.float32).reshape(DJ, 128).T.copy(),
        amask=amask,
    )
    w1 = np.asarray(ins["L1_ffn_moe_w1"], np.float32)
    w2 = np.asarray(ins["L1_ffn_moe_w2"], np.float32)
    for e_ in range(8):
        d["moe1_%d" % e_] = np.ascontiguousarray(
            w1[e_].reshape(8, 128, 2, 8, 128).transpose(2, 3, 1, 0, 4)
            .reshape(16, 128, 1024))
        d["moe2_%d" % e_] = np.ascontiguousarray(
            w2[e_].reshape(2, 8, 128, 2, 512).transpose(3, 0, 1, 2, 4)
            .reshape(2, 16, 128, 512))
    return d


def prep_l1_ffn(ins, core_id):
    qkv = np.asarray(ins["L1_ffn_qkv_weight"], np.float32)
    Wq = np.ascontiguousarray(qkv[0:1024].T)
    Wk = qkv[1024:1280].T
    Wv = qkv[1280:1536].T
    Wk_rep = np.ascontiguousarray(
        np.repeat(Wk.reshape(1024, 4, 64), 4, axis=1).reshape(1024, 1024))
    Wv_rep = np.ascontiguousarray(
        np.repeat(Wv.reshape(1024, 4, 64), 4, axis=1).reshape(1024, 1024))
    Wproj = np.ascontiguousarray(np.asarray(ins["L1_ffn_proj_weight"],
                                            np.float32).T)
    wrt = np.ascontiguousarray(np.asarray(ins["L1_ffn_router_weight"],
                                          np.float32).T)
    amask = np.zeros((128, 128), np.float32)
    for t in range(8):
        if t > core_id:
            amask[:, t * 16:(t + 1) * 16] = -1e30
    d = dict(
        f1q_t=Wq.reshape(DJ, 128, 1024),
        wkrep_t=Wk_rep.reshape(DJ, 128, 1024),
        wvrep_t=Wv_rep.reshape(DJ, 128, 1024),
        wproj_t=Wproj.reshape(DJ, 128, 1024),
        wrt_t=wrt.reshape(DJ, 128, 8),
        ln1w=np.asarray(ins["L1_ffn_ln1_weight"], np.float32).reshape(DJ, 128).T.copy(),
        ln1b=np.asarray(ins["L1_ffn_ln1_bias"], np.float32).reshape(DJ, 128).T.copy(),
        ln2w=np.asarray(ins["L1_ffn_ln2_weight"], np.float32).reshape(DJ, 128).T.copy(),
        ln2b=np.asarray(ins["L1_ffn_ln2_bias"], np.float32).reshape(DJ, 128).T.copy(),
        amask=amask,
    )
    w1 = np.asarray(ins["L1_ffn_moe_w1"], np.float32)
    w2 = np.asarray(ins["L1_ffn_moe_w2"], np.float32)
    for e_ in range(8):
        d["moe1_%d" % e_] = np.ascontiguousarray(
            w1[e_].reshape(8, 128, 2, 8, 128).transpose(2, 3, 1, 0, 4)
            .reshape(16, 128, 1024))
        d["moe2_%d" % e_] = np.ascontiguousarray(
            w2[e_].reshape(2, 8, 128, 2, 512).transpose(3, 0, 1, 2, 4)
            .reshape(2, 16, 128, 512))
    return d


L1FFN_PRE_SPECS = [("f1q_t", [DJ, 128, 1024]), ("wkrep_t", [DJ, 128, 1024]),
                   ("wvrep_t", [DJ, 128, 1024]),
                   ("ln1w", [128, DJ]), ("ln1b", [128, DJ])]
L1FFN_POST_SPECS = ([("wproj_t", [DJ, 128, 1024]), ("wrt_t", [DJ, 128, 8]),
                     ("ln2w", [128, DJ]), ("ln2b", [128, DJ]),
                     ("amask", [128, 128])] +
                    [("moe1_%d" % e_, [16, 128, 1024]) for e_ in range(8)] +
                    [("moe2_%d" % e_, [2, 16, 128, 512]) for e_ in range(8)])
L1FFN_IO_SPECS = L1FFN_PRE_SPECS + L1FFN_POST_SPECS


def emit_l3_ffn(p, ar, banks, io, pfx):
    m0 = ar.mark()
    xn3 = ar.alloc(DJ * NT, BF16)
    wcols = ar.alloc(NTT * 8, F32)
    mln = ar.mark()
    specs = emit_ln_to_fm(
        p, ar, banks, lambda i: io["res_in"][i * 128:(i + 1) * 128, :],
        lambda i, j: xn3[:, j * NT + i * 128: j * NT + i * 128 + 128],
        pfx=pfx + "ln")
    ar.reset(mln)
    wrt = ar.alloc(DJ * 8, BF16)
    rstt = [ar.alloc(24, F32) for _ in range(2)]
    for sem, vv in specs:
        p.wait("gpsimd", sem, vv)
    p.wait("gpsimd", pfx + "ln_nm", p.val(pfx + "ln_nm"))
    for j in range(DJ):
        p.dma("gpsimd", wrt[:, j * 8:(j + 1) * 8], io["wrt3_t"][j], pfx + "rw")
    for sem, vv in specs:
        p.wait("tensor", sem, vv)
    p.wait("tensor", pfx + "rw", p.val(pfx + "rw"))

    def rt_out(i, pbs, pe_ready):
        (pb, w0, w1, rel) = pbs[0]
        st = rstt[i % 2]
        s_rt = pfx + "rt"
        p.wait("vector", pe_ready[0], pe_ready[1])
        if i >= 2:
            p.wait("vector", s_rt, p.val(s_rt))
        p.do("vector", lambda e, pb=pb, st=st: e.tensor_reduce(
            st[:, 0:1], pb[:, 0:8], AX.X, ALU.max), inc=(s_rt, 1))
        p.wait("vector", s_rt, p.val(s_rt))
        p.do("vector", lambda e, pb=pb, st=st: e.tensor_scalar(
            st[:, 8:16], pb[:, 0:8], st[:, 0:1], None, ALU.is_equal,
            ALU.bypass), inc=(s_rt, 1))
        p.wait("vector", s_rt, p.val(s_rt))
        p.do("vector", lambda e, pb=pb, st=st: e.scalar_tensor_tensor(
            st[:, 16:24], st[:, 8:16], -1e30, pb[:, 0:8], ALU.mult, ALU.add),
            inc=rel)
        p.wait("vector", rel[0], p.val(rel[0]))
        p.do("vector", lambda e, st=st: e.tensor_reduce(
            st[:, 1:2], st[:, 16:24], AX.X, ALU.max), inc=(s_rt, 1))
        p.wait("vector", s_rt, p.val(s_rt))
        p.do("vector", lambda e, st=st: e.tensor_tensor(
            st[:, 2:3], st[:, 0:1], st[:, 1:2], ALU.subtract), inc=(s_rt, 1))
        p.wait("scalar", s_rt, p.val(s_rt))
        p.do("scalar", lambda e, st=st: e.activation(
            st[:, 3:4], st[:, 2:3], AF.Sigmoid), inc=(s_rt, 1))
        p.wait("vector", s_rt, p.val(s_rt))
        p.do("vector", lambda e, st=st: e.tensor_scalar(
            st[:, 4:5], st[:, 3:4], -1.0, 1.0, ALU.mult, ALU.add),
            inc=(s_rt, 1))
        p.wait("vector", s_rt, p.val(s_rt))
        p.do("vector", lambda e, st=st: e.tensor_scalar(
            st[:, 16:24], st[:, 16:24], st[:, 1:2], None, ALU.is_equal,
            ALU.bypass), inc=(s_rt, 1))
        p.wait("vector", s_rt, p.val(s_rt))
        p.do("vector", lambda e, st=st: e.tensor_scalar(
            st[:, 8:16], st[:, 8:16], st[:, 3:4], None, ALU.mult, ALU.bypass),
            inc=(s_rt, 1))
        p.wait("vector", s_rt, p.val(s_rt))
        p.do("vector", lambda e, st=st, i=i: e.scalar_tensor_tensor(
            wcols[:, i * 8:(i + 1) * 8], st[:, 16:24], st[:, 4:5],
            st[:, 8:16], ALU.mult, ALU.add), inc=(s_rt, 1))
        p.wait("vector", s_rt, p.val(s_rt))

    emit_proj_tm(p, xn3, wrt, rt_out, banks, (4, 5, 6, 7), pfx + "rt",
                 out_w=8, kdj=DJ)

    NQ = 4
    TPQ = NTT // NQ
    acc = ar.alloc(TPQ * 1024, F32)
    h1t = ar.alloc(32 * 1024, BF16)
    NS1, NS2 = 3, 2
    w1r = [ar.alloc(1024, BF16) for _ in range(NS1)]
    w2r = [ar.alloc(512, BF16) for _ in range(NS2)]
    b1all = ar.alloc(256, F32)
    b2sl = [ar.alloc(1024, BF16) for _ in range(2)]
    s_mm = pfx + "mm"
    w1g = [0] * NS1
    w2g = [0] * NS2
    b2g = [0, 0]
    w1n = [0]
    w2n = [0]

    pre = "w1b_0" in io

    def load_w1(e_, u):
        sl = w1n[0] % NS1
        q = ("gpsimd", "sync", "gpsimd")[sl] if pre else "gpsimd"
        s_w = pfx + "wa%d" % sl
        p.wait(q, s_mm, w1g[sl])
        srcs = io[("w1b_%d" if pre else "w1_%d") % e_]
        p.dma(q, w1r[sl][:, :], srcs[u], s_w)
        w1n[0] += 1
        return sl, (s_w, p.val(s_w))

    def load_w2(e_, k, u):
        sl = w2n[0] % NS2
        q = ("gpsimd", "sync")[sl] if pre else "gpsimd"
        s_w = pfx + "wb%d" % sl
        p.wait(q, s_mm, w2g[sl])
        srcs = io[("w2b_%d" if pre else "w2_%d") % e_]
        p.dma(q, w2r[sl][:, :], srcs[k, u], s_w)
        w2n[0] += 1
        return sl, (s_w, p.val(s_w))

    if pre:
        p.wait("gpsimd", "xce", p.val("xce"))
        p.wait("sync", "xce", p.val("xce"))

    p.dma("gpsimd", b1all[:, :], io["b1all"][:, :], pfx + "b1a")
    p.wait("scalar", pfx + "b1a", p.val(pfx + "b1a"))
    p.wait("vector", pfx + "rt_pe", p.val(pfx + "rt_pe"))
    p.wait("sync", pfx + "ln_tr", p.val(pfx + "ln_tr"))
    p.wait("sync", pfx + "ln_nm", p.val(pfx + "ln_nm"))
    p.wait("gpsimd", pfx + "ln_tr", p.val(pfx + "ln_tr"))
    for q in range(NQ):
        if q >= 1:
            p.wait("sync", pfx + "ro", p.val(pfx + "ro"))
        p.dma("sync", acc[:, :].rearrange("p (t c) -> p t c", c=1024),
              io["res_in"][:, :].rearrange("(qq t pp) c -> qq pp t c", pp=128,
                                           t=TPQ)[q], pfx + "al")
        p.wait("vector", pfx + "al", p.val(pfx + "al"))
        c0 = q * TPQ * 128
        for e_ in range(8):
            sb = e_ % 2
            s_b = pfx + "bb%d" % sb
            p.wait("gpsimd", s_mm, b2g[sb])
            p.dma("gpsimd", b2sl[sb][0:1, :], io["b2_%d" % e_][:, :], s_b)
            pend1 = [load_w1(e_, 0), load_w1(e_, 1)]
            pend2 = []
            for u in range(32):
                sl, rdy = pend1.pop(0)
                if u + 2 < 32:
                    pend1.append(load_w1(e_, u + 2))
                elif u == 31:
                    pend2.append(load_w2(e_, 0, 0))
                p.wait("tensor", rdy[0], rdy[1])
                for h in range(2):
                    bid = (2 * u + h) % 4
                    pb = banks[bid]
                    p.bank_acquire("tensor", bid)
                    for j in range(DJ):
                        p.do("tensor", lambda e, pb=pb, sl=sl, j=j, u=u, h=h,
                             c0=c0: e.matmul(
                                 pb[:, :], w1r[sl][:, j * 128:(j + 1) * 128],
                                 xn3[:, j * NT + c0 + h * 512:
                                     j * NT + c0 + h * 512 + 512],
                                 start=(j == 0), stop=(j == DJ - 1)),
                             inc=(s_mm, 1) if j == DJ - 1 else None)
                    p.wait("scalar", s_mm, p.val(s_mm))
                    p.do("scalar", lambda e, pb=pb, u=u, h=h, e_=e_:
                         e.activation(
                             h1t[:, u * 1024 + h * 512: u * 1024 + h * 512 + 512],
                             pb[:, :], AF.Gelu,
                             bias=b1all[:, e_ * 32 + u: e_ * 32 + u + 1],
                             scale=1.0), inc=(p.bk(bid), 1))
                w1g[sl] = p.val(s_mm)
            p.wait("tensor", s_b, p.val(s_b))
            for k in range(2):
                for tt in range(8):
                    p.bank_acquire("tensor", tt)
                for u in range(32):
                    sl, rdy = pend2.pop(0)
                    if u + 1 < 32:
                        pend2.append(load_w2(e_, k, u + 1))
                    elif k == 0:
                        pend2.append(load_w2(e_, 1, 0))
                    p.wait("tensor", rdy[0], rdy[1])
                    for tt in range(8):
                        pb = banks[tt]
                        p.do("tensor", lambda e, pb=pb, sl=sl, u=u, tt=tt:
                             e.matmul(
                                 pb[:, :],
                                 h1t[:, u * 1024 + tt * 128:
                                     u * 1024 + tt * 128 + 128],
                                 w2r[sl][:, :], start=(u == 0), stop=False),
                             inc=(s_mm, 1) if tt == 7 else None)
                    w2g[sl] = p.val(s_mm)
                for tt in range(8):
                    gt = q * TPQ + tt
                    pb = banks[tt]
                    p.do("tensor", lambda e, pb=pb, sb=sb, k=k: e.matmul(
                        pb[:, :], p.c["ones_bf"][0:1, 0:128],
                        b2sl[sb][0:1, k * 512:(k + 1) * 512], start=False,
                        stop=True), inc=(s_mm, 1))
                    p.wait("vector", s_mm, p.val(s_mm))
                    p.do("vector", lambda e, pb=pb, tt=tt, k=k, gt=gt, e_=e_:
                         e.scalar_tensor_tensor(
                             acc[:, tt * 1024 + k * 512: tt * 1024 + k * 512 + 512],
                             pb[:, :], wcols[:, gt * 8 + e_:gt * 8 + e_ + 1],
                             acc[:, tt * 1024 + k * 512: tt * 1024 + k * 512 + 512],
                             ALU.mult, ALU.add), inc=(p.bk(tt), 1))
            b2g[sb] = p.val(s_mm)
        for b in range(8):
            p.wait("sync", p.bk(b), p.val(p.bk(b)))
        p.dma("sync",
              io["res_out"][:, :].rearrange("(qq t pp) c -> qq pp t c",
                                            pp=128, t=TPQ)[q],
              acc[:, :].rearrange("p (t c) -> p t c", c=1024), pfx + "ro")
    p.wait("sync", pfx + "ro", p.val(pfx + "ro"))
    ar.reset(m0)


def prep_l3_ffn(ins):
    wrt = np.ascontiguousarray(np.asarray(ins["L3_ffn_router_w"], np.float32).T)
    w1 = np.asarray(ins["L3_ffn_w1"], np.float32)
    w2 = np.asarray(ins["L3_ffn_w2"], np.float32)
    b1 = np.asarray(ins["L3_ffn_b1"], np.float32)
    b2 = np.asarray(ins["L3_ffn_b2"], np.float32)
    d = dict(wrt3_t=wrt.reshape(DJ, 128, 8))
    d["b1all"] = np.ascontiguousarray(
        np.concatenate([b1[e_].reshape(32, 128).T for e_ in range(8)], axis=1))
    for e_ in range(8):
        w1t = w1[e_].T
        d["w1_%d" % e_] = np.ascontiguousarray(
            w1t.reshape(8, 128, 4, 8, 128).transpose(2, 3, 1, 0, 4)
            .reshape(32, 128, 1024))
        w2t = w2[e_].T
        d["w2_%d" % e_] = np.ascontiguousarray(
            w2t.reshape(4, 8, 128, 2, 512).transpose(3, 0, 1, 2, 4)
            .reshape(2, 32, 128, 512))
        d["b2_%d" % e_] = b2[e_].reshape(1, 1024).copy()
    return d


L3FFN_IO_SPECS = ([("wrt3_t", [DJ, 128, 8]), ("b1all", [128, 256])] +
                  [("w1_%d" % e_, [32, 128, 1024]) for e_ in range(8)] +
                  [("w2_%d" % e_, [2, 32, 128, 512]) for e_ in range(8)] +
                  [("b2_%d" % e_, [1, 1024]) for e_ in range(8)])


def emit_l3_attn(p, ar, banks, io, pfx):
    EPSA = 1e-6
    H = 16
    m0 = ar.mark()

    mpro = ar.mark()
    post = [ar.alloc(D, F32) for _ in range(2)]
    x3t = [ar.alloc(D, F32) for _ in range(2)]

    def pro_cb(i, xt, st, st_ready):
        po, x3 = post[i % 2], x3t[i % 2]
        s_po = pfx + "po%d" % (i % 2)
        s_xo = pfx + "xo%d" % (i % 2)
        if i >= 2:
            p.wait("sync", pfx + "pv", p.val(pfx + "pv"))
        p.dma("sync", po[:, :], io["pos_d"][i * 128:(i + 1) * 128, :], s_po)
        p.wait("vector", st_ready[0], st_ready[1])
        p.wait("vector", s_po, p.val(s_po))
        if i >= 2:
            p.wait("vector", s_xo, p.val(s_xo))
        p.do("vector", lambda e, st=st: e.tensor_scalar(
            st[:, 5:6], st[:, 2:3], 32.0, None, ALU.mult, ALU.bypass),
            inc=(pfx + "pv", 1))
        p.wait("vector", pfx + "pv", p.val(pfx + "pv"))
        p.do("vector", lambda e, xt=xt, st=st, x3=x3: e.tensor_scalar(
            x3[:, :], xt[:, :], st[:, 0:1], st[:, 5:6], ALU.subtract,
            ALU.mult), inc=(pfx + "pv", 1))
        p.wait("vector", pfx + "pv", p.val(pfx + "pv"))
        p.do("vector", lambda e, x3=x3, po=po: e.tensor_tensor(
            x3[:, :], x3[:, :], po[:, :], ALU.add), inc=(pfx + "pv", 1))
        p.wait("sync", pfx + "pv", p.val(pfx + "pv"))
        p.dma("sync", io["x3_d"][i * 128:(i + 1) * 128, :], x3[:, :], s_xo)
        return [(pfx + "pv", p.val(pfx + "pv"))]

    emit_ln_tiles(p, ar, lambda i: io["res_in"][i * 128:(i + 1) * 128, :],
                  NTT, pro_cb, pfx=pfx + "pl")
    ar.reset(mpro)
    for s in range(2):
        p.wait("sync", pfx + "xo%d" % s, p.val(pfx + "xo%d" % s))
        p.wait("gpsimd", pfx + "xo%d" % s, p.val(pfx + "xo%d" % s))

    fst = dict(w1g=[0, 0, 0], w2g=[0, 0], w1n=0, w2n=0)
    xcv = [0, 0, 0, 0]
    if "w13b" in io:
        p.dma("gpsimd", io["w13b"][0], io["w13"][0], "xca")
        for k in range(2):
            p.dma("gpsimd", io["w23b"][0][k], io["w23"][0][k], "xca")
        xcv[0] = p.val("xca")
    for L in range(4):
        lp = pfx + "Y"
        m1 = ar.mark()
        wc = ar.alloc(DJ, F32)
        bc = ar.alloc(DJ, F32)
        p.wait("gpsimd", pfx + "pv", p.val(pfx + "pv"))
        if L > 0:
            pass
        p.dma("gpsimd", wc[:, :], io["ln1s"][L], lp + "lw")
        p.dma("gpsimd", bc[:, :], io["ln1b"][L], lp + "lw")
        p.wait("vector", lp + "lw", p.val(lp + "lw"))
        ln1 = ar.alloc(DJ * NT, BF16)
        mln = ar.mark()
        specs = emit_ln_to_fm(
            p, ar, banks, lambda i: io["x3_d"][i * 128:(i + 1) * 128, :],
            lambda i, j: ln1[:, j * NT + i * 128: j * NT + i * 128 + 128],
            scale_bias=(wc, bc), pfx=lp + "n1")
        ar.reset(mln)
        wslot = ar.alloc(DJ * 1040, BF16)
        brow = ar.alloc(1040, BF16, parts=1)
        m_stg = ar.mark()
        kst = [ar.alloc(1024, BF16) for _ in range(2)]
        kfa = ar.alloc(1024, F32)
        kfb = ar.alloc(1024, F32)
        for sem, vv in specs:
            p.wait("gpsimd", sem, vv)
        p.wait("gpsimd", lp + "n1_nm", p.val(lp + "n1_nm"))

        for j in range(DJ):
            p.dma("gpsimd", wslot[:, j * 1040:j * 1040 + 1024], io["wk3"][L][j],
                  lp + "wk")
        p.dma("gpsimd", brow[0:1, 0:1024], io["bk3"][L][:, :], lp + "wk")
        p.wait("tensor", lp + "wk", p.val(lp + "wk"))
        for sem, vv in specs:
            p.wait("tensor", sem, vv)

        def k_out(i, pbs, pe_ready, lp=lp):
            ks = kst[i % 2]
            s_ks = lp + "ks%d" % (i % 2)
            if i >= 2:
                p.wait("scalar", s_ks, p.val(s_ks))
            p.wait("scalar", pe_ready[0], pe_ready[1])
            p.wait("vector", pe_ready[0], pe_ready[1])
            for pb, w0, w1, rel in pbs:
                p.wait("scalar", lp + "kv", p.val(lp + "kv"))
                p.do("scalar", lambda e, pb=pb, w0=w0, w1=w1: e.activation(
                    kfa[:, w0:w1], pb[:, 0:w1 - w0], AF.Relu),
                    inc=(lp + "ke", 1))
                p.wait("vector", lp + "ke", p.val(lp + "ke"))
                p.do("vector", lambda e, pb=pb, w0=w0, w1=w1: e.tensor_scalar(
                    kfb[:, w0:w1], pb[:, 0:w1 - w0], 0.0, None, ALU.min,
                    ALU.bypass), inc=rel)
                p.wait("scalar", rel[0], p.val(rel[0]))
                p.wait("scalar", lp + "ke", p.val(lp + "ke"))
                p.do("scalar", lambda e, w0=w0, w1=w1: e.activation(
                    kfb[:, w0:w1], kfb[:, w0:w1], AF.Exp), inc=(lp + "ke", 1))
                p.wait("vector", lp + "ke", p.val(lp + "ke"))
                p.do("vector", lambda e, ks=ks, w0=w0, w1=w1: e.tensor_tensor(
                    ks[:, w0:w1], kfa[:, w0:w1], kfb[:, w0:w1], ALU.add),
                    inc=(lp + "kv", 1))
            p.wait("sync", lp + "kv", p.val(lp + "kv"))
            p.dma("sync", io["k3_d"][i], ks[:, :], s_ks)

        emit_proj_tm(p, ln1, wslot, k_out, banks, (0, 1, 2, 3), lp + "k",
                     out_w=1024, kdj=DJ, erow=brow[0:1, 0:1024], w_stride=1040)
        for s in range(2):
            p.wait("gpsimd", lp + "ks%d" % s, p.val(lp + "ks%d" % s))
        p.wait("gpsimd", lp + "k_pe", p.val(lp + "k_pe"))

        vst = [ar.alloc(1040, BF16) for _ in range(2)]
        for j in range(DJ):
            p.dma("gpsimd", wslot[:, j * 1040:(j + 1) * 1040],
                  io["wv3aug"][L][j], lp + "wv")
        p.dma("gpsimd", brow[:, :], io["bv3erow"][L][:, :], lp + "wv")
        p.wait("tensor", lp + "wv", p.val(lp + "wv"))

        def v_out(i, pbs, pe_ready, lp=lp):
            vs = vst[i % 2]
            s_vs = lp + "vs%d" % (i % 2)
            if i >= 2:
                p.wait("vector", s_vs, p.val(s_vs))
            p.wait("vector", pe_ready[0], pe_ready[1])
            rels = []
            for pb, w0, w1, rel in pbs:
                p.do("vector", lambda e, pb=pb, vs=vs, w0=w0, w1=w1:
                     e.tensor_copy(vs[:, w0:w1], pb[:, 0:w1 - w0]), inc=rel)
                rels.append((rel[0], p.val(rel[0])))
            for sem, vv in rels:
                p.wait("sync", sem, vv)
            p.dma("sync", io["v3_d"][i], vs[:, :], s_vs)

        emit_proj_tm(p, ln1, wslot, v_out, banks, (0, 1, 2, 3, 4, 5), lp + "v",
                     out_w=1040, kdj=DJ, erow=brow, w_stride=1040)
        for s in range(2):
            p.wait("gpsimd", lp + "vs%d" % s, p.val(lp + "vs%d" % s))
        p.wait("gpsimd", lp + "v_pe", p.val(lp + "v_pe"))

        for j in range(DJ):
            p.dma("gpsimd", wslot[:, j * 1040:j * 1040 + 1024], io["wq3"][L][j],
                  lp + "wq")
        p.dma("gpsimd", brow[0:1, 0:1024], io["bq3"][L][:, :], lp + "wq")
        p.wait("tensor", lp + "wq", p.val(lp + "wq"))
        qst = [ar.alloc(512, BF16) for _ in range(3)]
        nq = [0]
        for h in range(H):
            for tc in range(8):
                kq = nq[0]
                bid = kq % 2
                pb = banks[bid]
                qs = qst[kq % 3]
                s_qs = lp + "qs%d" % (kq % 3)
                p.bank_acquire("tensor", bid)
                for j in range(DJ):
                    p.do("tensor", lambda e, pb=pb, j=j, h=h, tc=tc: e.matmul(
                        pb[0:64, :],
                        wslot[:, j * 1040 + h * 64: j * 1040 + h * 64 + 64],
                        ln1[:, j * NT + tc * 512: j * NT + (tc + 1) * 512],
                        start=(j == 0), stop=False))
                p.do("tensor", lambda e, pb=pb, h=h: e.matmul(
                    pb[0:64, :], brow[0:1, h * 64:h * 64 + 64],
                    p.c["ones_bf"][0:1, 0:512], start=False, stop=True),
                    inc=(lp + "qm", 1))
                if kq >= 3:
                    p.wait("scalar", s_qs, p.val(s_qs))
                p.wait("scalar", lp + "qm", p.val(lp + "qm"))
                p.wait("vector", lp + "qm", p.val(lp + "qm"))
                p.wait("scalar", lp + "qv", p.val(lp + "qv"))
                p.do("scalar", lambda e, pb=pb: e.activation(
                    kfa[0:64, 0:512], pb[0:64, :], AF.Relu), inc=(lp + "qe", 1))
                p.wait("vector", lp + "qe", p.val(lp + "qe"))
                p.do("vector", lambda e, pb=pb: e.tensor_scalar(
                    kfb[0:64, 0:512], pb[0:64, :], 0.0, None, ALU.min,
                    ALU.bypass), inc=(p.bk(bid), 1))
                p.wait("scalar", p.bk(bid), p.val(p.bk(bid)))
                p.wait("scalar", lp + "qe", p.val(lp + "qe"))
                p.do("scalar", lambda e: e.activation(
                    kfb[0:64, 0:512], kfb[0:64, 0:512], AF.Exp),
                    inc=(lp + "qe", 1))
                p.wait("vector", lp + "qe", p.val(lp + "qe"))
                p.do("vector", lambda e, qs=qs: e.tensor_tensor(
                    qs[0:64, :], kfa[0:64, 0:512], kfb[0:64, 0:512], ALU.add),
                    inc=(lp + "qv", 1))
                p.wait("sync", lp + "qv", p.val(lp + "qv"))
                p.dma("sync", io["q3_d"][h, :, tc * 512:(tc + 1) * 512],
                      qs[0:64, :], s_qs)
                nq[0] += 1
        for s in range(3):
            p.wait("sync", lp + "qs%d" % s, p.val(lp + "qs%d" % s))

        for s in range(3):
            p.wait("sync", lp + "qs%d" % s, p.val(lp + "qs%d" % s))
        for s in range(2):
            p.wait("sync", lp + "ks%d" % s, p.val(lp + "ks%d" % s))
            p.wait("sync", lp + "vs%d" % s, p.val(lp + "vs%d" % s))
        p.wait("sync", lp + "qv", p.val(lp + "qv"))
        p.wait("sync", lp + "kv", p.val(lp + "kv"))
        p.wait("sync", lp + "qe", p.val(lp + "qe"))
        p.wait("sync", lp + "ke", p.val(lp + "ke"))
        ar.reset(m_stg)
        ar2 = m_stg
        kv_sb = ar.alloc(16 * 65, BF16, parts=64)
        ksl = [ar.alloc(1024, BF16) for _ in range(2)]
        vsl = [ar.alloc(1040, BF16) for _ in range(2)]
        for hp in range(2):
            for b in range(8):
                p.bank_acquire("tensor", b)
            for i in range(NTT):
                kk, vv2 = ksl[i % 2], vsl[i % 2]
                s_sl = lp + "c%d" % (i % 2)
                if i >= 2 or hp == 1:
                    p.wait("sync", lp + "cm", p.val(lp + "cm"))
                p.dma("sync", kk[:, :], io["k3_d"][i], s_sl)
                p.dma("sync", vv2[:, :], io["v3_d"][i], s_sl)
                p.wait("tensor", s_sl, p.val(s_sl))
                for h8 in range(8):
                    h = hp * 8 + h8
                    p.do("tensor", lambda e, kk=kk, vv2=vv2, h=h, i=i, h8=h8:
                         e.matmul(
                             banks[h8][0:64, 0:65],
                             kk[:, h * 64:(h + 1) * 64],
                             vv2[:, h * 65:(h + 1) * 65],
                             start=(i == 0), stop=(i == NTT - 1)),
                         inc=(lp + "cm", 1) if h8 == 7 else None)
            p.wait("vector", lp + "cm", p.val(lp + "cm"))
            for h8 in range(8):
                h = hp * 8 + h8
                p.do("vector", lambda e, h=h, h8=h8: e.tensor_copy(
                    kv_sb[:, h * 65:(h + 1) * 65], banks[h8][0:64, 0:65]),
                    inc=(p.bk(h8), 1))
        for b in range(8):
            p.wait("tensor", p.bk(b), p.val(p.bk(b)))

        at3 = ln1
        p.wait("vector", lp + "qm", p.val(lp + "qm"))
        p.wait("sync", lp + "qm", p.val(lp + "qm"))
        qh2 = [ar.alloc(NT, BF16, parts=64) for _ in range(2)]
        tmp64 = ar.alloc(512, BF16, parts=64)
        rl = ar.alloc(512, F32, parts=1)
        bl = ar.alloc(512, F32, parts=64)
        for h in range(H):
            qh = qh2[h % 2]
            s_qh = lp + "qh%d" % (h % 2)
            if h >= 2:
                p.wait("sync", lp + "am", p.val(lp + "am"))
            p.dma("sync", qh[:, :], io["q3_d"][h], s_qh)
            p.wait("tensor", s_qh, p.val(s_qh))
            for qc in range(8):
                p.bank_acquire("tensor", 6)
                p.do("tensor", lambda e, qh=qh, h=h, qc=qc: e.matmul(
                    banks[6][0:65, :], kv_sb[:, h * 65:(h + 1) * 65],
                    qh[:, qc * 512:(qc + 1) * 512], start=True, stop=True),
                    inc=(lp + "am", 1))
                p.wait("vector", lp + "am", p.val(lp + "am"))
                p.wait("vector", lp + "bm", p.val(lp + "bm"))
                p.do("vector", lambda e: e.tensor_scalar(
                    rl[:, :], banks[6][64:65, :], float(EPSA), None, ALU.add,
                    ALU.bypass), inc=(lp + "rl", 1))
                p.wait("vector", lp + "rl", p.val(lp + "rl"))
                p.do("vector", lambda e: e.reciprocal(rl[:, :], rl[:, :]),
                     inc=(lp + "rl", 1))
                p.wait("tensor", lp + "rl", p.val(lp + "rl"))
                p.bank_acquire("tensor", 7)
                p.do("tensor", lambda e: e.matmul(
                    banks[7][0:64, :], p.c["ones_f"][0:1, 0:64], rl[:, :],
                    start=True, stop=True), inc=(lp + "bm", 1))
                p.wait("scalar", lp + "bm", p.val(lp + "bm"))
                p.wait("scalar", p.bk(6), p.val(p.bk(6)))
                p.do("scalar", lambda e: e.copy(bl[:, :], banks[7][0:64, :]),
                     inc=(p.bk(7), 1))
                p.wait("vector", p.bk(7), p.val(p.bk(7)))
                col0 = (h // 2) * NT + qc * 512
                if h % 2 == 0:
                    p.do("vector", lambda e, col0=col0: e.tensor_tensor(
                        at3[0:64, col0:col0 + 512], banks[6][0:64, :],
                        bl[:, :], ALU.mult), inc=(p.bk(6), 1))
                else:
                    if p.val(lp + "od"):
                        p.wait("vector", lp + "od", p.val(lp + "od"))
                    p.do("vector", lambda e: e.tensor_tensor(
                        tmp64[:, :], banks[6][0:64, :], bl[:, :], ALU.mult),
                        inc=(p.bk(6), 1))
                    p.wait("sync", p.bk(6), p.val(p.bk(6)))
                    p.dma("sync", at3[64:128, col0:col0 + 512], tmp64[:, :],
                          lp + "od")
        at_fin = [(p.bk(6), p.val(p.bk(6))), (lp + "od", p.val(lp + "od"))]

        s_w2 = lp + "wo"
        p.wait("gpsimd", lp + "am", p.val(lp + "am"))
        p.wait("gpsimd", lp + "qm", p.val(lp + "qm"))
        for j in range(DJ):
            p.dma("gpsimd", wslot[:, j * 1040:j * 1040 + 1024], io["wo3"][L][j],
                  s_w2)
        p.dma("gpsimd", brow[0:1, 0:1024], io["bo3"][L][:, :], s_w2)
        p.wait("tensor", s_w2, p.val(s_w2))
        for sem, vv in at_fin:
            p.wait("tensor", sem, vv)
        x3a = [ar.alloc(D, F32) for _ in range(2)]
        x3o = [ar.alloc(D, F32) for _ in range(2)]
        rg = {}
        for i in range(NTT):
            xi, xo = x3a[i % 2], x3o[i % 2]
            s_xl = lp + "xl%d" % (i % 2)
            s_xs = lp + "xs%d" % (i % 2)
            if i >= 2:
                for sem, vv in rg[i - 2]:
                    p.wait("sync", sem, vv)
            p.dma("sync", xi[:, :], io["x3_d"][i * 128:(i + 1) * 128, :], s_xl)
            g4 = 4 * (i % 2)
            for k in range(2):
                bid_e = g4 + k
                pbe = banks[bid_e]
                p.bank_acquire("tensor", bid_e)
                for n, h in enumerate(range(0, H, 2)):
                    p.do("tensor", lambda e, pbe=pbe, h=h, i=i, k=k, n=n:
                         e.matmul(
                             pbe[:, :],
                             at3[0:64, (h // 2) * NT + i * 128:(h // 2) * NT + i * 128 + 128],
                             wslot[0:64, (h // 2) * 1040 + k * 512:(h // 2) * 1040 + k * 512 + 512],
                             start=(n == 0), stop=False))
                p.do("tensor", lambda e, pbe=pbe, k=k: e.matmul(
                    pbe[:, :], p.c["ones_bf"][0:1, 0:128],
                    brow[0:1, k * 512:(k + 1) * 512], start=False, stop=True),
                    inc=(lp + "pw", 1))
                bid_o = g4 + 2 + k
                pbo = banks[bid_o]
                p.bank_acquire("tensor", bid_o)
                for n, h in enumerate(range(1, H, 2)):
                    p.do("tensor", lambda e, pbo=pbo, h=h, i=i, k=k, n=n:
                         e.matmul(
                             pbo[:, :],
                             at3[64:128, (h // 2) * NT + i * 128:(h // 2) * NT + i * 128 + 128],
                             wslot[64:128, (h // 2) * 1040 + k * 512:(h // 2) * 1040 + k * 512 + 512],
                             start=(n == 0), stop=(n == H // 2 - 1)),
                         inc=(lp + "pw", 1) if n == H // 2 - 1 else None)
            p.wait("vector", lp + "pw", p.val(lp + "pw"))
            p.wait("vector", s_xl, p.val(s_xl))
            if i >= 2:
                p.wait("vector", s_xs, p.val(s_xs))
            specs_i = []
            for k in range(2):
                bid_e, bid_o = g4 + k, g4 + 2 + k
                p.do("vector", lambda e, pb=banks[bid_e], xi=xi, xo=xo, k=k:
                     e.tensor_tensor(xo[:, k * 512:(k + 1) * 512], pb[:, :],
                                     xi[:, k * 512:(k + 1) * 512], ALU.add),
                     inc=(p.bk(bid_e), 1))
                specs_i.append((p.bk(bid_e), p.val(p.bk(bid_e))))
                p.wait("vector", p.bk(bid_e), p.val(p.bk(bid_e)))
                p.do("vector", lambda e, pb=banks[bid_o], xo=xo, k=k:
                     e.tensor_tensor(xo[:, k * 512:(k + 1) * 512],
                                     xo[:, k * 512:(k + 1) * 512], pb[:, :],
                                     ALU.add), inc=(p.bk(bid_o), 1))
                specs_i.append((p.bk(bid_o), p.val(p.bk(bid_o))))
            rg[i] = specs_i
            for sem, vv in specs_i:
                p.wait("sync", sem, vv)
            p.dma("sync", io["x3_d"][i * 128:(i + 1) * 128, :], xo[:, :], s_xs)
        for s in range(2):
            p.wait("sync", lp + "xs%d" % s, p.val(lp + "xs%d" % s))

        ar.reset(ar2)
        p.wait("gpsimd", lp + "pw", p.val(lp + "pw"))
        for s in range(2):
            p.wait("gpsimd", lp + "xs%d" % s, p.val(lp + "xs%d" % s))
            p.wait("sync", lp + "xs%d" % s, p.val(lp + "xs%d" % s))
        w2c2 = ar.alloc(DJ, F32)
        b2c2 = ar.alloc(DJ, F32)
        p.dma("gpsimd", w2c2[:, :], io["ln2s"][L], lp + "l2w")
        p.dma("gpsimd", b2c2[:, :], io["l3n2b"][L], lp + "l2w")
        p.wait("vector", lp + "l2w", p.val(lp + "l2w"))
        ln2 = ln1
        p.wait("sync", lp + "qm", p.val(lp + "qm"))
        mln2 = ar.mark()
        specs2 = emit_ln_to_fm(
            p, ar, banks, lambda i: io["x3_d"][i * 128:(i + 1) * 128, :],
            lambda i, j: ln2[:, j * NT + i * 128: j * NT + i * 128 + 128],
            scale_bias=(w2c2, b2c2), pfx=lp + "n2")
        ar.reset(mln2)
        w1r = [ar.alloc(1024, BF16) for _ in range(3)]
        w2r = [ar.alloc(512, BF16) for _ in range(2)]
        h1t = ar.alloc(32 * 1024, BF16)
        b1sb = ar.alloc(32, F32)
        b2row = ar.alloc(1024, BF16, parts=1)
        xf = [ar.alloc(512, F32) for _ in range(2)]
        xg = [ar.alloc(512, F32) for _ in range(2)]
        for sem, vv in specs2:
            p.wait("gpsimd", sem, vv)
        p.wait("gpsimd", lp + "n2_nm", p.val(lp + "n2_nm"))
        p.dma("gpsimd", b1sb[:, :], io["b13"][L], lp + "fb")
        p.dma("gpsimd", b2row[:, :], io["b23"][L][:, :], lp + "fb")
        p.wait("scalar", lp + "fb", p.val(lp + "fb"))
        p.wait("tensor", lp + "fb", p.val(lp + "fb"))
        for sem, vv in specs2:
            p.wait("tensor", sem, vv)
        s_mm = pfx + "fm"

        fpre = "w13b" in io

        def load_w1f(u):
            sl = fst["w1n"] % 3
            q = ("gpsimd", "sync", "gpsimd")[sl] if fpre else "gpsimd"
            s_w = pfx + "fa%d" % sl
            p.wait(q, s_mm, fst["w1g"][sl])
            p.dma(q, w1r[sl][:, :],
                  io["w13b" if fpre else "w13"][L][u], s_w)
            fst["w1n"] += 1
            return sl, (s_w, p.val(s_w))

        def load_w2f(k, u):
            sl = fst["w2n"] % 2
            q = ("gpsimd", "sync")[sl] if fpre else "gpsimd"
            s_w = pfx + "fc%d" % sl
            p.wait(q, s_mm, fst["w2g"][sl])
            p.dma(q, w2r[sl][:, :],
                  io["w23b" if fpre else "w23"][L][k, u], s_w)
            fst["w2n"] += 1
            return sl, (s_w, p.val(s_w))

        if fpre:
            xs = "xca" if L % 2 == 0 else "xcb"
            p.wait("gpsimd", xs, xcv[L])
            p.wait("sync", xs, xcv[L])
            for sem, vv in specs2:
                p.wait("sync", sem, vv)
            p.wait("sync", lp + "n2_nm", p.val(lp + "n2_nm"))

        for ch in range(4):
            c0 = ch * 1024
            pend1 = [load_w1f(0), load_w1f(1)]
            pend2 = []
            for u in range(32):
                sl, rdy = pend1.pop(0)
                if u + 2 < 32:
                    pend1.append(load_w1f(u + 2))
                elif u == 31:
                    pend2.append(load_w2f(0, 0))
                p.wait("tensor", rdy[0], rdy[1])
                for h in range(2):
                    bid = (2 * u + h) % 4
                    pb = banks[bid]
                    p.bank_acquire("tensor", bid)
                    for j in range(DJ):
                        p.do("tensor", lambda e, pb=pb, sl=sl, j=j, c0=c0, h=h:
                             e.matmul(
                                 pb[:, :], w1r[sl][:, j * 128:(j + 1) * 128],
                                 ln2[:, j * NT + c0 + h * 512:
                                     j * NT + c0 + h * 512 + 512],
                                 start=(j == 0), stop=(j == DJ - 1)),
                             inc=(s_mm, 1) if j == DJ - 1 else None)
                    p.wait("scalar", s_mm, p.val(s_mm))
                    p.do("scalar", lambda e, pb=pb, u=u, h=h: e.activation(
                        h1t[:, u * 1024 + h * 512: u * 1024 + h * 512 + 512],
                        pb[:, :], AF.Relu, bias=b1sb[:, u:u + 1], scale=1.0),
                        inc=(p.bk(bid), 1))
                fst["w1g"][sl] = p.val(s_mm)
            for k in range(2):
                for tt in range(8):
                    p.bank_acquire("tensor", tt)
                for u in range(32):
                    sl, rdy = pend2.pop(0)
                    if u + 1 < 32:
                        pend2.append(load_w2f(k, u + 1))
                    elif k == 0:
                        pend2.append(load_w2f(1, 0))
                    p.wait("tensor", rdy[0], rdy[1])
                    for tt in range(8):
                        pb = banks[tt]
                        p.do("tensor", lambda e, pb=pb, sl=sl, u=u, tt=tt:
                             e.matmul(
                                 pb[:, :],
                                 h1t[:, u * 1024 + tt * 128:
                                     u * 1024 + tt * 128 + 128],
                                 w2r[sl][:, :], start=(u == 0), stop=False),
                             inc=(s_mm, 1) if tt == 7 else None)
                    fst["w2g"][sl] = p.val(s_mm)
                for tt in range(8):
                    i = ch * 8 + tt
                    sx = tt % 2
                    xi, xo = xf[sx], xg[sx]
                    s_xl = lp + "fl%d" % sx
                    s_xs = lp + "fs%d" % sx
                    if tt >= 2:
                        p.wait("sync", p.bk(tt - 2), p.val(p.bk(tt - 2)))
                    p.dma("sync", xi[:, :],
                          io["x3_d"][i * 128:(i + 1) * 128,
                                     k * 512:(k + 1) * 512], s_xl)
                    pb = banks[tt]
                    p.do("tensor", lambda e, pb=pb, k=k: e.matmul(
                        pb[:, :], p.c["ones_bf"][0:1, 0:128],
                        b2row[0:1, k * 512:(k + 1) * 512], start=False,
                        stop=True), inc=(s_mm, 1))
                    p.wait("vector", s_mm, p.val(s_mm))
                    p.wait("vector", s_xl, p.val(s_xl))
                    p.wait("vector", s_xs, p.val(s_xs))
                    p.do("vector", lambda e, pb=pb, xi=xi, xo=xo:
                         e.tensor_tensor(xo[:, :], pb[:, :], xi[:, :],
                                         ALU.add), inc=(p.bk(tt), 1))
                    p.wait("sync", p.bk(tt), p.val(p.bk(tt)))
                    p.dma("sync",
                          io["x3_d"][i * 128:(i + 1) * 128,
                                     k * 512:(k + 1) * 512], xo[:, :], s_xs)
        if fpre and L < 3:
            xs2 = "xca" if (L + 1) % 2 == 0 else "xcb"
            p.dma("gpsimd", io["w13b"][L + 1], io["w13"][L + 1], xs2)
            for k in range(2):
                p.dma("gpsimd", io["w23b"][L + 1][k], io["w23"][L + 1][k],
                      xs2)
            xcv[L + 1] = p.val(xs2)
        if fpre and "w1b_0" in io:
            for e_ in (2 * L, 2 * L + 1):
                p.dma("gpsimd", io["w1b_%d" % e_][:, :, :],
                      io["w1_%d" % e_][:, :, :], "xce")
                for k in range(2):
                    p.dma("gpsimd", io["w2b_%d" % e_][k],
                          io["w2_%d" % e_][k], "xce")
        for s in range(2):
            p.wait("sync", lp + "fs%d" % s, p.val(lp + "fs%d" % s))
            p.wait("gpsimd", lp + "fs%d" % s, p.val(lp + "fs%d" % s))
        for rs in ("fa0", "fa1", "fa2", "fc0", "fc1"):
            for eng in ("vector", "scalar", "sync", "gpsimd"):
                p.wait(eng, pfx + rs, p.val(pfx + rs))
        ar.reset(m1)
        p.wait("sync", s_mm, p.val(s_mm))
        p.wait("gpsimd", s_mm, p.val(s_mm))

    sbc = ar.alloc(D, F32)
    bbc = ar.alloc(D, F32)
    srow = ar.alloc(1024, F32, parts=1)
    brw = ar.alloc(1024, F32, parts=1)
    p.dma("gpsimd", srow[:, :], io["lnfs"][:, :], pfx + "fw")
    p.dma("gpsimd", brw[:, :], io["lnfb"][:, :], pfx + "fw")
    p.wait("tensor", pfx + "fw", p.val(pfx + "fw"))
    for k in range(2):
        p.bank_acquire("tensor", 4 + k)
        p.do("tensor", lambda e, k=k: e.matmul(
            banks[4 + k][:, :], p.c["ones_f"][0:1, 0:128],
            srow[0:1, k * 512:(k + 1) * 512], start=True, stop=True),
            inc=(pfx + "bm", 1))
        p.wait("vector", pfx + "bm", p.val(pfx + "bm"))
        p.do("vector", lambda e, k=k: e.tensor_copy(
            sbc[:, k * 512:(k + 1) * 512], banks[4 + k][:, :]),
            inc=(p.bk(4 + k), 1))
        p.bank_acquire("tensor", 4 + k)
        p.do("tensor", lambda e, k=k: e.matmul(
            banks[4 + k][:, :], p.c["ones_f"][0:1, 0:128],
            brw[0:1, k * 512:(k + 1) * 512], start=True, stop=True),
            inc=(pfx + "bm", 1))
        p.wait("vector", pfx + "bm", p.val(pfx + "bm"))
        p.do("vector", lambda e, k=k: e.tensor_copy(
            bbc[:, k * 512:(k + 1) * 512], banks[4 + k][:, :]),
            inc=(p.bk(4 + k), 1))
    p.wait("vector", p.bk(4), p.val(p.bk(4)))
    p.wait("vector", p.bk(5), p.val(p.bk(5)))
    rsl = [ar.alloc(D, F32) for _ in range(2)]
    rso = [ar.alloc(D, F32) for _ in range(2)]
    mfin = ar.mark()

    def fin_cb(i, xt, st, st_ready):
        rs, ro = rsl[i % 2], rso[i % 2]
        s_rl = pfx + "gr%d" % (i % 2)
        s_ro = pfx + "go%d" % (i % 2)
        if i >= 2:
            p.wait("sync", pfx + "gv", p.val(pfx + "gv"))
        p.dma("sync", rs[:, :], io["res_in"][i * 128:(i + 1) * 128, :], s_rl)
        p.wait("vector", st_ready[0], st_ready[1])
        p.wait("vector", s_rl, p.val(s_rl))
        if i >= 2:
            p.wait("vector", s_ro, p.val(s_ro))
        p.do("vector", lambda e, xt=xt, st=st, ro=ro: e.tensor_scalar(
            ro[:, :], xt[:, :], st[:, 0:1], st[:, 2:3], ALU.subtract,
            ALU.mult), inc=(pfx + "gv", 1))
        p.wait("vector", pfx + "gv", p.val(pfx + "gv"))
        p.do("vector", lambda e, ro=ro: e.tensor_tensor(
            ro[:, :], ro[:, :], sbc[:, :], ALU.mult), inc=(pfx + "gv", 1))
        p.wait("vector", pfx + "gv", p.val(pfx + "gv"))
        p.do("vector", lambda e, ro=ro: e.tensor_tensor(
            ro[:, :], ro[:, :], bbc[:, :], ALU.add), inc=(pfx + "gv", 1))
        p.wait("vector", pfx + "gv", p.val(pfx + "gv"))
        p.do("vector", lambda e, ro=ro, rs=rs: e.tensor_tensor(
            ro[:, :], ro[:, :], rs[:, :], ALU.add), inc=(pfx + "gv", 1))
        p.wait("sync", pfx + "gv", p.val(pfx + "gv"))
        p.dma("sync", io["res_out"][i * 128:(i + 1) * 128, :], ro[:, :], s_ro)
        return [(pfx + "gv", p.val(pfx + "gv"))]

    emit_ln_tiles(p, ar, lambda i: io["x3_d"][i * 128:(i + 1) * 128, :],
                  NTT, fin_cb, pfx=pfx + "gl")
    ar.reset(mfin)
    for s in range(2):
        p.wait("sync", pfx + "go%d" % s, p.val(pfx + "go%d" % s))
    ar.reset(m0)


def prep_l3_attn(ins):
    import math
    d = {}
    for nm in ("Wq", "Wk", "Wv", "Wo"):
        W = np.asarray(ins["L3_attn_" + nm], np.float32)
        if nm == "Wv":
            d["wv3aug"] = np.stack([make_vaug(np.ascontiguousarray(W[i]))
                                    for i in range(4)])
        else:
            key = {"Wq": "wq3", "Wk": "wk3", "Wo": "wo3"}[nm]
            d[key] = np.ascontiguousarray(W.reshape(4, DJ, 128, 1024))
    d["bv3erow"] = np.stack([make_erow(np.asarray(ins["L3_attn_bv"][i], np.float32))
                             for i in range(4)])
    d["bq3"] = np.asarray(ins["L3_attn_bq"], np.float32).reshape(4, 1, 1024)
    d["bk3"] = np.asarray(ins["L3_attn_bk"], np.float32).reshape(4, 1, 1024)
    d["bo3"] = np.asarray(ins["L3_attn_bo"], np.float32).reshape(4, 1, 1024)
    d["w13"] = np.ascontiguousarray(
        np.asarray(ins["L3_attn_W1"], np.float32)
        .reshape(4, 8, 128, 4, 8, 128).transpose(0, 3, 4, 2, 1, 5)
        .reshape(4, 32, 128, 1024))
    d["w23"] = np.ascontiguousarray(
        np.asarray(ins["L3_attn_W2"], np.float32)
        .reshape(4, 4, 8, 128, 2, 512).transpose(0, 4, 1, 2, 3, 5)
        .reshape(4, 2, 32, 128, 512))
    d["b13"] = np.ascontiguousarray(
        np.asarray(ins["L3_attn_b1"], np.float32).reshape(4, 32, 128)
        .transpose(0, 2, 1))
    d["b23"] = np.asarray(ins["L3_attn_b2"], np.float32).reshape(4, 1, 1024)
    for nm, key in (("ln1_s", "ln1s"), ("ln1_b", "ln1b"),
                    ("ln2_s", "ln2s"), ("ln2_b", "l3n2b")):
        d[key] = np.ascontiguousarray(
            np.asarray(ins["L3_attn_" + nm], np.float32).reshape(4, DJ, 128)
            .transpose(0, 2, 1))
    d["lnfs"] = np.asarray(ins["L3_attn_lnf_s"], np.float32).reshape(1, 1024)
    d["lnfb"] = np.asarray(ins["L3_attn_lnf_b"], np.float32).reshape(1, 1024)
    half = 512
    freq = np.exp(np.arange(half, dtype=np.float32) * (-math.log(10000.0) / (half - 1)))
    pos = (np.arange(NT, dtype=np.float32) + 1.0)[:, None] * freq[None, :]
    d["pos_d"] = np.concatenate([np.sin(pos), np.cos(pos)], axis=-1).astype(np.float32)
    return d


L3ATTN_IO_SPECS = [("wq3", [4, DJ, 128, 1024]), ("wk3", [4, DJ, 128, 1024]),
                   ("wv3aug", [4, DJ, 128, 1040]), ("wo3", [4, DJ, 128, 1024]),
                   ("bv3erow", [4, 1, 1040]),
                   ("bq3", [4, 1, 1024]), ("bk3", [4, 1, 1024]),
                   ("bo3", [4, 1, 1024]),
                   ("w13", [4, 32, 128, 1024]), ("w23", [4, 2, 32, 128, 512]),
                   ("b13", [4, 128, 32]), ("b23", [4, 1, 1024]),
                   ("ln1s", [4, 128, DJ]), ("ln1b", [4, 128, DJ]),
                   ("ln2s", [4, 128, DJ]), ("l3n2b", [4, 128, DJ]),
                   ("lnfs", [1, 1024]), ("lnfb", [1, 1024]),
                   ("pos_d", [NT, D])]


def _dram_in(nc, name, shp, dt=F32):
    return nc.dram_tensor(name, list(shp), dt, kind="ExternalInput")


def _dram_out(nc, name, shp, dt=F32):
    return nc.dram_tensor(name, list(shp), dt, kind="ExternalOutput")


def _specs_in(nc, io, specs, dt=F32):
    for nm, shp in specs:
        io[nm] = _dram_in(nc, nm, shp, dt)


def _cd(nc):
    return dict(
        ident=_dram_in(nc, "ident", [128, 128], BF16),
        ident_f=_dram_in(nc, "ident_f", [128, 128], F32),
    )


def build_launch_A():
    nc = bass.Bass("TRN2", target_bir_lowering=False)
    stack = ExitStack()
    io = dict(
        res_in=_dram_in(nc, "res_in", [NT, D]),
        res_out=_dram_out(nc, "res_out", [NT, D]),
        qt=nc.dram_tensor("qt_scr", [16, 65, NT], BF16),
        kt=nc.dram_tensor("kt_scr", [16, 65, NT], BF16),
        counts_out=_dram_out(nc, "counts_out", [8, 1]),
    )
    _specs_in(nc, io, ATTN_IO_SPECS)
    _specs_in(nc, io, L0FFN_IO_SPECS)
    cd = _cd(nc)
    p = Prog(nc, stack)
    with stack:
        banks = make_banks(p)
        ar = Arena(p, 176 * 1024)
        emit_const_setup(p, cd)
        emit_attn_causal(p, ar, banks, io, "a0", scale=0.125)
        io2 = dict(io)
        io2["res_in"] = io["res_out"]
        for nm in ("a0rout0", "a0rout1", "a0pw", "a0ra"):
            p.wait("gpsimd", nm, p.val(nm))
        emit_l0_ffn(p, ar, banks, io2, "f0", mode="counts")
        p.finalize()
    return nc


def build_launch_B():
    nc = bass.Bass("TRN2", target_bir_lowering=False)
    stack = ExitStack()
    io = dict(
        res_in=_dram_in(nc, "res_in", [NT, D]),
        counts_all=_dram_in(nc, "counts_all", [8, 8]),
        res_mid=nc.dram_tensor("res_mid", [NT, D], F32),
        res_out=_dram_out(nc, "res_out", [NT, D]),
        k_d=_dram_out(nc, "k_d", [NTT, 128, 1024], BF16),
        v_d=_dram_out(nc, "v_d", [NTT, 128, 1024], BF16),
        q_d=_dram_out(nc, "q_d", [NTT, 128, 1024], BF16),
        hs_d=_dram_out(nc, "hs_d", [NT, D]),
        k1_d=nc.dram_tensor("k1_scr", [NTT, 128, 1024], BF16),
        v1_d=nc.dram_tensor("v1_scr", [NTT, 128, 1024], BF16),
        q1_d=nc.dram_tensor("q1_scr", [DJ, 128, NT], BF16),
    )
    _specs_in(nc, io, L0FFN_IO_SPECS)
    _specs_in(nc, io, L1ATTN_IO_SPECS)
    _specs_in(nc, io, L1FFN_PRE_SPECS)
    cd = _cd(nc)
    p = Prog(nc, stack)
    with stack:
        banks = make_banks(p)
        ar = Arena(p, 176 * 1024)
        emit_const_setup(p, cd)
        iof = dict(io)
        iof["res_out"] = io["res_mid"]
        emit_l0_ffn(p, ar, banks, iof, "f0", mode="full")
        ioa = dict(io)
        ioa["res_in"] = io["res_mid"]
        ioa["res_out"] = io["res_out"]
        ioa["k_d"] = io["k1_d"]
        ioa["v_d"] = io["v1_d"]
        ioa["q_d"] = io["q1_d"]
        for nm in ("f0ro", "f0em"):
            p.wait("gpsimd", nm, p.val(nm))
        emit_l1_attn(p, ar, banks, ioa, "a1")
        iop = dict(io)
        iop["res_in"] = io["res_out"]
        for nm in ("a1rout0", "a1rout1", "a1o_pe", "a1am", "a1ra"):
            p.wait("gpsimd", nm, p.val(nm))
        emit_l1_ffn_pre(p, ar, banks, iop, "p1")
        p.finalize()
    return nc


def build_launch_C():
    nc = bass.Bass("TRN2", target_bir_lowering=False)
    stack = ExitStack()
    io = dict(
        res_in=_dram_in(nc, "res_in", [NT, D]),
        q_d=_dram_in(nc, "q_d", [NTT, 128, 1024], BF16),
        hs_d=_dram_in(nc, "hs_d", [NT, D]),
        kall=_dram_in(nc, "kall", [8, NTT, 128, 1024], BF16),
        vall=_dram_in(nc, "vall", [8, NTT, 128, 1024], BF16),
        ha_d=nc.dram_tensor("ha_d", [NT, D], F32),
        res_mid=nc.dram_tensor("res_mid", [NT, D], F32),
        res_out=_dram_out(nc, "res_out", [NT, D]),
        qt=nc.dram_tensor("qt_scr", [16, 65, NT], BF16),
        kt=nc.dram_tensor("kt_scr", [16, 65, NT], BF16),
    )
    _specs_in(nc, io, L1FFN_POST_SPECS)
    _specs_in(nc, io, ATTN_IO_SPECS)
    cd = _cd(nc)
    p = Prog(nc, stack)
    with stack:
        banks = make_banks(p)
        ar = Arena(p, 176 * 1024)
        emit_const_setup(p, cd)
        iop = dict(io)
        iop["res_out"] = io["res_mid"]
        emit_l1_ffn_post(p, ar, banks, iop, "q1")
        ioa = dict(io)
        ioa["res_in"] = io["res_mid"]
        p.wait("sync", "q1ro", p.val("q1ro"))
        for nm in ("q1ro", "q1mm"):
            p.wait("gpsimd", nm, p.val(nm))
        emit_attn_causal(p, ar, banks, ioa, "a2", scale=0.125)
        p.finalize()
    return nc


def build_launch_D1():
    nc = bass.Bass("TRN2", target_bir_lowering=False)
    stack = ExitStack()
    io = dict(
        res_in=_dram_in(nc, "res_in", [NT, D]),
        res_out=_dram_out(nc, "res_out", [NT, D]),
    )
    _specs_in(nc, io, L2FFN_IO_SPECS)
    cd = _cd(nc)
    p = Prog(nc, stack)
    with stack:
        banks = make_banks(p)
        ar = Arena(p, 176 * 1024)
        emit_const_setup(p, cd)
        emit_l2_ffn(p, ar, banks, io, "f2")
        p.finalize()
    return nc


def build_launch_D2():
    nc = bass.Bass("TRN2", target_bir_lowering=False)
    stack = ExitStack()
    io = dict(
        res_in=_dram_in(nc, "res_in", [NT, D]),
        res_out=_dram_out(nc, "res_out", [NT, D]),
        x3_d=nc.dram_tensor("x3_d", [NT, D], F32),
        k3_d=nc.dram_tensor("k3_d", [NTT, 128, 1024], BF16),
        v3_d=nc.dram_tensor("v3_d", [NTT, 128, 1040], BF16),
        q3_d=nc.dram_tensor("q3_d", [16, 64, NT], BF16),
    )
    _specs_in(nc, io, L3ATTN_IO_SPECS)
    cd = _cd(nc)
    p = Prog(nc, stack)
    with stack:
        banks = make_banks(p)
        ar = Arena(p, 176 * 1024)
        emit_const_setup(p, cd)
        emit_l3_attn(p, ar, banks, io, "a3")
        p.finalize()
    return nc


def build_launch_E():
    nc = bass.Bass("TRN2", target_bir_lowering=False)
    stack = ExitStack()
    io = dict(
        res_in=_dram_in(nc, "res_in", [NT, D]),
        res_out=_dram_out(nc, "res_out", [NT, D]),
    )
    _specs_in(nc, io, L3FFN_IO_SPECS)
    cd = _cd(nc)
    p = Prog(nc, stack)
    with stack:
        banks = make_banks(p)
        ar = Arena(p, 176 * 1024)
        emit_const_setup(p, cd)
        emit_l3_ffn(p, ar, banks, io, "f3")
        p.finalize()
    return nc


def build_launch_CDE(parts=("C", "D1", "D2", "E")):
    nc = bass.Bass("TRN2", target_bir_lowering=False)
    stack = ExitStack()
    io = dict(
        res_in=_dram_in(nc, "res_in", [NT, D]),
        q_d=_dram_in(nc, "q_d", [NTT, 128, 1024], BF16),
        hs_d=_dram_in(nc, "hs_d", [NT, D]),
        kall=_dram_in(nc, "kall", [8, NTT, 128, 1024], BF16),
        vall=_dram_in(nc, "vall", [8, NTT, 128, 1024], BF16),
        ha_d=nc.dram_tensor("ha_d", [NT, D], F32),
        res_mid=nc.dram_tensor("res_mid", [NT, D], F32),
        r5=nc.dram_tensor("r5", [NT, D], F32),
        r6=nc.dram_tensor("r6", [NT, D], F32),
        r7=nc.dram_tensor("r7", [NT, D], F32),
        res_out=_dram_out(nc, "res_out", [NT, D]),
        qt=nc.dram_tensor("qt_scr", [16, 65, NT], BF16),
        kt=nc.dram_tensor("kt_scr", [16, 65, NT], BF16),
        x3_d=nc.dram_tensor("x3_d", [NT, D], F32),
        k3_d=nc.dram_tensor("k3_d", [NTT, 128, 1024], BF16),
        v3_d=nc.dram_tensor("v3_d", [NTT, 128, 1040], BF16),
        q3_d=nc.dram_tensor("q3_d", [16, 64, NT], BF16),
        w13b=nc.dram_tensor("w13b", [4, 32, 128, 1024], BF16),
        w23b=nc.dram_tensor("w23b", [4, 2, 32, 128, 512], BF16),
    )
    for e_ in range(8):
        io["w1b_%d" % e_] = nc.dram_tensor("w1b_%d" % e_, [32, 128, 1024],
                                           BF16)
        io["w2b_%d" % e_] = nc.dram_tensor("w2b_%d" % e_, [2, 32, 128, 512],
                                           BF16)
    for specs in (L1FFN_POST_SPECS, ATTN_IO_SPECS, L2FFN_IO_SPECS,
                  L3ATTN_IO_SPECS, L3FFN_IO_SPECS):
        _specs_in(nc, io, specs)
    cd = _cd(nc)
    p = Prog(nc, stack)
    with stack:
        banks = make_banks(p)
        ar = Arena(p, 176 * 1024)
        emit_const_setup(p, cd)
        for b in range(8):
            p._phys(p.bk(b))
        for nm in ("cst", "cst2", "dma_sync", "dma_gpsimd",
                   "xca", "xcb", "xce"):
            p._phys(nm)
        p.pool_start()
        chain = {"C": io["res_mid"], "D1": io["r6"], "D2": io["r7"],
                 "E": io["res_out"]}
        cur_in = io["res_in"]
        first = True
        for part in parts:
            if not first:
                p.new_epoch()
            first = False
            out = chain[part] if part != parts[-1] else io["res_out"]
            if part == "C":
                iop = dict(io)
                iop["res_in"] = cur_in
                iop["res_out"] = io["res_mid"]
                emit_l1_ffn_post(p, ar, banks, iop, "q1")
                ioa = dict(io)
                ioa["res_in"] = io["res_mid"]
                ioa["res_out"] = out if out is not io["res_mid"] else io["r5"]
                p.wait("sync", "q1ro", p.val("q1ro"))
                for nm in ("q1ro", "q1mm"):
                    p.wait("gpsimd", nm, p.val(nm))
                emit_attn_causal(p, ar, banks, ioa, "a2", scale=0.125)
                cur_in = ioa["res_out"]
            elif part == "D1":
                iof = dict(io)
                iof["res_in"] = cur_in
                iof["res_out"] = out
                emit_l2_ffn(p, ar, banks, iof, "f2")
                cur_in = out
            elif part == "D2":
                iog = dict(io)
                iog["res_in"] = cur_in
                iog["res_out"] = out
                emit_l3_attn(p, ar, banks, iog, "a3")
                cur_in = out
            elif part == "E":
                ioh = dict(io)
                ioh["res_in"] = cur_in
                emit_l3_ffn(p, ar, banks, ioh, "f3")
        p.finalize()
    return nc


_BUILD_CACHE = {}


def _get(name, fn):
    if name not in _BUILD_CACHE:
        _BUILD_CACHE[name] = fn()
    return _BUILD_CACHE[name]


def kernel(**inputs):
    ins = {k: np.asarray(v) for k, v in inputs.items()}
    x = np.asarray(ins["L0_attn_x"], np.float32)
    cst = const_inputs()
    w_l0a = prep_l0_attn(ins)
    w_l0f = [prep_l0_ffn(ins, c) for c in range(NCORES)]
    w_l1a = prep_l1_attn(ins)
    w_l1fpre = prep_l1_ffn_pre(ins)
    w_l1fpost = [prep_l1_ffn_post(ins, c) for c in range(NCORES)]
    w_l2a = prep_l2_attn(ins)
    w_l2f = prep_l2_ffn(ins)
    w_l3a = prep_l3_attn(ins)
    w_l3f = prep_l3_ffn(ins)

    ncA = _get("A", build_launch_A)
    in_maps = [dict(res_in=np.ascontiguousarray(x[c]), **w_l0a, **w_l0f[c],
                    **cst) for c in range(NCORES)]
    rA = run_bass_kernel_spmd(ncA, in_maps, core_ids=list(range(NCORES)))
    res1 = [np.asarray(rA.results[c]["res_out"]) for c in range(NCORES)]
    counts = [np.asarray(rA.results[c]["counts_out"]).reshape(8)
              for c in range(NCORES)]
    counts_all = np.ascontiguousarray(np.stack(counts)).astype(np.float32)

    ncB = _get("B", build_launch_B)
    in_maps = [dict(res_in=res1[c], counts_all=counts_all, **w_l0f[c],
                    **w_l1a, **w_l1fpre, **cst) for c in range(NCORES)]
    rB = run_bass_kernel_spmd(ncB, in_maps, core_ids=list(range(NCORES)))
    res3 = [np.asarray(rB.results[c]["res_out"]) for c in range(NCORES)]
    hs = [np.asarray(rB.results[c]["hs_d"]) for c in range(NCORES)]
    qd = [np.asarray(rB.results[c]["q_d"]) for c in range(NCORES)]
    kall = np.stack([np.asarray(rB.results[c]["k_d"]) for c in range(NCORES)])
    vall = np.stack([np.asarray(rB.results[c]["v_d"]) for c in range(NCORES)])

    ncC = _get("CDE", build_launch_CDE)
    in_maps = [dict(res_in=res3[c], q_d=qd[c], hs_d=hs[c], kall=kall,
                    vall=vall, **w_l1fpost[c], **w_l2a, **w_l2f, **w_l3a,
                    **w_l3f, **cst) for c in range(NCORES)]
    rC = run_bass_kernel_spmd(ncC, in_maps, core_ids=list(range(NCORES)))
    out = np.stack([np.asarray(rC.results[c]["res_out"], np.float32)
                    for c in range(NCORES)])
    return out
